# version 02
# baseline (speedup 1.0000x reference)
import numpy as np
import ml_dtypes
import concourse.bass as bass
import concourse.mybir as mybir
from concourse.bass_utils import run_bass_kernel_spmd
from contextlib import ExitStack

F32 = mybir.dt.float32
BF16 = mybir.dt.bfloat16
I32 = mybir.dt.int32
AF = mybir.ActivationFunctionType
ALU = mybir.AluOpType
AXX = mybir.AxisListType.X

EMBED = 2048
HEADS = 16
HD = 128
C = 128
KC = 16
TPC = 1024
NCH = 8
NCORES = 8
NE = 16
NEX = 18
NPAIR = 9
EKC = 8
EPS_G = 1e-6
EPS_LN = 1e-5
BASE = 10000.0

_DEBUG_TAPS = set()
_PARANOID = 1

BF = ml_dtypes.bfloat16


def _log_lam():
    h = np.arange(HEADS, dtype=np.float64)
    return -(2.0 ** (-8.0 * (h + 1.0) / HEADS))


def host_constants(core):
    ll = _log_lam()
    cb = core % 4
    t0 = cb * TPC
    half = HD // 2
    theta = BASE ** (-2.0 * np.arange(half, dtype=np.float64) / HD)
    pos = np.arange(t0, t0 + TPC, dtype=np.float64)
    ang = pos[None, :] * np.concatenate([theta, theta])[:, None]
    cosb = np.cos(ang).astype(BF)
    sinb = np.sin(ang).astype(BF)

    i = np.arange(C, dtype=np.float64)
    diff = i[:, None] - i[None, :]
    dmask = np.where(diff[None] >= 0, np.exp(ll[:, None, None] * diff[None]), 0.0)
    dt_c = np.transpose(dmask, (0, 2, 1)).transpose(1, 0, 2).reshape(C, HEADS * C)
    dt_c = np.ascontiguousarray(dt_c).astype(BF)
    dq = np.exp(ll[:, None] * (i[None, :] + 1.0))
    dqr = np.broadcast_to(dq[None], (C, HEADS, C)).reshape(C, HEADS * C)
    dqr = np.ascontiguousarray(dqr).astype(BF)
    dkcol = np.exp(ll[None, :] * (C - 1.0 - i[:, None])).astype(np.float32)
    lam_C = np.exp(ll * C)
    lamcol = np.broadcast_to(lam_C[None, :], (C, HEADS)).astype(np.float32)
    lamcol = np.ascontiguousarray(lamcol)
    lj = lam_C[:, None] ** np.arange(NCH)[None, :]
    lj = np.where(lj < 1e-30, 0.0, lj)
    lamj = np.broadcast_to(lj.reshape(1, -1), (C, HEADS * NCH)).astype(np.float32)
    lamj = np.ascontiguousarray(lamj)

    identb = np.eye(128, dtype=BF)
    perm = np.zeros((128, 128), np.float32)
    for m in range(64):
        perm[m + 64, m] = -1.0
        perm[m, m + 64] = 1.0
    permb = perm.astype(BF)
    ohb = np.zeros((128, HEADS * 16), np.float32)
    for h in range(HEADS):
        ohb[:, h * 16 + h] = 1.0
    ohb = ohb.astype(BF)
    rsel = np.zeros((16, HEADS * 128), np.float32)
    for h in range(HEADS):
        rsel[h, h * 128:(h + 1) * 128] = 1.0
    rsel = rsel.astype(BF)
    onesf = np.ones((128, 1), np.float32)
    onesrow = np.ones((1, 128), np.float32)
    cvec = np.zeros((128, 8), np.float32)
    cvec[:, 0] = 1.0 / HD
    cvec[:, 1] = EPS_G
    cvec[:, 2] = 1.0 / EMBED
    cvec[:, 3] = EPS_LN
    cvec[:, 4] = -1e9
    cvec[:, 5] = 1.0
    lam_B = np.exp(ll * C * NCH)
    wcol3 = np.zeros((128, 3 * HEADS), np.float32)
    poff = np.zeros((1, 4), np.int32)
    for i3 in range(3):
        src = cb - 1 - i3
        r = (core - 1 - i3) % NCORES
        poff[0, i3] = r * EMBED * 128
        if src >= 0:
            w = lam_B ** i3
            w = np.where(w < 1e-30, 0.0, w)
            wcol3[:, i3 * HEADS:(i3 + 1) * HEADS] = w[None, :].astype(np.float32)
    return dict(cosb=cosb, sinb=sinb, dt_c=dt_c, dqr=dqr, dkcol=dkcol,
                lamcol=lamcol, lamj=lamj, identb=identb, permb=permb, ohb=ohb,
                rsel=rsel, onesf=onesf, onesrow=onesrow, cvec=cvec,
                wcol3=wcol3, poff=poff)


def host_weights(inp):
    def mpanels(W, n_in_kc, n_m):
        return np.ascontiguousarray(
            W.reshape(n_in_kc, 128, n_m, 128).transpose(2, 1, 0, 3)
            .reshape(n_m, 128, n_in_kc * 128))

    out = {}
    out["wq"] = mpanels(np.asarray(inp["L0_attn_Wq"], np.float32), KC, 16)
    out["wk"] = mpanels(np.asarray(inp["L0_attn_Wk"], np.float32), KC, 16)
    out["wo"] = mpanels(np.asarray(inp["L0_attn_Wo"], np.float32), KC, 16)
    wv = np.asarray(inp["L0_attn_Wv"], np.float32)
    out["wv"] = np.ascontiguousarray(
        wv.reshape(EMBED, 4, 512).transpose(1, 0, 2))
    out["wg1"] = mpanels(np.asarray(inp["L0_attn_Wg1"], np.float32), KC, 1)[0]
    out["wg2"] = np.asarray(inp["L0_attn_Wg2"], np.float32)
    out["rwt"] = np.ascontiguousarray(
        np.asarray(inp["L0_ffn_router_w"], np.float32).T.reshape(KC, 128, NE))
    out["biasrep"] = np.ascontiguousarray(np.broadcast_to(
        np.asarray(inp["L0_ffn_router_bias"], np.float32)[None, :], (128, NE)))
    out["nwcol"] = np.ascontiguousarray(
        np.asarray(inp["L0_attn_norm_weight"], np.float32).reshape(HEADS, 128).T)

    wu = np.asarray(inp["L0_ffn_Wu"], np.float32)
    wg = np.asarray(inp["L0_ffn_Wg"], np.float32)
    su = np.asarray(inp["L0_ffn_Su"], np.float32)[0]
    sg = np.asarray(inp["L0_ffn_Sg"], np.float32)[0]
    def upanels(W):
        return mpanels(W, KC, EKC)
    wue = np.empty((NEX, EKC, 128, EMBED), np.float32)
    wge = np.empty((NEX, EKC, 128, EMBED), np.float32)
    for e in range(NE):
        wue[e] = upanels(wu[e])
        wge[e] = upanels(wg[e])
    wue[16] = upanels(su[:, :1024]); wue[17] = upanels(su[:, 1024:])
    wge[16] = upanels(sg[:, :1024]); wge[17] = upanels(sg[:, 1024:])
    out["wue"] = np.ascontiguousarray(wue)
    out["wge"] = np.ascontiguousarray(wge)

    wd = np.asarray(inp["L0_ffn_Wd"], np.float32)
    sd = np.asarray(inp["L0_ffn_Sd"], np.float32)[0]
    def dpanels(W):
        return np.ascontiguousarray(
            W.reshape(EKC, 128, 16, 128).transpose(2, 1, 0, 3)
            .reshape(16, 128, EKC * 128))
    wdx = np.empty((NEX, 16, 128, EKC * 128), np.float32)
    for e in range(NE):
        wdx[e] = dpanels(wd[e])
    wdx[16] = dpanels(sd[:1024, :])
    wdx[17] = dpanels(sd[1024:, :])
    out["wdp"] = np.ascontiguousarray(
        wdx.reshape(NPAIR, 2, 16, 128, EKC * 128).transpose(0, 2, 1, 3, 4))
    return out


class Trk:
    __slots__ = ("w", "rs")

    def __init__(self):
        self.w = None
        self.rs = []


class B:

    ENGS = ("pe", "dve", "act", "pool", "sync")

    def __init__(self, nc):
        self.nc = nc
        self.eng = {}
        self.sems = {}

    DKEYS = {"pool": ("stg0", "stg1", "xbf", "g", "misc"),
             "sync": ("main", "ccin", "xf", "out")}

    def init_sems(self, es):
        for name in self.ENGS:
            sem = es.enter_context(self.nc.semaphore(f"s_{name}"))
            d = dict(sem=sem, cnt=0, waited={}, script=[], dsems={})
            for key in self.DKEYS.get(name, ()):
                d["dsems"][key] = [
                    es.enter_context(self.nc.semaphore(f"s_{name}_{key}")), 0]
            self.eng[name] = d
        self.cc_sem = es.enter_context(self.nc.semaphore("s_cc"))

    def _waits(self, e, deps, skip_self=False):
        need = {}
        for (sem, val) in deps:
            if skip_self and sem is e["sem"]:
                continue
            if need.get(sem, 0) < val:
                need[sem] = val
        for sem, val in sorted(need.items(), key=lambda kv: str(kv[0].name)):
            if e["waited"].get(sem, 0) < val:
                e["waited"][sem] = val
                e["script"].append(
                    lambda eng, s=sem, v=val: eng.wait_ge(s, v))

    def emit(self, en, fn, reads=(), writes=(), dma=False, fns=None,
             dkey="main"):
        e = self.eng[en]
        deps = []
        for t in reads:
            if t.w:
                deps.append(t.w)
        for t in writes:
            if t.w:
                deps.append(t.w)
            deps.extend(t.rs)
        self._waits(e, deps, skip_self=(en == "pe"))
        if dma:
            ds = e["dsems"][dkey]
            ds[1] += 16
            sem, cnt = ds[0], ds[1]
            amt = 16
        else:
            e["cnt"] += 1
            sem, cnt = e["sem"], e["cnt"]
            amt = 1
        if fns is not None:
            for f in fns[:-1]:
                e["script"].append(lambda eng, f=f: f(eng))
            e["script"].append(
                lambda eng, f=fns[-1], s=sem, a=amt: f(eng).then_inc(s, a))
        else:
            e["script"].append(
                lambda eng, f=fn, s=sem, a=amt: f(eng).then_inc(s, a))
        tok = (sem, cnt)
        for t in reads:
            t.rs.append(tok)
        for t in writes:
            t.w = tok
            t.rs = []
        return tok

    def raw(self, en, fn):
        self.eng[en]["script"].append(fn)

    def barrier(self, level=1):
        import kernel as _k
        if _PARANOID < level:
            return
        sems = []
        for name in self.ENGS:
            e = self.eng[name]
            if e["cnt"]:
                sems.append((e["sem"], e["cnt"]))
            for key, sc in e["dsems"].items():
                if sc[1]:
                    sems.append((sc[0], sc[1]))
        for name in self.ENGS:
            e = self.eng[name]
            self._waits(e, [t for t in sems if t[0] is not e["sem"]])

    def run_blocks(self, block):
        block.tensor(lambda eng: [f(eng) for f in self.eng["pe"]["script"]])
        block.vector(lambda eng: [f(eng) for f in self.eng["dve"]["script"]])
        block.scalar(lambda eng: [f(eng) for f in self.eng["act"]["script"]])
        block.gpsimd(lambda eng: [f(eng) for f in self.eng["pool"]["script"]])
        block.sync(lambda eng: [f(eng) for f in self.eng["sync"]["script"]])


def build_program():
    nc = bass.Bass("TRN2", target_bir_lowering=False)
    ll = _log_lam()
    lam_C = np.exp(ll * C)

    d_xt = nc.dram_tensor("xt", [EMBED, TPC], F32, kind="ExternalInput")
    d_wq = nc.dram_tensor("wq", [16, 128, EMBED], F32, kind="ExternalInput")
    d_wk = nc.dram_tensor("wk", [16, 128, EMBED], F32, kind="ExternalInput")
    d_wo = nc.dram_tensor("wo", [16, 128, EMBED], F32, kind="ExternalInput")
    d_wv = nc.dram_tensor("wv", [4, EMBED, 512], F32, kind="ExternalInput")
    d_wg1 = nc.dram_tensor("wg1", [128, EMBED], F32, kind="ExternalInput")
    d_wg2 = nc.dram_tensor("wg2", [128, EMBED], F32, kind="ExternalInput")
    d_rwt = nc.dram_tensor("rwt", [KC, 128, NE], F32, kind="ExternalInput")
    d_biasrep = nc.dram_tensor("biasrep", [128, NE], F32, kind="ExternalInput")
    d_nwcol = nc.dram_tensor("nwcol", [128, HEADS], F32, kind="ExternalInput")
    d_wue = nc.dram_tensor("wue", [NEX, EKC, 128, EMBED], F32, kind="ExternalInput")
    d_wge = nc.dram_tensor("wge", [NEX, EKC, 128, EMBED], F32, kind="ExternalInput")
    d_wdp = nc.dram_tensor("wdp", [NPAIR, 16, 2, 128, EKC * 128], F32,
                           kind="ExternalInput")
    d_cosb = nc.dram_tensor("cosb", [128, TPC], BF16, kind="ExternalInput")
    d_sinb = nc.dram_tensor("sinb", [128, TPC], BF16, kind="ExternalInput")
    d_dt = nc.dram_tensor("dt_c", [128, HEADS * C], BF16, kind="ExternalInput")
    d_dqr = nc.dram_tensor("dqr", [128, HEADS * C], BF16, kind="ExternalInput")
    d_dkcol = nc.dram_tensor("dkcol", [128, HEADS], F32, kind="ExternalInput")
    d_lamcol = nc.dram_tensor("lamcol", [128, HEADS], F32, kind="ExternalInput")
    d_lamj = nc.dram_tensor("lamj", [128, HEADS * NCH], F32, kind="ExternalInput")
    d_identb = nc.dram_tensor("identb", [128, 128], BF16, kind="ExternalInput")
    d_permb = nc.dram_tensor("permb", [128, 128], BF16, kind="ExternalInput")
    d_ohb = nc.dram_tensor("ohb", [128, HEADS * 16], BF16, kind="ExternalInput")
    d_rsel = nc.dram_tensor("rsel", [16, HEADS * 128], BF16, kind="ExternalInput")
    d_onesf = nc.dram_tensor("onesf", [128, 1], F32, kind="ExternalInput")
    d_onesrow = nc.dram_tensor("onesrow", [1, 128], F32, kind="ExternalInput")
    d_cvec = nc.dram_tensor("cvec", [128, 8], F32, kind="ExternalInput")
    d_wcol3 = nc.dram_tensor("wcol3", [128, 3 * HEADS], F32, kind="ExternalInput")
    d_poff = nc.dram_tensor("poff", [1, 4], I32, kind="ExternalInput")
    d_out = nc.dram_tensor("out_t", [EMBED, TPC], F32, kind="ExternalOutput")
    d_ccin = nc.dram_tensor("cc_in", [EMBED, 128], F32)
    d_ccout = nc.dram_tensor("cc_out", [NCORES * EMBED, 128], F32)

    b = B(nc)

    def tap(name, ap_fn, reads, shape, dtype):
        if name not in _DEBUG_TAPS:
            return
        d = nc.dram_tensor("tap_" + name, shape, dtype, kind="ExternalOutput")
        b.emit("sync", lambda g, d=d, a=ap_fn(): g.dma_start(out=d.ap(), in_=a),
               reads=list(reads), dma=True, dkey="out")
    es = ExitStack()
    with es:
        b.init_sems(es)
        sb = lambda name, shape, dt, **kw: es.enter_context(
            nc.sbuf_tensor(name, shape, dt, **kw))
        t_cvec = sb("cvec_s", [128, 8], F32)
        t_identb = sb("identb_s", [128, 128], BF16)
        t_permb = sb("permb_s", [128, 128], BF16)
        t_ohb = sb("ohb_s", [128, HEADS * 16], BF16)
        t_rsel = sb("rsel_s", [16, HEADS * 128], BF16)
        t_onesf = sb("onesf_s", [128, 1], F32)
        t_onesrow = sb("onesrow_s", [1, 128], F32)
        t_biasrep = sb("biasrep_s", [128, NE], F32)
        t_nwcol = sb("nwcol_s", [128, HEADS], F32)
        t_dkcol = sb("dkcol_s", [128, HEADS], F32)
        t_lamcol = sb("lamcol_s", [128, HEADS], F32)
        t_lamj = sb("lamj_s", [128, HEADS * NCH], F32)
        t_wcol3 = sb("wcol3_s", [128, 3 * HEADS], F32)
        t_poff = sb("poff_s", [1, 4], I32)
        t_rwt = sb("rwt_s", [128, KC * NE], F32)
        t_g1 = sb("g1_s", [128, TPC], BF16)
        t_stg = [sb(f"stg{i}", [128, 4096], BF16) for i in range(2)]
        ps = [es.enter_context(nc.psum_tensor(f"ps{i}", [128, 512], F32))
              for i in range(7)]
        psb = es.enter_context(nc.psum_tensor("psb", [128, 1024], BF16))

        T = lambda: Trk()
        k_cvec, k_consts = T(), T()
        k_rwt, k_g1 = T(), T()
        k_stg = [T(), T()]
        k_ps = [T() for _ in range(7)]
        k_psb = T()

        def ld(dst_ap, src_ap, trk, queue="sync", dkey="main"):
            b.emit(queue, lambda g, d=dst_ap, s=src_ap: g.dma_start(out=d, in_=s),
                   writes=[trk], dma=True, dkey=dkey)

        ld(t_cvec[:, :], d_cvec.ap(), k_cvec)
        for dst, src in ((t_identb, d_identb), (t_permb, d_permb),
                         (t_ohb, d_ohb), (t_rsel, d_rsel), (t_onesf, d_onesf),
                         (t_onesrow, d_onesrow), (t_biasrep, d_biasrep),
                         (t_nwcol, d_nwcol), (t_dkcol, d_dkcol),
                         (t_lamcol, d_lamcol), (t_lamj, d_lamj),
                         (t_wcol3, d_wcol3), (t_poff, d_poff)):
            ld(dst[:, :], src.ap(), k_consts)
        ld(t_rwt.ap().rearrange("p (kc n) -> p kc n", kc=KC),
           d_rwt.ap().rearrange("kc p n -> p kc n"), k_rwt)

        b.barrier()
        esA = ExitStack()
        t_obuf = esA.enter_context(nc.sbuf_tensor("obuf", [128, HEADS * TPC], BF16))
        t_qbuf = esA.enter_context(nc.sbuf_tensor("qbuf", [128, HEADS * TPC], BF16))
        t_S = esA.enter_context(nc.sbuf_tensor("S_f", [128, HEADS * HD], F32))
        t_Sb = esA.enter_context(nc.sbuf_tensor("S_b", [128, HEADS * HD], BF16))
        k_obuf = [T() for _ in range(HEADS)]
        k_qbuf = [T() for _ in range(HEADS)]
        k_S = [T() for _ in range(HEADS)]
        k_Sb = [T() for _ in range(HEADS)]
        k_dqr, k_dt = T(), T()

        esA2 = ExitStack()
        t_kfm = esA2.enter_context(nc.sbuf_tensor("kfm", [128, HEADS * TPC], BF16))
        t_scb = esA2.enter_context(nc.sbuf_tensor("scb", [128, C], BF16))
        t_ktb = esA2.enter_context(nc.sbuf_tensor("ktb", [128, C], BF16))
        esX = ExitStack()
        t_xbf = esX.enter_context(nc.sbuf_tensor("xbf", [128, KC * TPC], BF16,
                                                 side="right"))
        k_xbf = T()
        k_kfm = [T() for _ in range(HEADS)]
        k_vtm = [T() for _ in range(NCH * 4)]
        k_scb, k_ktb = T(), T()

        ld(t_xbf.ap().rearrange("p (kc t) -> p kc t", kc=KC),
           d_xt.ap().rearrange("(kc p) t -> p kc t", p=128), k_xbf, queue="pool",
           dkey="xbf")

        esR = ExitStack()
        t_cos = esR.enter_context(nc.sbuf_tensor("cosb_s", [128, TPC], BF16, side="right"))
        t_sin = esR.enter_context(nc.sbuf_tensor("sinb_s", [128, TPC], BF16, side="right"))
        t_qsil = esR.enter_context(nc.sbuf_tensor("qsil", [128, TPC], BF16, side="right"))
        t_rt1 = esR.enter_context(nc.sbuf_tensor("rt1", [128, TPC], BF16, side="right"))
        t_rt2 = esR.enter_context(nc.sbuf_tensor("rt2", [128, TPC], BF16, side="right"))
        k_cos, k_sin, k_qsil, k_rt1, k_rt2 = T(), T(), T(), T(), T()
        ld(t_cos[:, :], d_cosb.ap(), k_cos)
        ld(t_sin[:, :], d_sinb.ap(), k_sin)

        def mm_sweep(bank, lhs_fn, rhs_fn, n_k, reads):
            fns = []
            for kci in range(n_k):
                fns.append(lambda t, l=lhs_fn(kci), r=rhs_fn(kci),
                           st=(kci == 0), sp=(kci == n_k - 1), bk=bank:
                           t.matmul(ps[bk][:, :], l, r, start=st, stop=sp))
            return b.emit("pe", None, fns=fns, reads=list(reads),
                          writes=[k_ps[bank]])

        def mm_sweep2(b0, b1, lhs_fn, rhs_fn, n_k, reads):
            fns = []
            for kci in range(n_k):
                l = lhs_fn(kci)
                for half, bk in ((0, b0), (1, b1)):
                    fns.append(lambda t, l=l, r=rhs_fn(kci, half),
                               st=(kci == 0), sp=(kci == n_k - 1), bk=bk:
                               t.matmul(ps[bk][:, :], l, r, start=st, stop=sp,
                                        skip_group_check=True))
            return b.emit("pe", None, fns=fns, reads=list(reads),
                          writes=[k_ps[b0], k_ps[b1]])

        def qk_phase(dram_w, dest, k_dest):
            for m in range(HEADS):
                sg = m % 2
                b.emit("pool", lambda g, m=m, sg=sg: g.dma_start(
                    out=t_stg[sg][:, :EMBED], in_=dram_w.ap()[m, :, :]),
                    writes=[k_stg[sg]], dma=True, dkey=f"stg{sg}")
                mm_sweep2(0, 1,
                          lambda kci, sg=sg: t_stg[sg][:, kci * 128:(kci + 1) * 128],
                          lambda kci, half: t_xbf[:, kci * TPC + half * 512:
                                                  kci * TPC + half * 512 + 512],
                          KC, (k_stg[sg], k_xbf))
                for half in range(2):
                    bank = half
                    b.emit("act", lambda s, bank=bank, half=half: s.activation(
                        t_qsil[:, half * 512:half * 512 + 512], ps[bank][:, :],
                        AF.Silu), reads=[k_ps[bank]], writes=[k_qsil])
                    b.emit("pe", lambda t, half=half: t.matmul(
                        ps[2 + half][:, :], t_permb[:, :],
                        t_qsil[:, half * 512:half * 512 + 512],
                        start=True, stop=True),
                        reads=[k_qsil, k_consts], writes=[k_ps[2 + half]])
                b.emit("dve", lambda v, m=m: v.tensor_tensor(
                    t_rt1[:, :], t_qsil[:, :], t_cos[:, :], ALU.mult),
                    reads=[k_qsil, k_cos], writes=[k_rt1])
                for half in range(2):
                    b.emit("dve", lambda v, half=half: v.tensor_tensor(
                        t_rt2[:, half * 512:half * 512 + 512],
                        ps[2 + half][:, :],
                        t_sin[:, half * 512:half * 512 + 512], ALU.mult),
                        reads=[k_ps[2 + half], k_sin], writes=[k_rt2])
                b.emit("dve", lambda v, m=m: v.tensor_tensor(
                    dest[:, m * TPC:(m + 1) * TPC], t_rt1[:, :], t_rt2[:, :],
                    ALU.add), reads=[k_rt1, k_rt2], writes=[k_dest[m]])

        qk_phase(d_wq, t_qbuf, k_qbuf)
        b.barrier()
        qk_phase(d_wk, t_kfm, k_kfm)
        b.barrier()
        esR.close()
        tap("qk_q", lambda: t_qbuf[:, :], k_qbuf, [128, HEADS * TPC], BF16)
        tap("qk_k", lambda: t_kfm[:, :], k_kfm, [128, HEADS * TPC], BF16)

        b.emit("pool", lambda g: g.dma_start(
            out=t_stg[0][:, :EMBED], in_=d_wg1.ap()), writes=[k_stg[0]],
            dma=True, dkey="stg0")
        mm_sweep2(0, 1,
                  lambda kci: t_stg[0][:, kci * 128:(kci + 1) * 128],
                  lambda kci, half: t_xbf[:, kci * TPC + half * 512:
                                          kci * TPC + half * 512 + 512],
                  KC, (k_stg[0], k_xbf))
        for half in range(2):
            b.emit("act", lambda s, half=half: s.copy(
                t_g1[:, half * 512:half * 512 + 512], ps[half][:, :]),
                reads=[k_ps[half]], writes=[k_g1])

        tap("g1", lambda: t_g1[:, :], [k_g1], [128, TPC], BF16)
        esV = ExitStack()
        t_vtm = esV.enter_context(nc.sbuf_tensor("vtm", [128, NCH * HEADS * HD],
                                                 BF16))
        for nf in range(4):
            for kh in range(2):
                b.emit("pool", lambda g, nf=nf, kh=kh: g.dma_start(
                    out=t_stg[kh].ap().rearrange("p (kc f) -> p kc f", kc=8),
                    in_=d_wv.ap()[nf, kh * 1024:(kh + 1) * 1024, :]
                    .rearrange("(kc p) f -> p kc f", p=128)),
                    writes=[k_stg[kh]], dma=True, dkey=f"stg{kh}")
            for tc in range(NCH):
                bank = tc % 2
                fns = []
                for kci in range(KC):
                    kh, kloc = kci // 8, kci % 8
                    fns.append(lambda t, kci=kci, kh=kh, kloc=kloc, tc=tc,
                               bank=bank: t.matmul(
                                   ps[bank][:, :],
                                   t_xbf[:, kci * TPC + tc * 128:
                                         kci * TPC + tc * 128 + 128],
                                   t_stg[kh][:, kloc * 512:(kloc + 1) * 512],
                                   start=(kci == 0), stop=(kci == KC - 1)))
                b.emit("pe", None, fns=fns,
                       reads=[k_stg[0], k_stg[1], k_xbf], writes=[k_ps[bank]])
                b.emit("act", lambda s, tc=tc, nf=nf, bank=bank: s.copy(
                    t_vtm[:, (tc * HEADS + nf * 4) * HD:
                          (tc * HEADS + nf * 4) * HD + 512], ps[bank][:, :]),
                    reads=[k_ps[bank]], writes=[k_vtm[tc * 4 + nf]])
        esX.close()
        t_dqr = esV.enter_context(nc.sbuf_tensor("dqr_s", [128, HEADS * C], BF16))
        t_dt = esV.enter_context(nc.sbuf_tensor("dt_s", [128, HEADS * C], BF16))
        ld(t_dqr[:, :], d_dqr.ap(), k_dqr)
        ld(t_dt[:, :], d_dt.ap(), k_dt)

        b.barrier()
        tap("v", lambda: t_vtm[:, :], k_vtm, [128, NCH * HEADS * HD], BF16)
        for j in range(NCH):
            b.barrier(2)
            for h in range(HEADS):
                qsl = lambda h=h, j=j: t_qbuf[:, h * TPC + j * C: h * TPC + (j + 1) * C]
                ksl = lambda h=h, j=j: t_kfm[:, h * TPC + j * C: h * TPC + (j + 1) * C]
                vsl = lambda h=h, j=j: t_vtm[:, (j * HEADS + h) * HD:
                                             (j * HEADS + h + 1) * HD]
                k_v = k_vtm[j * 4 + h // 4]
                b.emit("pe", lambda t, ksl=ksl: t.transpose(
                    psb[:, 0:C], ksl(), t_identb[:, :]),
                    reads=[k_kfm[h], k_consts], writes=[k_psb])
                b.emit("act", lambda s, h=h: s.activation(
                    t_ktb[:, :], psb[:, 0:C], AF.Copy,
                    scale=t_dkcol[:, h:h + 1]),
                    reads=[k_psb, k_consts], writes=[k_ktb])
                b.emit("pe", lambda t, ksl=ksl, qsl=qsl: t.matmul(
                    ps[4][:, 0:C], ksl(), qsl(), start=True, stop=True),
                    reads=[k_kfm[h], k_qbuf[h]], writes=[k_ps[4]])
                b.emit("dve", lambda v, h=h: v.tensor_tensor(
                    t_scb[:, :], ps[4][:, 0:C], t_dt[:, h * C:(h + 1) * C],
                    ALU.mult), reads=[k_ps[4], k_dt], writes=[k_scb])
                b.emit("dve", lambda v, h=h, qsl=qsl: v.tensor_tensor(
                    qsl(), qsl(), t_dqr[:, h * C:(h + 1) * C], ALU.mult),
                    reads=[k_dqr], writes=[k_qbuf[h]])
                if j == 0:
                    b.emit("pe", lambda t, vsl=vsl: t.matmul(
                        ps[5][:, 0:C], vsl(), t_scb[:, :], start=True, stop=True),
                        reads=[k_v, k_scb], writes=[k_ps[5]])
                else:
                    b.emit("pe", None, fns=[
                        lambda t, vsl=vsl: t.matmul(
                            ps[5][:, 0:C], vsl(), t_scb[:, :],
                            start=True, stop=False),
                        lambda t, h=h, qsl=qsl: t.matmul(
                            ps[5][:, 0:C], t_Sb[:, h * HD:(h + 1) * HD], qsl(),
                            start=False, stop=True)],
                        reads=[k_v, k_scb, k_Sb[h], k_qbuf[h]],
                        writes=[k_ps[5]])
                b.emit("act", lambda s, h=h, j=j: s.copy(
                    t_obuf[:, h * TPC + j * C: h * TPC + (j + 1) * C],
                    ps[5][:, 0:C]), reads=[k_ps[5]], writes=[k_obuf[h]])
                b.emit("pe", lambda t, vsl=vsl: t.matmul(
                    ps[6][:, 0:C], t_ktb[:, :], vsl(), start=True, stop=True),
                    reads=[k_ktb, k_v], writes=[k_ps[6]])
                if j == 0:
                    b.emit("dve", lambda v, h=h: v.tensor_copy(
                        t_S[:, h * HD:(h + 1) * HD], ps[6][:, 0:C]),
                        reads=[k_ps[6]], writes=[k_S[h]])
                else:
                    b.emit("dve", lambda v, h=h: v.scalar_tensor_tensor(
                        t_S[:, h * HD:(h + 1) * HD], t_S[:, h * HD:(h + 1) * HD],
                        t_lamcol[:, h:h + 1], ps[6][:, 0:C], ALU.mult, ALU.add),
                        reads=[k_ps[6], k_consts], writes=[k_S[h]])
                if j < NCH - 1:
                    b.emit("act", lambda s, h=h: s.copy(
                        t_Sb[:, h * HD:(h + 1) * HD], t_S[:, h * HD:(h + 1) * HD]),
                        reads=[k_S[h]], writes=[k_Sb[h]])
        tap("o_scan", lambda: t_obuf[:, :], k_obuf, [128, HEADS * TPC], BF16)
        tap("s_loc", lambda: t_S[:, :], k_S, [128, HEADS * HD], F32)
        esV.close()
        esA2.close()

        b.barrier()
        esC = ExitStack()
        t_G = [esC.enter_context(nc.sbuf_tensor(f"G{i}", [128, HEADS * HD], F32))
               for i in range(3)]
        t_tmpA = esC.enter_context(nc.sbuf_tensor("tmpA", [128, HD], F32))
        t_tmpB = esC.enter_context(nc.sbuf_tensor("tmpB", [128, HD], F32))
        t_Sin = esC.enter_context(nc.sbuf_tensor("Sin", [128, HEADS * HD], BF16))
        t_qt = esC.enter_context(nc.sbuf_tensor("qt", [128, C], BF16))
        k_G = [T() for _ in range(3)]
        k_tmpA, k_tmpB, k_Sin, k_qt = T(), T(), T(), T()
        k_ccin, k_ccout = T(), T()

        b.emit("sync", lambda g: g.dma_start(
            out=d_ccin.ap().rearrange("(h p) e -> p h e", p=128),
            in_=t_S.ap().rearrange("p (h e) -> p h e", h=HEADS)),
            reads=list(k_S), writes=[k_ccin], dma=True, dkey="ccin")

        e = b.eng["pool"]
        b._waits(e, [k_ccin.w])
        cc_sem = b.cc_sem
        e["script"].append(lambda g, s=cc_sem: g.collective_compute(
            "AllGather", ALU.bypass, replica_groups=[list(range(NCORES))],
            ins=[d_ccin.ap().opt()], outs=[d_ccout.ap().opt()]).then_inc(s, 1))
        k_ccout.w = (cc_sem, 1)

        dsem = e["dsems"]["g"][0]

        def fetch_G(g, s):
            with g.register("gofs") as gofs:
                for i in range(3):
                    g.reg_load(gofs, t_poff[:1, i:i + 1])
                    g.dma_start(
                        out=t_G[i].ap().rearrange("p (h e) -> p h e", h=HEADS),
                        in_=bass.AP(d_ccout, gofs,
                                    [[128, 128], [128 * 128, HEADS], [1, 128]]),
                    ).then_inc(s, 16)
        b._waits(e, [k_ccout.w, k_consts.w])
        e["dsems"]["g"][1] += 48
        g_tok = (dsem, e["dsems"]["g"][1])
        e["script"].append(lambda g, s=dsem: fetch_G(g, s))
        for kk in k_G:
            kk.w = g_tok

        for h in range(HEADS):
            b.emit("dve", lambda v, h=h: v.tensor_scalar(
                t_tmpA[:, :], t_G[0][:, h * HD:(h + 1) * HD],
                t_wcol3[:, h:h + 1], None, ALU.mult),
                reads=[k_G[0], k_consts], writes=[k_tmpA])
            b.emit("dve", lambda v, h=h: v.scalar_tensor_tensor(
                t_tmpB[:, :], t_G[1][:, h * HD:(h + 1) * HD],
                t_wcol3[:, HEADS + h:HEADS + h + 1], t_tmpA[:, :],
                ALU.mult, ALU.add),
                reads=[k_G[1], k_consts, k_tmpA], writes=[k_tmpB])
            b.emit("dve", lambda v, h=h: v.scalar_tensor_tensor(
                t_tmpA[:, :], t_G[2][:, h * HD:(h + 1) * HD],
                t_wcol3[:, 2 * HEADS + h:2 * HEADS + h + 1], t_tmpB[:, :],
                ALU.mult, ALU.add),
                reads=[k_G[2], k_consts, k_tmpB], writes=[k_tmpA])
            b.emit("act", lambda s, h=h: s.copy(
                t_Sin[:, h * HD:(h + 1) * HD], t_tmpA[:, :]),
                reads=[k_tmpA], writes=[k_Sin])

        lj_np = lam_C[:, None] ** np.arange(NCH)[None, :]
        for h in range(HEADS):
            for j in range(NCH):
                if lj_np[h, j] < 1e-30:
                    continue
                b.emit("dve", lambda v, h=h, j=j: v.tensor_scalar(
                    t_qt[:, :], t_qbuf[:, h * TPC + j * C: h * TPC + (j + 1) * C],
                    t_lamj[:, h * NCH + j: h * NCH + j + 1], None, ALU.mult),
                    reads=[k_qbuf[h], k_consts], writes=[k_qt])
                b.emit("pe", lambda t, h=h: t.matmul(
                    ps[4][:, 0:C], t_Sin[:, h * HD:(h + 1) * HD], t_qt[:, :],
                    start=True, stop=True),
                    reads=[k_Sin, k_qt], writes=[k_ps[4]])
                b.emit("dve", lambda v, h=h, j=j: v.tensor_tensor(
                    t_obuf[:, h * TPC + j * C: h * TPC + (j + 1) * C],
                    t_obuf[:, h * TPC + j * C: h * TPC + (j + 1) * C],
                    ps[4][:, 0:C], ALU.add),
                    reads=[k_ps[4]], writes=[k_obuf[h]])
        tap("o_corr", lambda: t_obuf[:, :], k_obuf, [128, HEADS * TPC], BF16)
        tap("sin_t", lambda: t_Sin[:, :], [k_Sin], [128, HEADS * HD], BF16)
        esC.close()

        b.barrier()
        esG = ExitStack()
        b.emit("pool", lambda g: g.dma_start(
            out=t_stg[0][:, :EMBED], in_=d_wg2.ap()),
            writes=[k_stg[0]], dma=True, dkey="stg0")
        k_wg2v = k_stg[0]
        t_gsig = esG.enter_context(nc.sbuf_tensor("gsig", [128, TPC], F32))
        t_sq = esG.enter_context(nc.sbuf_tensor("sqg", [128, TPC], BF16))
        t_rstd = esG.enter_context(nc.sbuf_tensor("rstdg", [16, TPC], F32))
        t_rstdb = esG.enter_context(nc.sbuf_tensor("rstdgb", [16, TPC], BF16))
        k_gsig, k_sq, k_rstd, k_rstdb = T(), T(), T(), T()
        for h in range(HEADS):
            for half in range(2):
                bank = half
                b.emit("pe", lambda t, h=h, half=half, bank=bank: t.matmul(
                    ps[bank][:, :], t_stg[0][:, h * 128:(h + 1) * 128],
                    t_g1[:, half * 512:half * 512 + 512], start=True, stop=True),
                    reads=[k_wg2v, k_g1], writes=[k_ps[bank]])
                b.emit("act", lambda s, half=half, bank=bank: s.activation(
                    t_gsig[:, half * 512:half * 512 + 512], ps[bank][:, :],
                    AF.Sigmoid), reads=[k_ps[bank]], writes=[k_gsig])
            b.emit("dve", lambda v, h=h: v.tensor_tensor(
                t_obuf[:, h * TPC:(h + 1) * TPC],
                t_obuf[:, h * TPC:(h + 1) * TPC], t_gsig[:, :], ALU.mult),
                reads=[k_gsig], writes=[k_obuf[h]])
            b.emit("act", lambda s, h=h: s.activation(
                t_sq[:, :], t_obuf[:, h * TPC:(h + 1) * TPC], AF.Square),
                reads=[k_obuf[h]], writes=[k_sq])
            for half in range(2):
                b.emit("pe", lambda t, h=h, half=half: t.matmul(
                    ps[2 + half][0:16, :], t_ohb[:, h * 16:(h + 1) * 16],
                    t_sq[:, half * 512:half * 512 + 512],
                    start=(h == 0), stop=(h == HEADS - 1)),
                    reads=[k_sq, k_consts], writes=[k_ps[2 + half]])
        tap("xg", lambda: t_obuf[:, :], k_obuf, [128, HEADS * TPC], BF16)
        tap("gsig15", lambda: t_gsig[:, :], [k_gsig], [128, TPC], F32)
        for half in range(2):
            b.emit("act", lambda s, half=half: s.activation(
                t_rstd[:, half * 512:half * 512 + 512], ps[2 + half][0:16, :],
                AF.Sqrt, bias=t_cvec[0:16, 1:2], scale=t_cvec[0:16, 0:1]),
                reads=[k_ps[2 + half], k_cvec], writes=[k_rstd])
        b.emit("dve", lambda v: v.reciprocal(t_rstd[:, :], t_rstd[:, :]),
               reads=[], writes=[k_rstd])
        b.emit("act", lambda s: s.copy(t_rstdb[:, :], t_rstd[:, :]),
               reads=[k_rstd], writes=[k_rstdb])
        tap("rstd", lambda: t_rstd[:, :], [k_rstd], [16, TPC], F32)
        for h in range(HEADS):
            for half in range(2):
                bank = half
                b.emit("pe", lambda t, h=h, half=half, bank=bank: t.matmul(
                    ps[bank][:, :], t_rsel[:, h * 128:(h + 1) * 128],
                    t_rstdb[:, half * 512:half * 512 + 512],
                    start=True, stop=True),
                    reads=[k_rstdb, k_consts], writes=[k_ps[bank]])
                b.emit("dve", lambda v, h=h, half=half, bank=bank:
                       v.scalar_tensor_tensor(
                           t_obuf[:, h * TPC + half * 512: h * TPC + half * 512 + 512],
                           t_obuf[:, h * TPC + half * 512: h * TPC + half * 512 + 512],
                           t_nwcol[:, h:h + 1], ps[bank][:, :],
                           ALU.mult, ALU.mult),
                       reads=[k_ps[bank], k_consts], writes=[k_obuf[h]])
        tap("o_n", lambda: t_obuf[:, :], k_obuf, [128, HEADS * TPC], BF16)
        esG.close()

        b.barrier()
        t_f = sb("f_acc", [128, KC * TPC], F32, side="right")
        k_f = T()
        b.emit("sync", lambda g: g.dma_start(
            out=t_f.ap().rearrange("p (kc t) -> p kc t", kc=KC),
            in_=d_xt.ap().rearrange("(kc p) t -> p kc t", p=128)),
            writes=[k_f], dma=True, dkey="xf")
        for m in range(KC):
            sg = m % 2
            b.emit("pool", lambda g, m=m, sg=sg: g.dma_start(
                out=t_stg[sg][:, :EMBED], in_=d_wo.ap()[m, :, :]),
                writes=[k_stg[sg]], dma=True, dkey=f"stg{sg}")
            mm_sweep2(0, 1,
                      lambda kci, sg=sg: t_stg[sg][:, kci * 128:(kci + 1) * 128],
                      lambda kci, half: t_obuf[:, kci * TPC + half * 512:
                                               kci * TPC + half * 512 + 512],
                      KC, tuple([k_stg[sg]] + k_obuf))
            for half in range(2):
                bank = half
                b.emit("dve", lambda v, m=m, half=half, bank=bank: v.tensor_tensor(
                    t_f[:, m * TPC + half * 512: m * TPC + half * 512 + 512],
                    t_f[:, m * TPC + half * 512: m * TPC + half * 512 + 512],
                    ps[bank][:, :], ALU.add),
                    reads=[k_ps[bank]], writes=[k_f])
        tap("r1", lambda: t_f[:, :], [k_f], [128, KC * TPC], F32)
        esA.close()

        def layer_norm(t_x, k_x, sq_scope_name):
            esL = ExitStack()
            t_sqf = esL.enter_context(nc.sbuf_tensor(
                f"sqf_{sq_scope_name}", [128, TPC], F32))
            t_st = esL.enter_context(nc.sbuf_tensor(
                f"st_{sq_scope_name}", [1, 4 * TPC], F32))
            k_sqf, k_st = T(), T()
            for kci in range(KC):
                b.emit("act", lambda s, kci=kci: s.activation(
                    t_sqf[:, :], t_x[:, kci * TPC:(kci + 1) * TPC], AF.Square),
                    reads=[k_x], writes=[k_sqf])
                for half in range(2):
                    b.emit("pe", lambda t, kci=kci, half=half: t.matmul(
                        ps[4 + half][0:1, :], t_onesf[:, :],
                        t_x[:, kci * TPC + half * 512: kci * TPC + half * 512 + 512],
                        start=(kci == 0), stop=(kci == KC - 1)),
                        reads=[k_x, k_consts], writes=[k_ps[4 + half]])
                    b.emit("pe", lambda t, kci=kci, half=half: t.matmul(
                        ps[2 + half][0:1, :], t_onesf[:, :],
                        t_sqf[:, half * 512:half * 512 + 512],
                        start=(kci == 0), stop=(kci == KC - 1)),
                        reads=[k_sqf, k_consts], writes=[k_ps[2 + half]])
            for half in range(2):
                b.emit("dve", lambda v, half=half: v.tensor_scalar(
                    t_st[0:1, half * 512:half * 512 + 512],
                    ps[4 + half][0:1, :], t_cvec[0:1, 2:3], None, ALU.mult),
                    reads=[k_ps[4 + half], k_cvec], writes=[k_st])
                b.emit("dve", lambda v, half=half: v.tensor_scalar(
                    t_st[0:1, TPC + half * 512: TPC + half * 512 + 512],
                    ps[2 + half][0:1, :], t_cvec[0:1, 2:3], None, ALU.mult),
                    reads=[k_ps[2 + half], k_cvec], writes=[k_st])
            b.emit("dve", lambda v: v.tensor_tensor(
                t_st[0:1, 2 * TPC:3 * TPC], t_st[0:1, 0:TPC], t_st[0:1, 0:TPC],
                ALU.mult), reads=[], writes=[k_st])
            b.emit("dve", lambda v: v.tensor_tensor(
                t_st[0:1, 2 * TPC:3 * TPC], t_st[0:1, TPC:2 * TPC],
                t_st[0:1, 2 * TPC:3 * TPC], ALU.subtract),
                reads=[], writes=[k_st])
            b.emit("act", lambda s: s.activation(
                t_st[0:1, 2 * TPC:3 * TPC], t_st[0:1, 2 * TPC:3 * TPC],
                AF.Sqrt, bias=t_cvec[0:1, 3:4]),
                reads=[k_cvec], writes=[k_st])
            b.emit("dve", lambda v: v.reciprocal(
                t_st[0:1, 2 * TPC:3 * TPC], t_st[0:1, 2 * TPC:3 * TPC]),
                reads=[], writes=[k_st])
            b.emit("dve", lambda v: v.tensor_tensor(
                t_st[0:1, 3 * TPC:4 * TPC], t_st[0:1, 0:TPC],
                t_st[0:1, 2 * TPC:3 * TPC], ALU.mult),
                reads=[], writes=[k_st])
            for half in range(2):
                b.emit("pe", lambda t, half=half: t.matmul(
                    ps[4 + half][:, :], t_onesrow[:, :],
                    t_st[0:1, 2 * TPC + half * 512: 2 * TPC + half * 512 + 512],
                    start=True, stop=True),
                    reads=[k_st, k_consts], writes=[k_ps[4 + half]])
                b.emit("pe", lambda t, half=half: t.matmul(
                    ps[2 + half][:, :], t_onesrow[:, :],
                    t_st[0:1, 3 * TPC + half * 512: 3 * TPC + half * 512 + 512],
                    start=True, stop=True),
                    reads=[k_st, k_consts], writes=[k_ps[2 + half]])
            for kci in range(KC):
                for half in range(2):
                    sl = lambda kci=kci, half=half: t_x[
                        :, kci * TPC + half * 512: kci * TPC + half * 512 + 512]
                    b.emit("dve", lambda v, sl=sl, half=half: v.tensor_tensor(
                        sl(), sl(), ps[4 + half][:, :], ALU.mult),
                        reads=[k_ps[4 + half]], writes=[k_x])
                    b.emit("dve", lambda v, sl=sl, half=half: v.tensor_tensor(
                        sl(), sl(), ps[2 + half][:, :], ALU.subtract),
                        reads=[k_ps[2 + half]], writes=[k_x])
            esL.close()

        b.barrier()
        layer_norm(t_f, k_f, "ln1")
        tap("h1", lambda: t_f[:, :], [k_f], [128, KC * TPC], F32)

        esF = ExitStack()
        t_h1b = esF.enter_context(nc.sbuf_tensor("h1b", [128, KC * TPC], BF16))
        t_lsb = esF.enter_context(nc.sbuf_tensor("lsb", [128, NE], F32))
        t_ysb = esF.enter_context(nc.sbuf_tensor("ysb", [128, NE], F32))
        t_msk = esF.enter_context(nc.sbuf_tensor("msk", [128, NE], F32))
        t_red = esF.enter_context(nc.sbuf_tensor("red", [128, 4], F32))
        t_ctm = esF.enter_context(nc.sbuf_tensor("ctm", [128, NE], BF16))
        t_cT = esF.enter_context(nc.sbuf_tensor("cT", [16, TPC], BF16))
        k_h1b, k_lsb, k_ysb, k_msk, k_red, k_ctm, k_cT = (
            T(), T(), T(), T(), T(), T(), T())
        b.emit("pool", lambda g: g.dma_start(out=t_h1b[:, :], in_=t_f[:, :]),
               reads=[k_f], writes=[k_h1b], dma=True, dkey="misc")

        tap("h1b", lambda: t_h1b[:, :], [k_h1b], [128, KC * TPC], BF16)
        for tc in range(NCH):
            fns = []
            for kci in range(KC):
                fns.append(lambda t, kci=kci, tc=tc: t.matmul(
                    ps[6][:, 0:NE],
                    t_f[:, kci * TPC + tc * 128: kci * TPC + tc * 128 + 128],
                    t_rwt[:, kci * NE:(kci + 1) * NE],
                    start=(kci == 0), stop=(kci == KC - 1)))
            b.emit("pe", None, fns=fns, reads=[k_f, k_rwt], writes=[k_ps[6]])
            b.emit("act", lambda s: s.copy(t_lsb[:, :], ps[6][:, 0:NE]),
                   reads=[k_ps[6]], writes=[k_lsb])
            b.emit("dve", lambda v: v.tensor_tensor(
                t_ysb[:, :], t_lsb[:, :], t_biasrep[:, :], ALU.add),
                reads=[k_lsb, k_consts], writes=[k_ysb])
            b.emit("dve", lambda v: v.reduce_max(
                t_red[:, 0:1], t_ysb[:, :], axis=AXX),
                reads=[k_ysb], writes=[k_red])
            b.emit("dve", lambda v: v.tensor_single_scalar(
                t_msk[:, :], t_ysb[:, :], t_red[:, 0:1], ALU.is_ge),
                reads=[k_ysb, k_red], writes=[k_msk])
            b.emit("dve", lambda v: v.scalar_tensor_tensor(
                t_ysb[:, :], t_msk[:, :], t_cvec[:, 4:5], t_ysb[:, :],
                ALU.mult, ALU.add),
                reads=[k_msk, k_cvec], writes=[k_ysb])
            b.emit("dve", lambda v: v.reduce_max(
                t_red[:, 1:2], t_ysb[:, :], axis=AXX),
                reads=[k_ysb], writes=[k_red])
            b.emit("dve", lambda v: v.tensor_single_scalar(
                t_ysb[:, :], t_ysb[:, :], t_red[:, 1:2], ALU.is_ge),
                reads=[k_red], writes=[k_ysb])
            b.emit("dve", lambda v: v.tensor_tensor(
                t_msk[:, :], t_msk[:, :], t_ysb[:, :], ALU.add),
                reads=[k_ysb], writes=[k_msk])
            b.emit("dve", lambda v: v.reduce_max(
                t_red[:, 2:3], t_lsb[:, :], axis=AXX),
                reads=[k_lsb], writes=[k_red])
            b.emit("dve", lambda v: v.tensor_single_scalar(
                t_lsb[:, :], t_lsb[:, :], t_red[:, 2:3], ALU.subtract),
                reads=[k_red], writes=[k_lsb])
            b.emit("act", lambda s: s.activation(
                t_lsb[:, :], t_lsb[:, :], AF.Exp),
                reads=[], writes=[k_lsb])
            b.emit("dve", lambda v: v.tensor_tensor(
                t_lsb[:, :], t_lsb[:, :], t_msk[:, :], ALU.mult),
                reads=[k_msk], writes=[k_lsb])
            b.emit("dve", lambda v: v.reduce_sum(
                t_red[:, 3:4], t_lsb[:, :], axis=AXX),
                reads=[k_lsb], writes=[k_red])
            b.emit("dve", lambda v: v.reciprocal(
                t_red[:, 3:4], t_red[:, 3:4]), reads=[], writes=[k_red])
            b.emit("dve", lambda v: v.tensor_scalar(
                t_ctm[:, :], t_lsb[:, :], t_red[:, 3:4], None, ALU.mult),
                reads=[k_lsb, k_red], writes=[k_ctm])
            if tc == 7:
                tap("rlogit", lambda: t_lsb[:, :], [k_lsb], [128, NE], F32)
                tap("rmsk", lambda: t_msk[:, :], [k_msk], [128, NE], F32)
                tap("rctm", lambda: t_ctm[:, :], [k_ctm], [128, NE], BF16)
            b.emit("pe", lambda t: t.transpose(
                psb[0:NE, 0:128], t_ctm[:, :], t_identb[:, :]),
                reads=[k_ctm, k_consts], writes=[k_psb])
            b.emit("act", lambda s, tc=tc: s.copy(
                t_cT[:, tc * 128:(tc + 1) * 128], psb[0:NE, 0:128]),
                reads=[k_psb], writes=[k_cT])

        tap("cT", lambda: t_cT[:, :], [k_cT], [16, TPC], BF16)

        b.barrier()
        t_usil = esF.enter_context(nc.sbuf_tensor("usil", [128, EKC * TPC], BF16))
        t_hb = [esF.enter_context(nc.sbuf_tensor(f"hb{i}", [128, EKC * TPC], BF16))
                for i in range(2)]
        t_cb = esF.enter_context(nc.sbuf_tensor("cb", [128, TPC], BF16))
        k_usil, k_cb = T(), T()
        k_hb = [T(), T()]

        for pr in range(NPAIR):
            b.barrier(2)
            for ei in range(2):
                e_idx = pr * 2 + ei
                for wgt, dram_w, is_u in ((0, d_wue, True), (1, d_wge, False)):
                    for mp in range(0, EKC, 2):
                        sg = (mp // 2) % 2
                        b.emit("pool", lambda g, e_idx=e_idx, mp=mp, sg=sg,
                               dram_w=dram_w: g.dma_start(
                                   out=t_stg[sg][:, 0:2 * EMBED].rearrange(
                                       "p (m f) -> p m f", m=2),
                                   in_=dram_w.ap()[e_idx, mp:mp + 2, :, :]
                                   .rearrange("m p f -> p m f")),
                               writes=[k_stg[sg]], dma=True, dkey=f"stg{sg}")
                        for mi in range(2):
                            m = mp + mi
                            mm_sweep2(
                                0, 1,
                                lambda kci, sg=sg, mi=mi:
                                t_stg[sg][:, mi * EMBED + kci * 128:
                                          mi * EMBED + (kci + 1) * 128],
                                lambda kci, half:
                                t_h1b[:, kci * TPC + half * 512:
                                      kci * TPC + half * 512 + 512],
                                KC, (k_stg[sg], k_h1b))
                            for half in range(2):
                                bank = half
                                dst = lambda m=m, half=half: (
                                    m * TPC + half * 512, m * TPC + half * 512 + 512)
                                if is_u:
                                    b.emit("act", lambda s, d=dst(), bank=bank:
                                           s.activation(t_usil[:, d[0]:d[1]],
                                                        ps[bank][:, :], AF.Silu),
                                           reads=[k_ps[bank]], writes=[k_usil])
                                else:
                                    b.emit("dve", lambda v, d=dst(), bank=bank,
                                           ei=ei: v.tensor_tensor(
                                               t_hb[ei][:, d[0]:d[1]],
                                               ps[bank][:, :],
                                               t_usil[:, d[0]:d[1]], ALU.mult),
                                           reads=[k_ps[bank], k_usil],
                                           writes=[k_hb[ei]])
                if pr == 0 and ei == 0:
                    tap("usil0", lambda: t_usil[:, :], [k_usil],
                        [128, EKC * TPC], BF16)
                    tap("hb0raw", lambda: t_hb[0][:, :], [k_hb[0]],
                        [128, EKC * TPC], BF16)
                if e_idx < NE:
                    for half in range(2):
                        bank = 2 + half
                        b.emit("pe", lambda t, e_idx=e_idx, half=half, bank=bank:
                               t.matmul(ps[bank][:, :],
                                        t_rsel[:, e_idx * 128:(e_idx + 1) * 128],
                                        t_cT[:, half * 512:half * 512 + 512],
                                        start=True, stop=True),
                               reads=[k_cT, k_consts], writes=[k_ps[bank]])
                        b.emit("act", lambda s, half=half, bank=bank: s.copy(
                            t_cb[:, half * 512:half * 512 + 512],
                            ps[bank][:, :]),
                            reads=[k_ps[bank]], writes=[k_cb])
                    for m in range(EKC):
                        b.emit("dve", lambda v, m=m, ei=ei: v.tensor_tensor(
                            t_hb[ei][:, m * TPC:(m + 1) * TPC],
                            t_hb[ei][:, m * TPC:(m + 1) * TPC],
                            t_cb[:, :], ALU.mult),
                            reads=[k_cb], writes=[k_hb[ei]])
            if pr == 0:
                tap("hb0", lambda: t_hb[0][:, :], [k_hb[0]],
                    [128, EKC * TPC], BF16)
            for mp in range(0, 16, 2):
                sg = (mp // 2) % 2
                b.emit("pool", lambda g, pr=pr, mp=mp, sg=sg: g.dma_start(
                    out=t_stg[sg][:, 0:4096].rearrange("p (m i f) -> p m i f",
                                                       m=2, i=2),
                    in_=d_wdp.ap()[pr, mp:mp + 2, :, :, :]
                    .rearrange("m i p f -> p m i f")),
                    writes=[k_stg[sg]], dma=True, dkey=f"stg{sg}")
                for mi in range(2):
                    m = mp + mi
                    fns = []
                    for ei in range(2):
                        for ke in range(EKC):
                            for half, bank in ((0, 0), (1, 1)):
                                fns.append(lambda t, sg=sg, mi=mi, ei=ei, ke=ke,
                                           half=half, bank=bank: t.matmul(
                                               ps[bank][:, :],
                                               t_stg[sg][:, (mi * 2 + ei) * 1024 +
                                                         ke * 128:
                                                         (mi * 2 + ei) * 1024 +
                                                         (ke + 1) * 128],
                                               t_hb[ei][:, ke * TPC + half * 512:
                                                        ke * TPC + half * 512 + 512],
                                               start=(ei == 0 and ke == 0),
                                               stop=(ei == 1 and ke == EKC - 1),
                                               skip_group_check=True))
                    b.emit("pe", None, fns=fns,
                           reads=[k_stg[sg], k_hb[0], k_hb[1]],
                           writes=[k_ps[0], k_ps[1]])
                    for half in range(2):
                        bank = half
                        b.emit("dve", lambda v, m=m, half=half, bank=bank:
                               v.tensor_tensor(
                                   t_f[:, m * TPC + half * 512:
                                       m * TPC + half * 512 + 512],
                                   t_f[:, m * TPC + half * 512:
                                       m * TPC + half * 512 + 512],
                                   ps[bank][:, :], ALU.add),
                               reads=[k_ps[bank]], writes=[k_f])
        tap("f", lambda: t_f[:, :], [k_f], [128, KC * TPC], F32)
        esF.close()

        b.barrier()
        layer_norm(t_f, k_f, "ln2")

        out_tok = b.emit("sync", lambda g: g.dma_start(
            out=d_out.ap().rearrange("(kc p) t -> p kc t", p=128),
            in_=t_f.ap().rearrange("p (kc t) -> p kc t", kc=KC)),
            reads=[k_f], dma=True, dkey="out")
        b.raw("sync", lambda g, s=out_tok[0], v=out_tok[1]: g.wait_ge(s, v))

        with nc.Block() as block:
            b.run_blocks(block)
    return nc


_NC_CACHE = None
_TAP_RESULTS = {}


def kernel(**inputs):
    global _NC_CACHE
    x = np.asarray(inputs["L0_attn_x"], np.float32)
    shared = host_weights(inputs)
    if _NC_CACHE is None:
        _NC_CACHE = build_program()
    nc = _NC_CACHE
    in_maps = []
    for c in range(NCORES):
        bi, cb = c // 4, c % 4
        xt = np.ascontiguousarray(x[bi, cb * TPC:(cb + 1) * TPC, :].T)
        m = dict(shared)
        m["xt"] = xt
        m.update(host_constants(c))
        in_maps.append(m)
    res = run_bass_kernel_spmd(nc, in_maps, core_ids=list(range(NCORES)))
    outs = []
    for c in range(NCORES):
        outs.append(res.results[c]["out_t"].T)
        for kk, vv in res.results[c].items():
            if kk.startswith("tap_"):
                _TAP_RESULTS.setdefault(kk, []).append(np.asarray(vv))
    full = np.concatenate(outs, axis=0).reshape(2, 4096, EMBED)
    return full.astype(np.float32)


# version 03
# speedup vs baseline: 1.1898x; 1.1898x over previous
import numpy as np
import ml_dtypes
import concourse.bass as bass
import concourse.mybir as mybir
from concourse.bass_utils import run_bass_kernel_spmd
from contextlib import ExitStack

F32 = mybir.dt.float32
BF16 = mybir.dt.bfloat16
I32 = mybir.dt.int32
AF = mybir.ActivationFunctionType
ALU = mybir.AluOpType
AXX = mybir.AxisListType.X

EMBED = 2048
HEADS = 16
HD = 128
C = 128
KC = 16
TPC = 1024
NCH = 8
NCORES = 8
NE = 16
NEX = 18
NPAIR = 9
EKC = 8
EPS_G = 1e-6
EPS_LN = 1e-5
BASE = 10000.0

_DEBUG_TAPS = set()
_PARANOID = 1

BF = ml_dtypes.bfloat16


def _log_lam():
    h = np.arange(HEADS, dtype=np.float64)
    return -(2.0 ** (-8.0 * (h + 1.0) / HEADS))


def host_constants(core):
    ll = _log_lam()
    cb = core % 4
    t0 = cb * TPC
    half = HD // 2
    theta = BASE ** (-2.0 * np.arange(half, dtype=np.float64) / HD)
    pos = np.arange(t0, t0 + TPC, dtype=np.float64)
    ang = pos[None, :] * np.concatenate([theta, theta])[:, None]
    cosb = np.cos(ang).astype(BF)
    sinb = np.sin(ang).astype(BF)

    i = np.arange(C, dtype=np.float64)
    diff = i[:, None] - i[None, :]
    dmask = np.where(diff[None] >= 0, np.exp(ll[:, None, None] * diff[None]), 0.0)
    dt_c = np.transpose(dmask, (0, 2, 1)).transpose(1, 0, 2).reshape(C, HEADS * C)
    dt_c = np.ascontiguousarray(dt_c).astype(BF)
    dq = np.exp(ll[:, None] * (i[None, :] + 1.0))
    dqr = np.broadcast_to(dq[None], (C, HEADS, C)).reshape(C, HEADS * C)
    dqr = np.ascontiguousarray(dqr).astype(BF)
    dkcol = np.exp(ll[None, :] * (C - 1.0 - i[:, None])).astype(np.float32)
    lam_C = np.exp(ll * C)
    lamcol = np.broadcast_to(lam_C[None, :], (C, HEADS)).astype(np.float32)
    lamcol = np.ascontiguousarray(lamcol)
    lj = lam_C[:, None] ** np.arange(NCH)[None, :]
    lj = np.where(lj < 1e-30, 0.0, lj)
    lamj = np.broadcast_to(lj.reshape(1, -1), (C, HEADS * NCH)).astype(np.float32)
    lamj = np.ascontiguousarray(lamj)

    identb = np.eye(128, dtype=BF)
    perm = np.zeros((128, 128), np.float32)
    for m in range(64):
        perm[m + 64, m] = -1.0
        perm[m, m + 64] = 1.0
    permb = perm.astype(BF)
    ohb = np.zeros((128, HEADS * 16), np.float32)
    for h in range(HEADS):
        ohb[:, h * 16 + h] = 1.0
    ohb = ohb.astype(BF)
    rsel = np.zeros((16, HEADS * 128), np.float32)
    for h in range(HEADS):
        rsel[h, h * 128:(h + 1) * 128] = 1.0
    rsel = rsel.astype(BF)
    onesf = np.ones((128, 1), np.float32)
    onesrow = np.ones((1, 128), np.float32)
    cvec = np.zeros((128, 8), np.float32)
    cvec[:, 0] = 1.0 / HD
    cvec[:, 1] = EPS_G
    cvec[:, 2] = 1.0 / EMBED
    cvec[:, 3] = EPS_LN
    cvec[:, 4] = -1e9
    cvec[:, 5] = 1.0
    lam_B = np.exp(ll * C * NCH)
    wcol3 = np.zeros((128, 3 * HEADS), np.float32)
    poff = np.zeros((1, 4), np.int32)
    for i3 in range(3):
        src = cb - 1 - i3
        r = (core - 1 - i3) % NCORES
        poff[0, i3] = r * EMBED * 128
        if src >= 0:
            w = lam_B ** i3
            w = np.where(w < 1e-30, 0.0, w)
            wcol3[:, i3 * HEADS:(i3 + 1) * HEADS] = w[None, :].astype(np.float32)
    return dict(cosb=cosb, sinb=sinb, dt_c=dt_c, dqr=dqr, dkcol=dkcol,
                lamcol=lamcol, lamj=lamj, identb=identb, permb=permb, ohb=ohb,
                rsel=rsel, onesf=onesf, onesrow=onesrow, cvec=cvec,
                wcol3=wcol3, poff=poff)


def host_weights(inp):
    def mpanels(W, n_in_kc, n_m):
        return np.ascontiguousarray(
            W.reshape(n_in_kc, 128, n_m, 128).transpose(2, 1, 0, 3)
            .reshape(n_m, 128, n_in_kc * 128))

    out = {}
    out["wq"] = mpanels(np.asarray(inp["L0_attn_Wq"], np.float32), KC, 16)
    out["wk"] = mpanels(np.asarray(inp["L0_attn_Wk"], np.float32), KC, 16)
    out["wo"] = mpanels(np.asarray(inp["L0_attn_Wo"], np.float32), KC, 16)
    wv = np.asarray(inp["L0_attn_Wv"], np.float32)
    out["wv"] = np.ascontiguousarray(
        wv.reshape(EMBED, 4, 512).transpose(1, 0, 2))
    out["wg1"] = mpanels(np.asarray(inp["L0_attn_Wg1"], np.float32), KC, 1)[0]
    out["wg2"] = np.asarray(inp["L0_attn_Wg2"], np.float32)
    out["rwt"] = np.ascontiguousarray(
        np.asarray(inp["L0_ffn_router_w"], np.float32).T.reshape(KC, 128, NE))
    out["biasrep"] = np.ascontiguousarray(np.broadcast_to(
        np.asarray(inp["L0_ffn_router_bias"], np.float32)[None, :], (128, NE)))
    out["nwcol"] = np.ascontiguousarray(
        np.asarray(inp["L0_attn_norm_weight"], np.float32).reshape(HEADS, 128).T)

    wu = np.asarray(inp["L0_ffn_Wu"], np.float32)
    wg = np.asarray(inp["L0_ffn_Wg"], np.float32)
    su = np.asarray(inp["L0_ffn_Su"], np.float32)[0]
    sg = np.asarray(inp["L0_ffn_Sg"], np.float32)[0]
    def upanels(W):
        return mpanels(W, KC, EKC)
    wue = np.empty((NEX, EKC, 128, EMBED), np.float32)
    wge = np.empty((NEX, EKC, 128, EMBED), np.float32)
    for e in range(NE):
        wue[e] = upanels(wu[e])
        wge[e] = upanels(wg[e])
    wue[16] = upanels(su[:, :1024]); wue[17] = upanels(su[:, 1024:])
    wge[16] = upanels(sg[:, :1024]); wge[17] = upanels(sg[:, 1024:])
    out["wue"] = np.ascontiguousarray(wue)
    out["wge"] = np.ascontiguousarray(wge)

    wd = np.asarray(inp["L0_ffn_Wd"], np.float32)
    sd = np.asarray(inp["L0_ffn_Sd"], np.float32)[0]
    def dpanels(W):
        return np.ascontiguousarray(
            W.reshape(EKC, 128, 16, 128).transpose(2, 1, 0, 3)
            .reshape(16, 128, EKC * 128))
    wdx = np.empty((NEX, 16, 128, EKC * 128), np.float32)
    for e in range(NE):
        wdx[e] = dpanels(wd[e])
    wdx[16] = dpanels(sd[:1024, :])
    wdx[17] = dpanels(sd[1024:, :])
    out["wdp"] = np.ascontiguousarray(
        wdx.reshape(NPAIR, 2, 16, 128, EKC * 128).transpose(0, 2, 1, 3, 4))
    return out


class Trk:
    __slots__ = ("w", "rs")

    def __init__(self):
        self.w = None
        self.rs = []


class B:

    ENGS = ("pe", "dve", "act", "pool", "sync")

    def __init__(self, nc):
        self.nc = nc
        self.eng = {}
        self.sems = {}

    DKEYS = {"pool": ("stg0", "stg1", "xbf", "g", "misc"),
             "sync": ("main", "ccin", "xf", "out")}

    def init_sems(self, es):
        for name in self.ENGS:
            sem = es.enter_context(self.nc.semaphore(f"s_{name}"))
            d = dict(sem=sem, cnt=0, waited={}, script=[], dsems={})
            for key in self.DKEYS.get(name, ()):
                d["dsems"][key] = [
                    es.enter_context(self.nc.semaphore(f"s_{name}_{key}")), 0]
            self.eng[name] = d
        self.cc_sem = es.enter_context(self.nc.semaphore("s_cc"))

    def _waits(self, e, deps, skip_self=False):
        need = {}
        for (sem, val) in deps:
            if skip_self and sem is e["sem"]:
                continue
            if need.get(sem, 0) < val:
                need[sem] = val
        for sem, val in sorted(need.items(), key=lambda kv: str(kv[0].name)):
            if e["waited"].get(sem, 0) < val:
                e["waited"][sem] = val
                e["script"].append(
                    lambda eng, s=sem, v=val: eng.wait_ge(s, v))

    def emit(self, en, fn, reads=(), writes=(), dma=False, fns=None,
             dkey="main"):
        e = self.eng[en]
        deps = []
        for t in reads:
            if t.w:
                deps.append(t.w)
        for t in writes:
            if t.w:
                deps.append(t.w)
            deps.extend(t.rs)
        self._waits(e, deps, skip_self=(en == "pe"))
        if dma:
            ds = e["dsems"][dkey]
            ds[1] += 16
            sem, cnt = ds[0], ds[1]
            amt = 16
        else:
            e["cnt"] += 1
            sem, cnt = e["sem"], e["cnt"]
            amt = 1
        if fns is not None:
            for f in fns[:-1]:
                e["script"].append(lambda eng, f=f: f(eng))
            e["script"].append(
                lambda eng, f=fns[-1], s=sem, a=amt: f(eng).then_inc(s, a))
        else:
            e["script"].append(
                lambda eng, f=fn, s=sem, a=amt: f(eng).then_inc(s, a))
        tok = (sem, cnt)
        for t in reads:
            t.rs.append(tok)
        for t in writes:
            t.w = tok
            t.rs = []
        return tok

    def raw(self, en, fn):
        self.eng[en]["script"].append(fn)

    def barrier(self, level=1):
        import kernel as _k
        if _PARANOID < level:
            return
        sems = []
        for name in self.ENGS:
            e = self.eng[name]
            if e["cnt"]:
                sems.append((e["sem"], e["cnt"]))
            for key, sc in e["dsems"].items():
                if sc[1]:
                    sems.append((sc[0], sc[1]))
        for name in self.ENGS:
            e = self.eng[name]
            self._waits(e, [t for t in sems if t[0] is not e["sem"]])

    def run_blocks(self, block):
        block.tensor(lambda eng: [f(eng) for f in self.eng["pe"]["script"]])
        block.vector(lambda eng: [f(eng) for f in self.eng["dve"]["script"]])
        block.scalar(lambda eng: [f(eng) for f in self.eng["act"]["script"]])
        block.gpsimd(lambda eng: [f(eng) for f in self.eng["pool"]["script"]])
        block.sync(lambda eng: [f(eng) for f in self.eng["sync"]["script"]])


def build_program():
    nc = bass.Bass("TRN2", target_bir_lowering=False)
    ll = _log_lam()
    lam_C = np.exp(ll * C)

    d_xt = nc.dram_tensor("xt", [EMBED, TPC], F32, kind="ExternalInput")
    d_wq = nc.dram_tensor("wq", [16, 128, EMBED], F32, kind="ExternalInput")
    d_wk = nc.dram_tensor("wk", [16, 128, EMBED], F32, kind="ExternalInput")
    d_wo = nc.dram_tensor("wo", [16, 128, EMBED], F32, kind="ExternalInput")
    d_wv = nc.dram_tensor("wv", [4, EMBED, 512], F32, kind="ExternalInput")
    d_wg1 = nc.dram_tensor("wg1", [128, EMBED], F32, kind="ExternalInput")
    d_wg2 = nc.dram_tensor("wg2", [128, EMBED], F32, kind="ExternalInput")
    d_rwt = nc.dram_tensor("rwt", [KC, 128, NE], F32, kind="ExternalInput")
    d_biasrep = nc.dram_tensor("biasrep", [128, NE], F32, kind="ExternalInput")
    d_nwcol = nc.dram_tensor("nwcol", [128, HEADS], F32, kind="ExternalInput")
    d_wue = nc.dram_tensor("wue", [NEX, EKC, 128, EMBED], F32, kind="ExternalInput")
    d_wge = nc.dram_tensor("wge", [NEX, EKC, 128, EMBED], F32, kind="ExternalInput")
    d_wdp = nc.dram_tensor("wdp", [NPAIR, 16, 2, 128, EKC * 128], F32,
                           kind="ExternalInput")
    d_cosb = nc.dram_tensor("cosb", [128, TPC], BF16, kind="ExternalInput")
    d_sinb = nc.dram_tensor("sinb", [128, TPC], BF16, kind="ExternalInput")
    d_dt = nc.dram_tensor("dt_c", [128, HEADS * C], BF16, kind="ExternalInput")
    d_dqr = nc.dram_tensor("dqr", [128, HEADS * C], BF16, kind="ExternalInput")
    d_dkcol = nc.dram_tensor("dkcol", [128, HEADS], F32, kind="ExternalInput")
    d_lamcol = nc.dram_tensor("lamcol", [128, HEADS], F32, kind="ExternalInput")
    d_lamj = nc.dram_tensor("lamj", [128, HEADS * NCH], F32, kind="ExternalInput")
    d_identb = nc.dram_tensor("identb", [128, 128], BF16, kind="ExternalInput")
    d_permb = nc.dram_tensor("permb", [128, 128], BF16, kind="ExternalInput")
    d_ohb = nc.dram_tensor("ohb", [128, HEADS * 16], BF16, kind="ExternalInput")
    d_rsel = nc.dram_tensor("rsel", [16, HEADS * 128], BF16, kind="ExternalInput")
    d_onesf = nc.dram_tensor("onesf", [128, 1], F32, kind="ExternalInput")
    d_onesrow = nc.dram_tensor("onesrow", [1, 128], F32, kind="ExternalInput")
    d_cvec = nc.dram_tensor("cvec", [128, 8], F32, kind="ExternalInput")
    d_wcol3 = nc.dram_tensor("wcol3", [128, 3 * HEADS], F32, kind="ExternalInput")
    d_poff = nc.dram_tensor("poff", [1, 4], I32, kind="ExternalInput")
    d_out = nc.dram_tensor("out_t", [EMBED, TPC], F32, kind="ExternalOutput")
    d_ccin = nc.dram_tensor("cc_in", [EMBED, 128], F32)
    d_ccout = nc.dram_tensor("cc_out", [NCORES * EMBED, 128], F32)

    b = B(nc)

    def tap(name, ap_fn, reads, shape, dtype):
        if name not in _DEBUG_TAPS:
            return
        d = nc.dram_tensor("tap_" + name, shape, dtype, kind="ExternalOutput")
        b.emit("sync", lambda g, d=d, a=ap_fn(): g.dma_start(out=d.ap(), in_=a),
               reads=list(reads), dma=True, dkey="out")
    es = ExitStack()
    with es:
        b.init_sems(es)
        sb = lambda name, shape, dt, **kw: es.enter_context(
            nc.sbuf_tensor(name, shape, dt, **kw))
        t_cvec = sb("cvec_s", [128, 8], F32)
        t_identb = sb("identb_s", [128, 128], BF16)
        t_permb = sb("permb_s", [128, 128], BF16)
        t_ohb = sb("ohb_s", [128, HEADS * 16], BF16)
        t_rsel = sb("rsel_s", [16, HEADS * 128], BF16)
        t_onesf = sb("onesf_s", [128, 1], F32)
        t_onesrow = sb("onesrow_s", [1, 128], F32)
        t_biasrep = sb("biasrep_s", [128, NE], F32)
        t_nwcol = sb("nwcol_s", [128, HEADS], F32)
        t_dkcol = sb("dkcol_s", [128, HEADS], F32)
        t_lamcol = sb("lamcol_s", [128, HEADS], F32)
        t_lamj = sb("lamj_s", [128, HEADS * NCH], F32)
        t_wcol3 = sb("wcol3_s", [128, 3 * HEADS], F32)
        t_poff = sb("poff_s", [1, 4], I32)
        t_rwt = sb("rwt_s", [128, KC * NE], F32)
        t_g1 = sb("g1_s", [128, TPC], BF16)
        t_stg = [sb(f"stg{i}", [128, 4096], BF16) for i in range(2)]
        ps = [es.enter_context(nc.psum_tensor(f"ps{i}", [128, 512], F32))
              for i in range(6)]
        psb2 = [es.enter_context(nc.psum_tensor(f"psb{i}", [128, 1024], BF16))
                for i in range(2)]
        psb = psb2[0]

        T = lambda: Trk()
        k_cvec, k_consts = T(), T()
        k_rwt, k_g1 = T(), T()
        k_stg = [T(), T()]
        k_ps = [T() for _ in range(6)]
        k_psb2 = [T(), T()]
        k_psb = k_psb2[0]

        def ld(dst_ap, src_ap, trk, queue="sync", dkey="main"):
            b.emit(queue, lambda g, d=dst_ap, s=src_ap: g.dma_start(out=d, in_=s),
                   writes=[trk], dma=True, dkey=dkey)

        ld(t_cvec[:, :], d_cvec.ap(), k_cvec)
        for dst, src in ((t_identb, d_identb), (t_permb, d_permb),
                         (t_ohb, d_ohb), (t_rsel, d_rsel), (t_onesf, d_onesf),
                         (t_onesrow, d_onesrow), (t_biasrep, d_biasrep),
                         (t_nwcol, d_nwcol), (t_dkcol, d_dkcol),
                         (t_lamcol, d_lamcol), (t_lamj, d_lamj),
                         (t_wcol3, d_wcol3), (t_poff, d_poff)):
            ld(dst[:, :], src.ap(), k_consts)
        ld(t_rwt.ap().rearrange("p (kc n) -> p kc n", kc=KC),
           d_rwt.ap().rearrange("kc p n -> p kc n"), k_rwt)

        b.barrier()
        esA = ExitStack()
        t_obuf = esA.enter_context(nc.sbuf_tensor("obuf", [128, HEADS * TPC], BF16))
        t_qbuf = esA.enter_context(nc.sbuf_tensor("qbuf", [128, HEADS * TPC], BF16))
        t_S = esA.enter_context(nc.sbuf_tensor("S_f", [128, HEADS * HD], F32))
        t_Sb = esA.enter_context(nc.sbuf_tensor("S_b", [128, HEADS * HD], BF16))
        k_obuf = [T() for _ in range(HEADS)]
        k_qbuf = [T() for _ in range(HEADS)]
        k_S = [T() for _ in range(HEADS)]
        k_Sb = [T() for _ in range(HEADS)]
        k_dqr, k_dt = T(), T()

        esA2 = ExitStack()
        t_kfm = esA2.enter_context(nc.sbuf_tensor("kfm", [128, HEADS * TPC], BF16))
        t_scb = esA2.enter_context(nc.sbuf_tensor("scb", [128, C], BF16))
        t_ktb = esA2.enter_context(nc.sbuf_tensor("ktb", [128, C], BF16))
        esX = ExitStack()
        t_xbf = esX.enter_context(nc.sbuf_tensor("xbf", [128, KC * TPC], BF16,
                                                 side="right"))
        k_xbf = T()
        k_kfm = [T() for _ in range(HEADS)]
        k_vtm = [T() for _ in range(NCH * 4)]
        k_scb, k_ktb = T(), T()

        ld(t_xbf.ap().rearrange("p (kc t) -> p kc t", kc=KC),
           d_xt.ap().rearrange("(kc p) t -> p kc t", p=128), k_xbf, queue="pool",
           dkey="xbf")

        esR = ExitStack()
        t_cos = esR.enter_context(nc.sbuf_tensor("cosb_s", [128, TPC], BF16, side="right"))
        t_sin = esR.enter_context(nc.sbuf_tensor("sinb_s", [128, TPC], BF16, side="right"))
        t_qsil = esR.enter_context(nc.sbuf_tensor("qsil", [128, TPC], BF16, side="right"))
        t_rt1 = esR.enter_context(nc.sbuf_tensor("rt1", [128, TPC], BF16, side="right"))
        t_rt2 = esR.enter_context(nc.sbuf_tensor("rt2", [128, TPC], BF16, side="right"))
        k_cos, k_sin, k_qsil, k_rt1, k_rt2 = T(), T(), T(), T(), T()
        ld(t_cos[:, :], d_cosb.ap(), k_cos)
        ld(t_sin[:, :], d_sinb.ap(), k_sin)

        def mm_sweep(bank, lhs_fn, rhs_fn, n_k, reads):
            fns = []
            for kci in range(n_k):
                fns.append(lambda t, l=lhs_fn(kci), r=rhs_fn(kci),
                           st=(kci == 0), sp=(kci == n_k - 1), bk=bank:
                           t.matmul(ps[bk][:, :], l, r, start=st, stop=sp))
            return b.emit("pe", None, fns=fns, reads=list(reads),
                          writes=[k_ps[bank]])

        def qk_phase(dram_w, dest, k_dest):
            for m in range(HEADS):
                sg = m % 2
                b.emit("pool", lambda g, m=m, sg=sg: g.dma_start(
                    out=t_stg[sg][:, :EMBED], in_=dram_w.ap()[m, :, :]),
                    writes=[k_stg[sg]], dma=True, dkey=f"stg{sg}")
                for half in range(2):
                    bank = half
                    mm_sweep(bank,
                             lambda kci, sg=sg: t_stg[sg][:, kci * 128:(kci + 1) * 128],
                             lambda kci, half=half: t_xbf[:, kci * TPC + half * 512:
                                                          kci * TPC + half * 512 + 512],
                             KC, (k_stg[sg], k_xbf))
                    b.emit("act", lambda s, bank=bank, half=half: s.activation(
                        t_qsil[:, half * 512:half * 512 + 512], ps[bank][:, :],
                        AF.Silu), reads=[k_ps[bank]], writes=[k_qsil])
                    b.emit("pe", lambda t, half=half: t.matmul(
                        ps[2 + half][:, :], t_permb[:, :],
                        t_qsil[:, half * 512:half * 512 + 512],
                        start=True, stop=True),
                        reads=[k_qsil, k_consts], writes=[k_ps[2 + half]])
                b.emit("dve", lambda v, m=m: v.tensor_tensor(
                    t_rt1[:, :], t_qsil[:, :], t_cos[:, :], ALU.mult),
                    reads=[k_qsil, k_cos], writes=[k_rt1])
                for half in range(2):
                    b.emit("dve", lambda v, half=half: v.tensor_tensor(
                        t_rt2[:, half * 512:half * 512 + 512],
                        ps[2 + half][:, :],
                        t_sin[:, half * 512:half * 512 + 512], ALU.mult),
                        reads=[k_ps[2 + half], k_sin], writes=[k_rt2])
                b.emit("dve", lambda v, m=m: v.tensor_tensor(
                    dest[:, m * TPC:(m + 1) * TPC], t_rt1[:, :], t_rt2[:, :],
                    ALU.add), reads=[k_rt1, k_rt2], writes=[k_dest[m]])

        qk_phase(d_wq, t_qbuf, k_qbuf)
        b.barrier()
        qk_phase(d_wk, t_kfm, k_kfm)
        b.barrier()
        esR.close()
        tap("qk_q", lambda: t_qbuf[:, :], k_qbuf, [128, HEADS * TPC], BF16)
        tap("qk_k", lambda: t_kfm[:, :], k_kfm, [128, HEADS * TPC], BF16)

        b.emit("pool", lambda g: g.dma_start(
            out=t_stg[0][:, :EMBED], in_=d_wg1.ap()), writes=[k_stg[0]],
            dma=True, dkey="stg0")
        for half in range(2):
            mm_sweep(half,
                     lambda kci: t_stg[0][:, kci * 128:(kci + 1) * 128],
                     lambda kci, half=half: t_xbf[:, kci * TPC + half * 512:
                                                  kci * TPC + half * 512 + 512],
                     KC, (k_stg[0], k_xbf))
            b.emit("act", lambda s, half=half: s.copy(
                t_g1[:, half * 512:half * 512 + 512], ps[half][:, :]),
                reads=[k_ps[half]], writes=[k_g1])

        tap("g1", lambda: t_g1[:, :], [k_g1], [128, TPC], BF16)
        esV = ExitStack()
        t_vtm = esV.enter_context(nc.sbuf_tensor("vtm", [128, NCH * HEADS * HD],
                                                 BF16))
        for nf in range(4):
            for kh in range(2):
                b.emit("pool", lambda g, nf=nf, kh=kh: g.dma_start(
                    out=t_stg[kh].ap().rearrange("p (kc f) -> p kc f", kc=8),
                    in_=d_wv.ap()[nf, kh * 1024:(kh + 1) * 1024, :]
                    .rearrange("(kc p) f -> p kc f", p=128)),
                    writes=[k_stg[kh]], dma=True, dkey=f"stg{kh}")
            for tc in range(NCH):
                bank = tc % 2
                fns = []
                for kci in range(KC):
                    kh, kloc = kci // 8, kci % 8
                    fns.append(lambda t, kci=kci, kh=kh, kloc=kloc, tc=tc,
                               bank=bank: t.matmul(
                                   ps[bank][:, :],
                                   t_xbf[:, kci * TPC + tc * 128:
                                         kci * TPC + tc * 128 + 128],
                                   t_stg[kh][:, kloc * 512:(kloc + 1) * 512],
                                   start=(kci == 0), stop=(kci == KC - 1)))
                b.emit("pe", None, fns=fns,
                       reads=[k_stg[0], k_stg[1], k_xbf], writes=[k_ps[bank]])
                b.emit("act", lambda s, tc=tc, nf=nf, bank=bank: s.copy(
                    t_vtm[:, (tc * HEADS + nf * 4) * HD:
                          (tc * HEADS + nf * 4) * HD + 512], ps[bank][:, :]),
                    reads=[k_ps[bank]], writes=[k_vtm[tc * 4 + nf]])
        esX.close()
        t_dqr = esV.enter_context(nc.sbuf_tensor("dqr_s", [128, HEADS * C], BF16))
        t_dt = esV.enter_context(nc.sbuf_tensor("dt_s", [128, HEADS * C], BF16))
        ld(t_dqr[:, :], d_dqr.ap(), k_dqr)
        ld(t_dt[:, :], d_dt.ap(), k_dt)

        b.barrier()
        tap("v", lambda: t_vtm[:, :], k_vtm, [128, NCH * HEADS * HD], BF16)
        t_scb2 = [t_scb, esV.enter_context(nc.sbuf_tensor("scb1", [128, C], BF16))]
        t_ktb2 = [t_ktb, esV.enter_context(nc.sbuf_tensor("ktb1", [128, C], BF16))]
        k_scb2 = [k_scb, T()]
        k_ktb2 = [k_ktb, T()]
        for j in range(NCH):
            b.barrier(2)
            for h in range(HEADS):
                par = h % 2
                qsl = lambda h=h, j=j: t_qbuf[:, h * TPC + j * C: h * TPC + (j + 1) * C]
                ksl = lambda h=h, j=j: t_kfm[:, h * TPC + j * C: h * TPC + (j + 1) * C]
                vsl = lambda h=h, j=j: t_vtm[:, (j * HEADS + h) * HD:
                                             (j * HEADS + h + 1) * HD]
                k_v = k_vtm[j * 4 + h // 4]
                p_kt, k_kt = psb2[par], k_psb2[par]
                p_sc, k_sc = ps[0 + par], k_ps[0 + par]
                p_o, k_o = ps[4 + par], k_ps[4 + par]
                p_a, k_a = ps[2 + par], k_ps[2 + par]
                t_sc, k_scB = t_scb2[par], k_scb2[par]
                t_kt, k_ktB = t_ktb2[par], k_ktb2[par]
                b.emit("pe", lambda t, ksl=ksl, p_kt=p_kt: t.transpose(
                    p_kt[:, 0:C], ksl(), t_identb[:, :]),
                    reads=[k_kfm[h], k_consts], writes=[k_kt])
                b.emit("act", lambda s, h=h, p_kt=p_kt, t_kt=t_kt: s.activation(
                    t_kt[:, :], p_kt[:, 0:C], AF.Copy,
                    scale=t_dkcol[:, h:h + 1]),
                    reads=[k_kt, k_consts], writes=[k_ktB])
                b.emit("pe", lambda t, ksl=ksl, qsl=qsl, p_sc=p_sc: t.matmul(
                    p_sc[:, 0:C], ksl(), qsl(), start=True, stop=True),
                    reads=[k_kfm[h], k_qbuf[h]], writes=[k_sc])
                b.emit("dve", lambda v, h=h, p_sc=p_sc, t_sc=t_sc: v.tensor_tensor(
                    t_sc[:, :], p_sc[:, 0:C], t_dt[:, h * C:(h + 1) * C],
                    ALU.mult), reads=[k_sc, k_dt], writes=[k_scB])
                b.emit("dve", lambda v, h=h, qsl=qsl: v.tensor_tensor(
                    qsl(), qsl(), t_dqr[:, h * C:(h + 1) * C], ALU.mult),
                    reads=[k_dqr], writes=[k_qbuf[h]])
                if j == 0:
                    b.emit("pe", lambda t, vsl=vsl, p_o=p_o, t_sc=t_sc: t.matmul(
                        p_o[:, 0:C], vsl(), t_sc[:, :], start=True, stop=True),
                        reads=[k_v, k_scB], writes=[k_o])
                else:
                    b.emit("pe", None, fns=[
                        lambda t, vsl=vsl, p_o=p_o, t_sc=t_sc: t.matmul(
                            p_o[:, 0:C], vsl(), t_sc[:, :],
                            start=True, stop=False),
                        lambda t, h=h, qsl=qsl, p_o=p_o: t.matmul(
                            p_o[:, 0:C], t_Sb[:, h * HD:(h + 1) * HD], qsl(),
                            start=False, stop=True)],
                        reads=[k_v, k_scB, k_Sb[h], k_qbuf[h]],
                        writes=[k_o])
                b.emit("act", lambda s, h=h, j=j, p_o=p_o: s.copy(
                    t_obuf[:, h * TPC + j * C: h * TPC + (j + 1) * C],
                    p_o[:, 0:C]), reads=[k_o], writes=[k_obuf[h]])
                b.emit("pe", lambda t, vsl=vsl, p_a=p_a, t_kt=t_kt: t.matmul(
                    p_a[:, 0:C], t_kt[:, :], vsl(), start=True, stop=True),
                    reads=[k_ktB, k_v], writes=[k_a])
                if j == 0:
                    b.emit("dve", lambda v, h=h, p_a=p_a: v.tensor_copy(
                        t_S[:, h * HD:(h + 1) * HD], p_a[:, 0:C]),
                        reads=[k_a], writes=[k_S[h]])
                else:
                    b.emit("dve", lambda v, h=h, p_a=p_a: v.scalar_tensor_tensor(
                        t_S[:, h * HD:(h + 1) * HD], t_S[:, h * HD:(h + 1) * HD],
                        t_lamcol[:, h:h + 1], p_a[:, 0:C], ALU.mult, ALU.add),
                        reads=[k_a, k_consts], writes=[k_S[h]])
                if j < NCH - 1:
                    b.emit("act", lambda s, h=h: s.copy(
                        t_Sb[:, h * HD:(h + 1) * HD], t_S[:, h * HD:(h + 1) * HD]),
                        reads=[k_S[h]], writes=[k_Sb[h]])
        tap("o_scan", lambda: t_obuf[:, :], k_obuf, [128, HEADS * TPC], BF16)
        tap("s_loc", lambda: t_S[:, :], k_S, [128, HEADS * HD], F32)
        esV.close()
        esA2.close()

        b.barrier()
        esC = ExitStack()
        t_G = [esC.enter_context(nc.sbuf_tensor(f"G{i}", [128, HEADS * HD], F32))
               for i in range(3)]
        t_tmpA = esC.enter_context(nc.sbuf_tensor("tmpA", [128, HD], F32))
        t_tmpB = esC.enter_context(nc.sbuf_tensor("tmpB", [128, HD], F32))
        t_Sin = esC.enter_context(nc.sbuf_tensor("Sin", [128, HEADS * HD], BF16))
        t_qt2 = [esC.enter_context(nc.sbuf_tensor(f"qt{i}", [128, C], BF16))
                 for i in range(2)]
        t_gsa = esC.enter_context(nc.sbuf_tensor("gsa", [128, HEADS * TPC], BF16))
        t_sq = esC.enter_context(nc.sbuf_tensor("sqg", [128, TPC], BF16))
        t_rstd = esC.enter_context(nc.sbuf_tensor("rstdg", [16, TPC], F32))
        t_rstdb = esC.enter_context(nc.sbuf_tensor("rstdgb", [16, TPC], BF16))
        k_G = [T() for _ in range(3)]
        k_tmpA, k_tmpB, k_Sin = T(), T(), T()
        k_qt2 = [T(), T()]
        k_gsa, k_sq, k_rstd, k_rstdb = T(), T(), T(), T()
        k_ccin, k_ccout = T(), T()

        b.emit("sync", lambda g: g.dma_start(
            out=d_ccin.ap().rearrange("(h p) e -> p h e", p=128),
            in_=t_S.ap().rearrange("p (h e) -> p h e", h=HEADS)),
            reads=list(k_S), writes=[k_ccin], dma=True, dkey="ccin")

        e = b.eng["pool"]
        b._waits(e, [k_ccin.w])
        cc_sem = b.cc_sem
        e["script"].append(lambda g, s=cc_sem: g.collective_compute(
            "AllGather", ALU.bypass, replica_groups=[list(range(NCORES))],
            ins=[d_ccin.ap().opt()], outs=[d_ccout.ap().opt()]).then_inc(s, 1))
        k_ccout.w = (cc_sem, 1)

        b.emit("pool", lambda g: g.dma_start(
            out=t_stg[0][:, :EMBED], in_=d_wg2.ap()),
            writes=[k_stg[0]], dma=True, dkey="stg0")
        k_wg2v = k_stg[0]
        for h in range(HEADS):
            for half in range(2):
                bank = (h * 2 + half) % 4
                b.emit("pe", lambda t, h=h, half=half, bank=bank: t.matmul(
                    ps[bank][:, :], t_stg[0][:, h * 128:(h + 1) * 128],
                    t_g1[:, half * 512:half * 512 + 512], start=True, stop=True),
                    reads=[k_wg2v, k_g1], writes=[k_ps[bank]])
                b.emit("act", lambda s, h=h, half=half, bank=bank: s.activation(
                    t_gsa[:, h * TPC + half * 512: h * TPC + half * 512 + 512],
                    ps[bank][:, :], AF.Sigmoid),
                    reads=[k_ps[bank]], writes=[k_gsa])

        dsem = e["dsems"]["g"][0]

        def fetch_G(g, s):
            with g.register("gofs") as gofs:
                for i in range(3):
                    g.reg_load(gofs, t_poff[:1, i:i + 1])
                    g.dma_start(
                        out=t_G[i].ap().rearrange("p (h e) -> p h e", h=HEADS),
                        in_=bass.AP(d_ccout, gofs,
                                    [[128, 128], [128 * 128, HEADS], [1, 128]]),
                    ).then_inc(s, 16)
        b._waits(e, [k_ccout.w, k_consts.w])
        e["dsems"]["g"][1] += 48
        g_tok = (dsem, e["dsems"]["g"][1])
        e["script"].append(lambda g, s=dsem: fetch_G(g, s))
        for kk in k_G:
            kk.w = g_tok

        for h in range(HEADS):
            b.emit("dve", lambda v, h=h: v.tensor_scalar(
                t_tmpA[:, :], t_G[0][:, h * HD:(h + 1) * HD],
                t_wcol3[:, h:h + 1], None, ALU.mult),
                reads=[k_G[0], k_consts], writes=[k_tmpA])
            b.emit("dve", lambda v, h=h: v.scalar_tensor_tensor(
                t_tmpB[:, :], t_G[1][:, h * HD:(h + 1) * HD],
                t_wcol3[:, HEADS + h:HEADS + h + 1], t_tmpA[:, :],
                ALU.mult, ALU.add),
                reads=[k_G[1], k_consts, k_tmpA], writes=[k_tmpB])
            b.emit("dve", lambda v, h=h: v.scalar_tensor_tensor(
                t_tmpA[:, :], t_G[2][:, h * HD:(h + 1) * HD],
                t_wcol3[:, 2 * HEADS + h:2 * HEADS + h + 1], t_tmpB[:, :],
                ALU.mult, ALU.add),
                reads=[k_G[2], k_consts, k_tmpB], writes=[k_tmpA])
            b.emit("act", lambda s, h=h: s.copy(
                t_Sin[:, h * HD:(h + 1) * HD], t_tmpA[:, :]),
                reads=[k_tmpA], writes=[k_Sin])

        lj_np = lam_C[:, None] ** np.arange(NCH)[None, :]
        pc = 0
        for h in range(HEADS):
            for j in range(NCH):
                if lj_np[h, j] < 1e-30:
                    continue
                par = pc % 2
                pc += 1
                b.emit("dve", lambda v, h=h, j=j, par=par: v.tensor_scalar(
                    t_qt2[par][:, :],
                    t_qbuf[:, h * TPC + j * C: h * TPC + (j + 1) * C],
                    t_lamj[:, h * NCH + j: h * NCH + j + 1], None, ALU.mult),
                    reads=[k_qbuf[h], k_consts], writes=[k_qt2[par]])
                b.emit("pe", lambda t, h=h, par=par: t.matmul(
                    ps[par][:, 0:C], t_Sin[:, h * HD:(h + 1) * HD],
                    t_qt2[par][:, :], start=True, stop=True),
                    reads=[k_Sin, k_qt2[par]], writes=[k_ps[par]])
                b.emit("dve", lambda v, h=h, j=j, par=par: v.tensor_tensor(
                    t_obuf[:, h * TPC + j * C: h * TPC + (j + 1) * C],
                    t_obuf[:, h * TPC + j * C: h * TPC + (j + 1) * C],
                    ps[par][:, 0:C], ALU.add),
                    reads=[k_ps[par]], writes=[k_obuf[h]])
        tap("o_corr", lambda: t_obuf[:, :], k_obuf, [128, HEADS * TPC], BF16)
        tap("sin_t", lambda: t_Sin[:, :], [k_Sin], [128, HEADS * HD], BF16)
        for h in range(HEADS):
            b.emit("dve", lambda v, h=h: v.tensor_tensor(
                t_obuf[:, h * TPC:(h + 1) * TPC],
                t_obuf[:, h * TPC:(h + 1) * TPC],
                t_gsa[:, h * TPC:(h + 1) * TPC], ALU.mult),
                reads=[k_gsa], writes=[k_obuf[h]])
            b.emit("act", lambda s, h=h: s.activation(
                t_sq[:, :], t_obuf[:, h * TPC:(h + 1) * TPC], AF.Square),
                reads=[k_obuf[h]], writes=[k_sq])
            for half in range(2):
                b.emit("pe", lambda t, h=h, half=half: t.matmul(
                    ps[2 + half][0:16, :], t_ohb[:, h * 16:(h + 1) * 16],
                    t_sq[:, half * 512:half * 512 + 512],
                    start=(h == 0), stop=(h == HEADS - 1)),
                    reads=[k_sq, k_consts], writes=[k_ps[2 + half]])
        tap("xg", lambda: t_obuf[:, :], k_obuf, [128, HEADS * TPC], BF16)
        for half in range(2):
            b.emit("act", lambda s, half=half: s.activation(
                t_rstd[:, half * 512:half * 512 + 512], ps[2 + half][0:16, :],
                AF.Sqrt, bias=t_cvec[0:16, 1:2], scale=t_cvec[0:16, 0:1]),
                reads=[k_ps[2 + half], k_cvec], writes=[k_rstd])
        b.emit("dve", lambda v: v.reciprocal(t_rstd[:, :], t_rstd[:, :]),
               reads=[], writes=[k_rstd])
        b.emit("act", lambda s: s.copy(t_rstdb[:, :], t_rstd[:, :]),
               reads=[k_rstd], writes=[k_rstdb])
        tap("rstd", lambda: t_rstd[:, :], [k_rstd], [16, TPC], F32)
        for h in range(HEADS):
            for half in range(2):
                bank = half
                b.emit("pe", lambda t, h=h, half=half, bank=bank: t.matmul(
                    ps[bank][:, :], t_rsel[:, h * 128:(h + 1) * 128],
                    t_rstdb[:, half * 512:half * 512 + 512],
                    start=True, stop=True),
                    reads=[k_rstdb, k_consts], writes=[k_ps[bank]])
                b.emit("dve", lambda v, h=h, half=half, bank=bank:
                       v.scalar_tensor_tensor(
                           t_obuf[:, h * TPC + half * 512: h * TPC + half * 512 + 512],
                           t_obuf[:, h * TPC + half * 512: h * TPC + half * 512 + 512],
                           t_nwcol[:, h:h + 1], ps[bank][:, :],
                           ALU.mult, ALU.mult),
                       reads=[k_ps[bank], k_consts], writes=[k_obuf[h]])
        tap("o_n", lambda: t_obuf[:, :], k_obuf, [128, HEADS * TPC], BF16)
        esC.close()
        b.barrier()

        b.barrier()
        t_f = sb("f_acc", [128, KC * TPC], F32, side="right")
        k_f = T()
        b.emit("sync", lambda g: g.dma_start(
            out=t_f.ap().rearrange("p (kc t) -> p kc t", kc=KC),
            in_=d_xt.ap().rearrange("(kc p) t -> p kc t", p=128)),
            writes=[k_f], dma=True, dkey="xf")
        for m in range(KC):
            sg = m % 2
            b.emit("pool", lambda g, m=m, sg=sg: g.dma_start(
                out=t_stg[sg][:, :EMBED], in_=d_wo.ap()[m, :, :]),
                writes=[k_stg[sg]], dma=True, dkey=f"stg{sg}")
            for half in range(2):
                bank = half
                mm_sweep(bank,
                         lambda kci, sg=sg: t_stg[sg][:, kci * 128:(kci + 1) * 128],
                         lambda kci, half=half: t_obuf[:, kci * TPC + half * 512:
                                                       kci * TPC + half * 512 + 512],
                         KC, tuple([k_stg[sg]] + k_obuf))
                b.emit("dve", lambda v, m=m, half=half, bank=bank: v.tensor_tensor(
                    t_f[:, m * TPC + half * 512: m * TPC + half * 512 + 512],
                    t_f[:, m * TPC + half * 512: m * TPC + half * 512 + 512],
                    ps[bank][:, :], ALU.add),
                    reads=[k_ps[bank]], writes=[k_f])
        tap("r1", lambda: t_f[:, :], [k_f], [128, KC * TPC], F32)
        esA.close()

        def layer_norm(t_x, k_x, sq_scope_name):
            esL = ExitStack()
            t_sqf = esL.enter_context(nc.sbuf_tensor(
                f"sqf_{sq_scope_name}", [128, TPC], F32))
            t_st = esL.enter_context(nc.sbuf_tensor(
                f"st_{sq_scope_name}", [1, 4 * TPC], F32))
            k_sqf, k_st = T(), T()
            for kci in range(KC):
                b.emit("act", lambda s, kci=kci: s.activation(
                    t_sqf[:, :], t_x[:, kci * TPC:(kci + 1) * TPC], AF.Square),
                    reads=[k_x], writes=[k_sqf])
                for half in range(2):
                    b.emit("pe", lambda t, kci=kci, half=half: t.matmul(
                        ps[4 + half][0:1, :], t_onesf[:, :],
                        t_x[:, kci * TPC + half * 512: kci * TPC + half * 512 + 512],
                        start=(kci == 0), stop=(kci == KC - 1)),
                        reads=[k_x, k_consts], writes=[k_ps[4 + half]])
                    b.emit("pe", lambda t, kci=kci, half=half: t.matmul(
                        ps[2 + half][0:1, :], t_onesf[:, :],
                        t_sqf[:, half * 512:half * 512 + 512],
                        start=(kci == 0), stop=(kci == KC - 1)),
                        reads=[k_sqf, k_consts], writes=[k_ps[2 + half]])
            for half in range(2):
                b.emit("dve", lambda v, half=half: v.tensor_scalar(
                    t_st[0:1, half * 512:half * 512 + 512],
                    ps[4 + half][0:1, :], t_cvec[0:1, 2:3], None, ALU.mult),
                    reads=[k_ps[4 + half], k_cvec], writes=[k_st])
                b.emit("dve", lambda v, half=half: v.tensor_scalar(
                    t_st[0:1, TPC + half * 512: TPC + half * 512 + 512],
                    ps[2 + half][0:1, :], t_cvec[0:1, 2:3], None, ALU.mult),
                    reads=[k_ps[2 + half], k_cvec], writes=[k_st])
            b.emit("dve", lambda v: v.tensor_tensor(
                t_st[0:1, 2 * TPC:3 * TPC], t_st[0:1, 0:TPC], t_st[0:1, 0:TPC],
                ALU.mult), reads=[], writes=[k_st])
            b.emit("dve", lambda v: v.tensor_tensor(
                t_st[0:1, 2 * TPC:3 * TPC], t_st[0:1, TPC:2 * TPC],
                t_st[0:1, 2 * TPC:3 * TPC], ALU.subtract),
                reads=[], writes=[k_st])
            b.emit("act", lambda s: s.activation(
                t_st[0:1, 2 * TPC:3 * TPC], t_st[0:1, 2 * TPC:3 * TPC],
                AF.Sqrt, bias=t_cvec[0:1, 3:4]),
                reads=[k_cvec], writes=[k_st])
            b.emit("dve", lambda v: v.reciprocal(
                t_st[0:1, 2 * TPC:3 * TPC], t_st[0:1, 2 * TPC:3 * TPC]),
                reads=[], writes=[k_st])
            b.emit("dve", lambda v: v.tensor_tensor(
                t_st[0:1, 3 * TPC:4 * TPC], t_st[0:1, 0:TPC],
                t_st[0:1, 2 * TPC:3 * TPC], ALU.mult),
                reads=[], writes=[k_st])
            for half in range(2):
                b.emit("pe", lambda t, half=half: t.matmul(
                    ps[4 + half][:, :], t_onesrow[:, :],
                    t_st[0:1, 2 * TPC + half * 512: 2 * TPC + half * 512 + 512],
                    start=True, stop=True),
                    reads=[k_st, k_consts], writes=[k_ps[4 + half]])
                b.emit("pe", lambda t, half=half: t.matmul(
                    ps[2 + half][:, :], t_onesrow[:, :],
                    t_st[0:1, 3 * TPC + half * 512: 3 * TPC + half * 512 + 512],
                    start=True, stop=True),
                    reads=[k_st, k_consts], writes=[k_ps[2 + half]])
            for kci in range(KC):
                for half in range(2):
                    sl = lambda kci=kci, half=half: t_x[
                        :, kci * TPC + half * 512: kci * TPC + half * 512 + 512]
                    b.emit("dve", lambda v, sl=sl, half=half: v.tensor_tensor(
                        sl(), sl(), ps[4 + half][:, :], ALU.mult),
                        reads=[k_ps[4 + half]], writes=[k_x])
                    b.emit("dve", lambda v, sl=sl, half=half: v.tensor_tensor(
                        sl(), sl(), ps[2 + half][:, :], ALU.subtract),
                        reads=[k_ps[2 + half]], writes=[k_x])
            esL.close()

        b.barrier()
        layer_norm(t_f, k_f, "ln1")
        tap("h1", lambda: t_f[:, :], [k_f], [128, KC * TPC], F32)

        esF = ExitStack()
        t_h1b = esF.enter_context(nc.sbuf_tensor("h1b", [128, KC * TPC], BF16))
        t_lsb = esF.enter_context(nc.sbuf_tensor("lsb", [128, NE], F32))
        t_ysb = esF.enter_context(nc.sbuf_tensor("ysb", [128, NE], F32))
        t_msk = esF.enter_context(nc.sbuf_tensor("msk", [128, NE], F32))
        t_red = esF.enter_context(nc.sbuf_tensor("red", [128, 4], F32))
        t_ctm = esF.enter_context(nc.sbuf_tensor("ctm", [128, NE], BF16))
        t_cT = esF.enter_context(nc.sbuf_tensor("cT", [16, TPC], BF16))
        k_h1b, k_lsb, k_ysb, k_msk, k_red, k_ctm, k_cT = (
            T(), T(), T(), T(), T(), T(), T())
        b.emit("pool", lambda g: g.dma_start(out=t_h1b[:, :], in_=t_f[:, :]),
               reads=[k_f], writes=[k_h1b], dma=True, dkey="misc")

        tap("h1b", lambda: t_h1b[:, :], [k_h1b], [128, KC * TPC], BF16)
        for tc in range(NCH):
            fns = []
            for kci in range(KC):
                fns.append(lambda t, kci=kci, tc=tc: t.matmul(
                    ps[0][:, 0:NE],
                    t_f[:, kci * TPC + tc * 128: kci * TPC + tc * 128 + 128],
                    t_rwt[:, kci * NE:(kci + 1) * NE],
                    start=(kci == 0), stop=(kci == KC - 1)))
            b.emit("pe", None, fns=fns, reads=[k_f, k_rwt], writes=[k_ps[0]])
            b.emit("act", lambda s: s.copy(t_lsb[:, :], ps[0][:, 0:NE]),
                   reads=[k_ps[0]], writes=[k_lsb])
            b.emit("dve", lambda v: v.tensor_tensor(
                t_ysb[:, :], t_lsb[:, :], t_biasrep[:, :], ALU.add),
                reads=[k_lsb, k_consts], writes=[k_ysb])
            b.emit("dve", lambda v: v.reduce_max(
                t_red[:, 0:1], t_ysb[:, :], axis=AXX),
                reads=[k_ysb], writes=[k_red])
            b.emit("dve", lambda v: v.tensor_single_scalar(
                t_msk[:, :], t_ysb[:, :], t_red[:, 0:1], ALU.is_ge),
                reads=[k_ysb, k_red], writes=[k_msk])
            b.emit("dve", lambda v: v.scalar_tensor_tensor(
                t_ysb[:, :], t_msk[:, :], t_cvec[:, 4:5], t_ysb[:, :],
                ALU.mult, ALU.add),
                reads=[k_msk, k_cvec], writes=[k_ysb])
            b.emit("dve", lambda v: v.reduce_max(
                t_red[:, 1:2], t_ysb[:, :], axis=AXX),
                reads=[k_ysb], writes=[k_red])
            b.emit("dve", lambda v: v.tensor_single_scalar(
                t_ysb[:, :], t_ysb[:, :], t_red[:, 1:2], ALU.is_ge),
                reads=[k_red], writes=[k_ysb])
            b.emit("dve", lambda v: v.tensor_tensor(
                t_msk[:, :], t_msk[:, :], t_ysb[:, :], ALU.add),
                reads=[k_ysb], writes=[k_msk])
            b.emit("dve", lambda v: v.reduce_max(
                t_red[:, 2:3], t_lsb[:, :], axis=AXX),
                reads=[k_lsb], writes=[k_red])
            b.emit("dve", lambda v: v.tensor_single_scalar(
                t_lsb[:, :], t_lsb[:, :], t_red[:, 2:3], ALU.subtract),
                reads=[k_red], writes=[k_lsb])
            b.emit("act", lambda s: s.activation(
                t_lsb[:, :], t_lsb[:, :], AF.Exp),
                reads=[], writes=[k_lsb])
            b.emit("dve", lambda v: v.tensor_tensor(
                t_lsb[:, :], t_lsb[:, :], t_msk[:, :], ALU.mult),
                reads=[k_msk], writes=[k_lsb])
            b.emit("dve", lambda v: v.reduce_sum(
                t_red[:, 3:4], t_lsb[:, :], axis=AXX),
                reads=[k_lsb], writes=[k_red])
            b.emit("dve", lambda v: v.reciprocal(
                t_red[:, 3:4], t_red[:, 3:4]), reads=[], writes=[k_red])
            b.emit("dve", lambda v: v.tensor_scalar(
                t_ctm[:, :], t_lsb[:, :], t_red[:, 3:4], None, ALU.mult),
                reads=[k_lsb, k_red], writes=[k_ctm])
            if tc == 7:
                tap("rlogit", lambda: t_lsb[:, :], [k_lsb], [128, NE], F32)
                tap("rmsk", lambda: t_msk[:, :], [k_msk], [128, NE], F32)
                tap("rctm", lambda: t_ctm[:, :], [k_ctm], [128, NE], BF16)
            b.emit("pe", lambda t: t.transpose(
                psb[0:NE, 0:128], t_ctm[:, :], t_identb[:, :]),
                reads=[k_ctm, k_consts], writes=[k_psb])
            b.emit("act", lambda s, tc=tc: s.copy(
                t_cT[:, tc * 128:(tc + 1) * 128], psb[0:NE, 0:128]),
                reads=[k_psb], writes=[k_cT])

        tap("cT", lambda: t_cT[:, :], [k_cT], [16, TPC], BF16)

        b.barrier()
        t_usil = esF.enter_context(nc.sbuf_tensor("usil", [128, EKC * TPC], BF16))
        t_hb = [esF.enter_context(nc.sbuf_tensor(f"hb{i}", [128, EKC * TPC], BF16))
                for i in range(2)]
        t_cb = esF.enter_context(nc.sbuf_tensor("cb", [128, TPC], BF16))
        k_usil, k_cb = T(), T()
        k_hb = [T(), T()]

        for pr in range(NPAIR):
            b.barrier(2)
            for ei in range(2):
                e_idx = pr * 2 + ei
                for wgt, dram_w, is_u in ((0, d_wue, True), (1, d_wge, False)):
                    for mp in range(0, EKC, 2):
                        sg = (mp // 2) % 2
                        b.emit("pool", lambda g, e_idx=e_idx, mp=mp, sg=sg,
                               dram_w=dram_w: g.dma_start(
                                   out=t_stg[sg][:, 0:2 * EMBED].rearrange(
                                       "p (m f) -> p m f", m=2),
                                   in_=dram_w.ap()[e_idx, mp:mp + 2, :, :]
                                   .rearrange("m p f -> p m f")),
                               writes=[k_stg[sg]], dma=True, dkey=f"stg{sg}")
                        for mi in range(2):
                            m = mp + mi
                            for half in range(2):
                                bank = half
                                mm_sweep(
                                    bank,
                                    lambda kci, sg=sg, mi=mi:
                                    t_stg[sg][:, mi * EMBED + kci * 128:
                                              mi * EMBED + (kci + 1) * 128],
                                    lambda kci, half=half:
                                    t_h1b[:, kci * TPC + half * 512:
                                          kci * TPC + half * 512 + 512],
                                    KC, (k_stg[sg], k_h1b))
                                dst = lambda m=m, half=half: (
                                    m * TPC + half * 512, m * TPC + half * 512 + 512)
                                if is_u:
                                    b.emit("act", lambda s, d=dst(), bank=bank:
                                           s.activation(t_usil[:, d[0]:d[1]],
                                                        ps[bank][:, :], AF.Silu),
                                           reads=[k_ps[bank]], writes=[k_usil])
                                else:
                                    b.emit("dve", lambda v, d=dst(), bank=bank,
                                           ei=ei: v.tensor_tensor(
                                               t_hb[ei][:, d[0]:d[1]],
                                               ps[bank][:, :],
                                               t_usil[:, d[0]:d[1]], ALU.mult),
                                           reads=[k_ps[bank], k_usil],
                                           writes=[k_hb[ei]])
                if pr == 0 and ei == 0:
                    tap("usil0", lambda: t_usil[:, :], [k_usil],
                        [128, EKC * TPC], BF16)
                    tap("hb0raw", lambda: t_hb[0][:, :], [k_hb[0]],
                        [128, EKC * TPC], BF16)
                if e_idx < NE:
                    for half in range(2):
                        bank = 2 + half
                        b.emit("pe", lambda t, e_idx=e_idx, half=half, bank=bank:
                               t.matmul(ps[bank][:, :],
                                        t_rsel[:, e_idx * 128:(e_idx + 1) * 128],
                                        t_cT[:, half * 512:half * 512 + 512],
                                        start=True, stop=True),
                               reads=[k_cT, k_consts], writes=[k_ps[bank]])
                        b.emit("act", lambda s, half=half, bank=bank: s.copy(
                            t_cb[:, half * 512:half * 512 + 512],
                            ps[bank][:, :]),
                            reads=[k_ps[bank]], writes=[k_cb])
                    for m in range(EKC):
                        b.emit("dve", lambda v, m=m, ei=ei: v.tensor_tensor(
                            t_hb[ei][:, m * TPC:(m + 1) * TPC],
                            t_hb[ei][:, m * TPC:(m + 1) * TPC],
                            t_cb[:, :], ALU.mult),
                            reads=[k_cb], writes=[k_hb[ei]])
            if pr == 0:
                tap("hb0", lambda: t_hb[0][:, :], [k_hb[0]],
                    [128, EKC * TPC], BF16)
            for mp in range(0, 16, 2):
                sg = (mp // 2) % 2
                b.emit("pool", lambda g, pr=pr, mp=mp, sg=sg: g.dma_start(
                    out=t_stg[sg][:, 0:4096].rearrange("p (m i f) -> p m i f",
                                                       m=2, i=2),
                    in_=d_wdp.ap()[pr, mp:mp + 2, :, :, :]
                    .rearrange("m i p f -> p m i f")),
                    writes=[k_stg[sg]], dma=True, dkey=f"stg{sg}")
                for mi in range(2):
                    m = mp + mi
                    for half in range(2):
                        bank = half
                        fns = []
                        for ei in range(2):
                            for ke in range(EKC):
                                fns.append(lambda t, sg=sg, mi=mi, ei=ei, ke=ke,
                                           half=half, bank=bank: t.matmul(
                                               ps[bank][:, :],
                                               t_stg[sg][:, (mi * 2 + ei) * 1024 +
                                                         ke * 128:
                                                         (mi * 2 + ei) * 1024 +
                                                         (ke + 1) * 128],
                                               t_hb[ei][:, ke * TPC + half * 512:
                                                        ke * TPC + half * 512 + 512],
                                               start=(ei == 0 and ke == 0),
                                               stop=(ei == 1 and ke == EKC - 1)))
                        b.emit("pe", None, fns=fns,
                               reads=[k_stg[sg], k_hb[0], k_hb[1]],
                               writes=[k_ps[bank]])
                        b.emit("dve", lambda v, m=m, half=half, bank=bank:
                               v.tensor_tensor(
                                   t_f[:, m * TPC + half * 512:
                                       m * TPC + half * 512 + 512],
                                   t_f[:, m * TPC + half * 512:
                                       m * TPC + half * 512 + 512],
                                   ps[bank][:, :], ALU.add),
                               reads=[k_ps[bank]], writes=[k_f])
        tap("f", lambda: t_f[:, :], [k_f], [128, KC * TPC], F32)
        esF.close()

        b.barrier()
        layer_norm(t_f, k_f, "ln2")

        out_tok = b.emit("sync", lambda g: g.dma_start(
            out=d_out.ap().rearrange("(kc p) t -> p kc t", p=128),
            in_=t_f.ap().rearrange("p (kc t) -> p kc t", kc=KC)),
            reads=[k_f], dma=True, dkey="out")
        b.raw("sync", lambda g, s=out_tok[0], v=out_tok[1]: g.wait_ge(s, v))

        with nc.Block() as block:
            b.run_blocks(block)
    return nc


_NC_CACHE = None
_TAP_RESULTS = {}


def kernel(**inputs):
    global _NC_CACHE
    x = np.asarray(inputs["L0_attn_x"], np.float32)
    shared = host_weights(inputs)
    if _NC_CACHE is None:
        _NC_CACHE = build_program()
    nc = _NC_CACHE
    in_maps = []
    for c in range(NCORES):
        bi, cb = c // 4, c % 4
        xt = np.ascontiguousarray(x[bi, cb * TPC:(cb + 1) * TPC, :].T)
        m = dict(shared)
        m["xt"] = xt
        m.update(host_constants(c))
        in_maps.append(m)
    res = run_bass_kernel_spmd(nc, in_maps, core_ids=list(range(NCORES)))
    outs = []
    for c in range(NCORES):
        outs.append(res.results[c]["out_t"].T)
        for kk, vv in res.results[c].items():
            if kk.startswith("tap_"):
                _TAP_RESULTS.setdefault(kk, []).append(np.asarray(vv))
    full = np.concatenate(outs, axis=0).reshape(2, 4096, EMBED)
    return full.astype(np.float32)


# version 06
# speedup vs baseline: 1.2029x; 1.0111x over previous
import numpy as np
import ml_dtypes
import concourse.bass as bass
import concourse.mybir as mybir
from concourse.bass_utils import run_bass_kernel_spmd
from contextlib import ExitStack

F32 = mybir.dt.float32
BF16 = mybir.dt.bfloat16
I32 = mybir.dt.int32
AF = mybir.ActivationFunctionType
ALU = mybir.AluOpType
AXX = mybir.AxisListType.X

EMBED = 2048
HEADS = 16
HD = 128
C = 128
KC = 16
TPC = 1024
NCH = 8
NCORES = 8
NE = 16
NEX = 18
NPAIR = 9
EKC = 8
EPS_G = 1e-6
EPS_LN = 1e-5
BASE = 10000.0

_DEBUG_TAPS = set()
_PARANOID = 1

BF = ml_dtypes.bfloat16


def _log_lam():
    h = np.arange(HEADS, dtype=np.float64)
    return -(2.0 ** (-8.0 * (h + 1.0) / HEADS))


def host_constants(core):
    ll = _log_lam()
    cb = core % 4
    t0 = cb * TPC
    half = HD // 2
    theta = BASE ** (-2.0 * np.arange(half, dtype=np.float64) / HD)
    pos = np.arange(t0, t0 + TPC, dtype=np.float64)
    ang = pos[None, :] * np.concatenate([theta, theta])[:, None]
    cosb = np.cos(ang).astype(BF)
    sinb = np.sin(ang).astype(BF)

    i = np.arange(C, dtype=np.float64)
    diff = i[:, None] - i[None, :]
    dmask = np.where(diff[None] >= 0, np.exp(ll[:, None, None] * diff[None]), 0.0)
    dt_c = np.transpose(dmask, (0, 2, 1)).transpose(1, 0, 2).reshape(C, HEADS * C)
    dt_c = np.ascontiguousarray(dt_c).astype(BF)
    dq = np.exp(ll[:, None] * (i[None, :] + 1.0))
    dqr = np.broadcast_to(dq[None], (C, HEADS, C)).reshape(C, HEADS * C)
    dqr = np.ascontiguousarray(dqr).astype(BF)
    dkcol = np.exp(ll[None, :] * (C - 1.0 - i[:, None])).astype(np.float32)
    lam_C = np.exp(ll * C)
    lamcol = np.broadcast_to(lam_C[None, :], (C, HEADS)).astype(np.float32)
    lamcol = np.ascontiguousarray(lamcol)
    lj = lam_C[:, None] ** np.arange(NCH)[None, :]
    lj = np.where(lj < 1e-30, 0.0, lj)
    lamj = np.broadcast_to(lj.reshape(1, -1), (C, HEADS * NCH)).astype(np.float32)
    lamj = np.ascontiguousarray(lamj)

    identb = np.eye(128, dtype=BF)
    perm = np.zeros((128, 128), np.float32)
    for m in range(64):
        perm[m + 64, m] = -1.0
        perm[m, m + 64] = 1.0
    permb = perm.astype(BF)
    ohb = np.zeros((128, HEADS * 16), np.float32)
    for h in range(HEADS):
        ohb[:, h * 16 + h] = 1.0
    ohb = ohb.astype(BF)
    rsel = np.zeros((16, HEADS * 128), np.float32)
    for h in range(HEADS):
        rsel[h, h * 128:(h + 1) * 128] = 1.0
    rsel = rsel.astype(BF)
    onesf = np.ones((128, 1), np.float32)
    onesrow = np.ones((1, 128), np.float32)
    cvec = np.zeros((128, 8), np.float32)
    cvec[:, 0] = 1.0 / HD
    cvec[:, 1] = EPS_G
    cvec[:, 2] = 1.0 / EMBED
    cvec[:, 3] = EPS_LN
    cvec[:, 4] = -1e9
    cvec[:, 5] = 1.0
    lam_B = np.exp(ll * C * NCH)
    wcol3 = np.zeros((128, 3 * HEADS), np.float32)
    poff = np.zeros((1, 4), np.int32)
    for i3 in range(3):
        src = cb - 1 - i3
        r = (core - 1 - i3) % NCORES
        poff[0, i3] = r * EMBED * 128
        if src >= 0:
            w = lam_B ** i3
            w = np.where(w < 1e-30, 0.0, w)
            wcol3[:, i3 * HEADS:(i3 + 1) * HEADS] = w[None, :].astype(np.float32)
    return dict(cosb=cosb, sinb=sinb, dt_c=dt_c, dqr=dqr, dkcol=dkcol,
                lamcol=lamcol, lamj=lamj, identb=identb, permb=permb, ohb=ohb,
                rsel=rsel, onesf=onesf, onesrow=onesrow, cvec=cvec,
                wcol3=wcol3, poff=poff)


def host_weights(inp):
    def mpanels(W, n_in_kc, n_m):
        return np.ascontiguousarray(
            W.reshape(n_in_kc, 128, n_m, 128).transpose(2, 1, 0, 3)
            .reshape(n_m, 128, n_in_kc * 128))

    out = {}
    out["wq"] = mpanels(np.asarray(inp["L0_attn_Wq"], np.float32), KC, 16)
    out["wk"] = mpanels(np.asarray(inp["L0_attn_Wk"], np.float32), KC, 16)
    out["wo"] = mpanels(np.asarray(inp["L0_attn_Wo"], np.float32), KC, 16)
    wv = np.asarray(inp["L0_attn_Wv"], np.float32)
    out["wv"] = np.ascontiguousarray(
        wv.reshape(EMBED, 4, 512).transpose(1, 0, 2))
    out["wg1"] = mpanels(np.asarray(inp["L0_attn_Wg1"], np.float32), KC, 1)[0]
    out["wg2"] = np.asarray(inp["L0_attn_Wg2"], np.float32)
    out["rwt"] = np.ascontiguousarray(
        np.asarray(inp["L0_ffn_router_w"], np.float32).T.reshape(KC, 128, NE))
    out["biasrep"] = np.ascontiguousarray(np.broadcast_to(
        np.asarray(inp["L0_ffn_router_bias"], np.float32)[None, :], (128, NE)))
    out["nwcol"] = np.ascontiguousarray(
        np.asarray(inp["L0_attn_norm_weight"], np.float32).reshape(HEADS, 128).T)

    wu = np.asarray(inp["L0_ffn_Wu"], np.float32)
    wg = np.asarray(inp["L0_ffn_Wg"], np.float32)
    su = np.asarray(inp["L0_ffn_Su"], np.float32)[0]
    sg = np.asarray(inp["L0_ffn_Sg"], np.float32)[0]
    def upanels(W):
        return mpanels(W, KC, EKC)
    wue = np.empty((NEX, EKC, 128, EMBED), np.float32)
    wge = np.empty((NEX, EKC, 128, EMBED), np.float32)
    for e in range(NE):
        wue[e] = upanels(wu[e])
        wge[e] = upanels(wg[e])
    wue[16] = upanels(su[:, :1024]); wue[17] = upanels(su[:, 1024:])
    wge[16] = upanels(sg[:, :1024]); wge[17] = upanels(sg[:, 1024:])
    out["wue"] = np.ascontiguousarray(wue)
    out["wge"] = np.ascontiguousarray(wge)

    wd = np.asarray(inp["L0_ffn_Wd"], np.float32)
    sd = np.asarray(inp["L0_ffn_Sd"], np.float32)[0]
    def dpanels(W):
        return np.ascontiguousarray(
            W.reshape(EKC, 128, 16, 128).transpose(2, 1, 0, 3)
            .reshape(16, 128, EKC * 128))
    wdx = np.empty((NEX, 16, 128, EKC * 128), np.float32)
    for e in range(NE):
        wdx[e] = dpanels(wd[e])
    wdx[16] = dpanels(sd[:1024, :])
    wdx[17] = dpanels(sd[1024:, :])
    out["wdp"] = np.ascontiguousarray(
        wdx.reshape(NPAIR, 2, 16, 128, EKC * 128).transpose(0, 2, 1, 3, 4))
    return out


class Trk:
    __slots__ = ("w", "rs")

    def __init__(self):
        self.w = None
        self.rs = []


class B:

    ENGS = ("pe", "dve", "act", "pool", "sync")

    def __init__(self, nc):
        self.nc = nc
        self.eng = {}
        self.sems = {}

    DKEYS = {"pool": ("stg0", "stg1", "xbf", "g", "misc"),
             "sync": ("main", "rwt", "cos", "sin", "dqr", "dtm",
                      "ccin", "xf", "out")}

    def init_sems(self, es):
        for name in self.ENGS:
            sem = es.enter_context(self.nc.semaphore(f"s_{name}"))
            d = dict(sem=sem, cnt=0, waited={}, script=[], dsems={})
            for key in self.DKEYS.get(name, ()):
                d["dsems"][key] = [
                    es.enter_context(self.nc.semaphore(f"s_{name}_{key}")), 0]
            self.eng[name] = d
        self.cc_sem = es.enter_context(self.nc.semaphore("s_cc"))

    def _waits(self, e, deps, skip_self=False):
        need = {}
        for (sem, val) in deps:
            if skip_self and sem is e["sem"]:
                continue
            if need.get(sem, 0) < val:
                need[sem] = val
        for sem, val in sorted(need.items(), key=lambda kv: str(kv[0].name)):
            if e["waited"].get(sem, 0) < val:
                e["waited"][sem] = val
                e["script"].append(
                    lambda eng, s=sem, v=val: eng.wait_ge(s, v))

    def emit(self, en, fn, reads=(), writes=(), dma=False, fns=None,
             dkey="main"):
        e = self.eng[en]
        deps = []
        for t in reads:
            if t.w:
                deps.append(t.w)
        for t in writes:
            if t.w:
                deps.append(t.w)
            deps.extend(t.rs)
        self._waits(e, deps, skip_self=(en == "pe"))
        if dma:
            ds = e["dsems"][dkey]
            ds[1] += 16
            sem, cnt = ds[0], ds[1]
            amt = 16
        else:
            e["cnt"] += 1
            sem, cnt = e["sem"], e["cnt"]
            amt = 1
        if fns is not None:
            for f in fns[:-1]:
                e["script"].append(lambda eng, f=f: f(eng))
            e["script"].append(
                lambda eng, f=fns[-1], s=sem, a=amt: f(eng).then_inc(s, a))
        else:
            e["script"].append(
                lambda eng, f=fn, s=sem, a=amt: f(eng).then_inc(s, a))
        tok = (sem, cnt)
        for t in reads:
            t.rs.append(tok)
        for t in writes:
            t.w = tok
            t.rs = []
        return tok

    def raw(self, en, fn):
        self.eng[en]["script"].append(fn)

    def barrier(self, level=1):
        import kernel as _k
        if _PARANOID < level:
            return
        sems = []
        for name in self.ENGS:
            e = self.eng[name]
            if e["cnt"]:
                sems.append((e["sem"], e["cnt"]))
            for key, sc in e["dsems"].items():
                if sc[1]:
                    sems.append((sc[0], sc[1]))
        for name in self.ENGS:
            e = self.eng[name]
            self._waits(e, [t for t in sems if t[0] is not e["sem"]])

    def run_blocks(self, block):
        block.tensor(lambda eng: [f(eng) for f in self.eng["pe"]["script"]])
        block.vector(lambda eng: [f(eng) for f in self.eng["dve"]["script"]])
        block.scalar(lambda eng: [f(eng) for f in self.eng["act"]["script"]])
        block.gpsimd(lambda eng: [f(eng) for f in self.eng["pool"]["script"]])
        block.sync(lambda eng: [f(eng) for f in self.eng["sync"]["script"]])


def build_program():
    nc = bass.Bass("TRN2", target_bir_lowering=False)
    ll = _log_lam()
    lam_C = np.exp(ll * C)

    d_xt = nc.dram_tensor("xt", [EMBED, TPC], F32, kind="ExternalInput")
    d_wq = nc.dram_tensor("wq", [16, 128, EMBED], F32, kind="ExternalInput")
    d_wk = nc.dram_tensor("wk", [16, 128, EMBED], F32, kind="ExternalInput")
    d_wo = nc.dram_tensor("wo", [16, 128, EMBED], F32, kind="ExternalInput")
    d_wv = nc.dram_tensor("wv", [4, EMBED, 512], F32, kind="ExternalInput")
    d_wg1 = nc.dram_tensor("wg1", [128, EMBED], F32, kind="ExternalInput")
    d_wg2 = nc.dram_tensor("wg2", [128, EMBED], F32, kind="ExternalInput")
    d_rwt = nc.dram_tensor("rwt", [KC, 128, NE], F32, kind="ExternalInput")
    d_biasrep = nc.dram_tensor("biasrep", [128, NE], F32, kind="ExternalInput")
    d_nwcol = nc.dram_tensor("nwcol", [128, HEADS], F32, kind="ExternalInput")
    d_wue = nc.dram_tensor("wue", [NEX, EKC, 128, EMBED], F32, kind="ExternalInput")
    d_wge = nc.dram_tensor("wge", [NEX, EKC, 128, EMBED], F32, kind="ExternalInput")
    d_wdp = nc.dram_tensor("wdp", [NPAIR, 16, 2, 128, EKC * 128], F32,
                           kind="ExternalInput")
    d_cosb = nc.dram_tensor("cosb", [128, TPC], BF16, kind="ExternalInput")
    d_sinb = nc.dram_tensor("sinb", [128, TPC], BF16, kind="ExternalInput")
    d_dt = nc.dram_tensor("dt_c", [128, HEADS * C], BF16, kind="ExternalInput")
    d_dqr = nc.dram_tensor("dqr", [128, HEADS * C], BF16, kind="ExternalInput")
    d_dkcol = nc.dram_tensor("dkcol", [128, HEADS], F32, kind="ExternalInput")
    d_lamcol = nc.dram_tensor("lamcol", [128, HEADS], F32, kind="ExternalInput")
    d_lamj = nc.dram_tensor("lamj", [128, HEADS * NCH], F32, kind="ExternalInput")
    d_identb = nc.dram_tensor("identb", [128, 128], BF16, kind="ExternalInput")
    d_permb = nc.dram_tensor("permb", [128, 128], BF16, kind="ExternalInput")
    d_ohb = nc.dram_tensor("ohb", [128, HEADS * 16], BF16, kind="ExternalInput")
    d_rsel = nc.dram_tensor("rsel", [16, HEADS * 128], BF16, kind="ExternalInput")
    d_onesf = nc.dram_tensor("onesf", [128, 1], F32, kind="ExternalInput")
    d_onesrow = nc.dram_tensor("onesrow", [1, 128], F32, kind="ExternalInput")
    d_cvec = nc.dram_tensor("cvec", [128, 8], F32, kind="ExternalInput")
    d_wcol3 = nc.dram_tensor("wcol3", [128, 3 * HEADS], F32, kind="ExternalInput")
    d_poff = nc.dram_tensor("poff", [1, 4], I32, kind="ExternalInput")
    d_out = nc.dram_tensor("out_t", [EMBED, TPC], F32, kind="ExternalOutput")
    d_ccin = nc.dram_tensor("cc_in", [EMBED, 128], F32)
    d_ccout = nc.dram_tensor("cc_out", [NCORES * EMBED, 128], F32)

    b = B(nc)

    def tap(name, ap_fn, reads, shape, dtype):
        if name not in _DEBUG_TAPS:
            return
        d = nc.dram_tensor("tap_" + name, shape, dtype, kind="ExternalOutput")
        b.emit("sync", lambda g, d=d, a=ap_fn(): g.dma_start(out=d.ap(), in_=a),
               reads=list(reads), dma=True, dkey="out")
    es = ExitStack()
    with es:
        b.init_sems(es)
        sb = lambda name, shape, dt, **kw: es.enter_context(
            nc.sbuf_tensor(name, shape, dt, **kw))
        t_cvec = sb("cvec_s", [128, 8], F32)
        t_identb = sb("identb_s", [128, 128], BF16)
        t_permb = sb("permb_s", [128, 128], BF16)
        t_ohb = sb("ohb_s", [128, HEADS * 16], BF16)
        t_rsel = sb("rsel_s", [16, HEADS * 128], BF16)
        t_onesf = sb("onesf_s", [128, 1], F32)
        t_onesrow = sb("onesrow_s", [1, 128], F32)
        t_biasrep = sb("biasrep_s", [128, NE], F32)
        t_nwcol = sb("nwcol_s", [128, HEADS], F32)
        t_dkcol = sb("dkcol_s", [128, HEADS], F32)
        t_lamcol = sb("lamcol_s", [128, HEADS], F32)
        t_lamj = sb("lamj_s", [128, HEADS * NCH], F32)
        t_wcol3 = sb("wcol3_s", [128, 3 * HEADS], F32)
        t_poff = sb("poff_s", [1, 4], I32)
        t_rwt = sb("rwt_s", [128, KC * NE], F32)
        t_g1 = sb("g1_s", [128, TPC], BF16)
        t_stg = [sb(f"stg{i}", [128, 4096], BF16) for i in range(2)]
        ps = [es.enter_context(nc.psum_tensor(f"ps{i}", [128, 512], F32))
              for i in range(6)]
        psb2 = [es.enter_context(nc.psum_tensor(f"psb{i}", [128, 1024], BF16))
                for i in range(2)]
        psb = psb2[0]

        T = lambda: Trk()
        k_cvec, k_consts = T(), T()
        k_rwt, k_g1 = T(), T()
        k_stg = [T(), T()]
        k_ps = [T() for _ in range(6)]
        k_psb2 = [T(), T()]
        k_psb = k_psb2[0]

        def ld(dst_ap, src_ap, trk, queue="sync", dkey="main"):
            b.emit(queue, lambda g, d=dst_ap, s=src_ap: g.dma_start(out=d, in_=s),
                   writes=[trk], dma=True, dkey=dkey)

        ld(t_cvec[:, :], d_cvec.ap(), k_cvec)
        for dst, src in ((t_identb, d_identb), (t_permb, d_permb),
                         (t_ohb, d_ohb), (t_rsel, d_rsel), (t_onesf, d_onesf),
                         (t_onesrow, d_onesrow), (t_biasrep, d_biasrep),
                         (t_nwcol, d_nwcol), (t_dkcol, d_dkcol),
                         (t_lamcol, d_lamcol), (t_lamj, d_lamj),
                         (t_wcol3, d_wcol3), (t_poff, d_poff)):
            ld(dst[:, :], src.ap(), k_consts)
        ld(t_rwt.ap().rearrange("p (kc n) -> p kc n", kc=KC),
           d_rwt.ap().rearrange("kc p n -> p kc n"), k_rwt, dkey="rwt")

        b.barrier()
        esA = ExitStack()
        t_obuf = esA.enter_context(nc.sbuf_tensor("obuf", [128, HEADS * TPC], BF16))
        t_qbuf = esA.enter_context(nc.sbuf_tensor("qbuf", [128, HEADS * TPC], BF16))
        t_S = esA.enter_context(nc.sbuf_tensor("S_f", [128, HEADS * HD], F32))
        t_Sb = esA.enter_context(nc.sbuf_tensor("S_b", [128, HEADS * HD], BF16))
        k_obuf = [T() for _ in range(HEADS)]
        k_qbuf = [T() for _ in range(HEADS)]
        k_S = [T() for _ in range(HEADS)]
        k_Sb = [T() for _ in range(HEADS)]
        k_dqr, k_dt = T(), T()

        esA2 = ExitStack()
        t_kfm = esA2.enter_context(nc.sbuf_tensor("kfm", [128, HEADS * TPC], BF16))
        t_scb = esA2.enter_context(nc.sbuf_tensor("scb", [128, C], BF16))
        t_ktb = esA2.enter_context(nc.sbuf_tensor("ktb", [128, C], BF16))
        esX = ExitStack()
        t_xbf = esX.enter_context(nc.sbuf_tensor("xbf", [128, KC * TPC], BF16,
                                                 side="right"))
        k_xbf = T()
        k_kfm = [T() for _ in range(HEADS)]
        k_vtm = [T() for _ in range(NCH * 4)]
        k_scb, k_ktb = T(), T()

        ld(t_xbf.ap().rearrange("p (kc t) -> p kc t", kc=KC),
           d_xt.ap().rearrange("(kc p) t -> p kc t", p=128), k_xbf, queue="pool",
           dkey="xbf")

        esR = ExitStack()
        t_cos = esR.enter_context(nc.sbuf_tensor("cosb_s", [128, TPC], BF16, side="right"))
        t_sin = esR.enter_context(nc.sbuf_tensor("sinb_s", [128, TPC], BF16, side="right"))
        t_qsil = esR.enter_context(nc.sbuf_tensor("qsil", [128, TPC], BF16, side="right"))
        t_rt1 = esR.enter_context(nc.sbuf_tensor("rt1", [128, TPC], BF16, side="right"))
        t_rt2 = esR.enter_context(nc.sbuf_tensor("rt2", [128, TPC], BF16, side="right"))
        k_cos, k_sin, k_qsil, k_rt1, k_rt2 = T(), T(), T(), T(), T()
        ld(t_cos[:, :], d_cosb.ap(), k_cos, dkey="cos")
        ld(t_sin[:, :], d_sinb.ap(), k_sin, dkey="sin")

        def mm_sweep(bank, lhs_fn, rhs_fn, n_k, reads):
            fns = []
            for kci in range(n_k):
                fns.append(lambda t, l=lhs_fn(kci), r=rhs_fn(kci),
                           st=(kci == 0), sp=(kci == n_k - 1), bk=bank:
                           t.matmul(ps[bk][:, :], l, r, start=st, stop=sp))
            return b.emit("pe", None, fns=fns, reads=list(reads),
                          writes=[k_ps[bank]])

        def qk_phase(dram_w, dest, k_dest):
            for m in range(HEADS):
                sg = m % 2
                b.emit("pool", lambda g, m=m, sg=sg: g.dma_start(
                    out=t_stg[sg][:, :EMBED], in_=dram_w.ap()[m, :, :]),
                    writes=[k_stg[sg]], dma=True, dkey=f"stg{sg}")
                for half in range(2):
                    bank = half
                    mm_sweep(bank,
                             lambda kci, sg=sg: t_stg[sg][:, kci * 128:(kci + 1) * 128],
                             lambda kci, half=half: t_xbf[:, kci * TPC + half * 512:
                                                          kci * TPC + half * 512 + 512],
                             KC, (k_stg[sg], k_xbf))
                    b.emit("act", lambda s, bank=bank, half=half: s.activation(
                        t_qsil[:, half * 512:half * 512 + 512], ps[bank][:, :],
                        AF.Silu), reads=[k_ps[bank]], writes=[k_qsil])
                    b.emit("pe", lambda t, half=half: t.matmul(
                        ps[2 + half][:, :], t_permb[:, :],
                        t_qsil[:, half * 512:half * 512 + 512],
                        start=True, stop=True),
                        reads=[k_qsil, k_consts], writes=[k_ps[2 + half]])
                b.emit("dve", lambda v, m=m: v.tensor_tensor(
                    t_rt1[:, :], t_qsil[:, :], t_cos[:, :], ALU.mult),
                    reads=[k_qsil, k_cos], writes=[k_rt1])
                for half in range(2):
                    b.emit("dve", lambda v, half=half: v.tensor_tensor(
                        t_rt2[:, half * 512:half * 512 + 512],
                        ps[2 + half][:, :],
                        t_sin[:, half * 512:half * 512 + 512], ALU.mult),
                        reads=[k_ps[2 + half], k_sin], writes=[k_rt2])
                b.emit("dve", lambda v, m=m: v.tensor_tensor(
                    dest[:, m * TPC:(m + 1) * TPC], t_rt1[:, :], t_rt2[:, :],
                    ALU.add), reads=[k_rt1, k_rt2], writes=[k_dest[m]])

        qk_phase(d_wq, t_qbuf, k_qbuf)
        b.barrier()
        qk_phase(d_wk, t_kfm, k_kfm)
        b.barrier()
        esR.close()
        tap("qk_q", lambda: t_qbuf[:, :], k_qbuf, [128, HEADS * TPC], BF16)
        tap("qk_k", lambda: t_kfm[:, :], k_kfm, [128, HEADS * TPC], BF16)

        b.emit("pool", lambda g: g.dma_start(
            out=t_stg[0][:, :EMBED], in_=d_wg1.ap()), writes=[k_stg[0]],
            dma=True, dkey="stg0")
        for half in range(2):
            mm_sweep(half,
                     lambda kci: t_stg[0][:, kci * 128:(kci + 1) * 128],
                     lambda kci, half=half: t_xbf[:, kci * TPC + half * 512:
                                                  kci * TPC + half * 512 + 512],
                     KC, (k_stg[0], k_xbf))
            b.emit("act", lambda s, half=half: s.copy(
                t_g1[:, half * 512:half * 512 + 512], ps[half][:, :]),
                reads=[k_ps[half]], writes=[k_g1])

        tap("g1", lambda: t_g1[:, :], [k_g1], [128, TPC], BF16)
        esV = ExitStack()
        t_vtm = esV.enter_context(nc.sbuf_tensor("vtm", [128, NCH * HEADS * HD],
                                                 BF16))
        for nf in range(4):
            for kh in range(2):
                b.emit("pool", lambda g, nf=nf, kh=kh: g.dma_start(
                    out=t_stg[kh].ap().rearrange("p (kc f) -> p kc f", kc=8),
                    in_=d_wv.ap()[nf, kh * 1024:(kh + 1) * 1024, :]
                    .rearrange("(kc p) f -> p kc f", p=128)),
                    writes=[k_stg[kh]], dma=True, dkey=f"stg{kh}")
            for tc in range(NCH):
                bank = tc % 2
                fns = []
                for kci in range(KC):
                    kh, kloc = kci // 8, kci % 8
                    fns.append(lambda t, kci=kci, kh=kh, kloc=kloc, tc=tc,
                               bank=bank: t.matmul(
                                   ps[bank][:, :],
                                   t_xbf[:, kci * TPC + tc * 128:
                                         kci * TPC + tc * 128 + 128],
                                   t_stg[kh][:, kloc * 512:(kloc + 1) * 512],
                                   start=(kci == 0), stop=(kci == KC - 1)))
                b.emit("pe", None, fns=fns,
                       reads=[k_stg[0], k_stg[1], k_xbf], writes=[k_ps[bank]])
                b.emit("act", lambda s, tc=tc, nf=nf, bank=bank: s.copy(
                    t_vtm[:, (tc * HEADS + nf * 4) * HD:
                          (tc * HEADS + nf * 4) * HD + 512], ps[bank][:, :]),
                    reads=[k_ps[bank]], writes=[k_vtm[tc * 4 + nf]])
        esX.close()
        t_dqr = esV.enter_context(nc.sbuf_tensor("dqr_s", [128, HEADS * C], BF16))
        t_dt = esV.enter_context(nc.sbuf_tensor("dt_s", [128, HEADS * C], BF16))
        ld(t_dqr[:, :], d_dqr.ap(), k_dqr, dkey="dqr")
        ld(t_dt[:, :], d_dt.ap(), k_dt, dkey="dtm")

        b.barrier()
        tap("v", lambda: t_vtm[:, :], k_vtm, [128, NCH * HEADS * HD], BF16)
        t_scg = [esV.enter_context(nc.sbuf_tensor(f"scg{i}", [128, 8 * C], BF16))
                 for i in range(2)]
        t_ktg = [esV.enter_context(nc.sbuf_tensor(f"ktg{i}", [128, 8 * C], BF16))
                 for i in range(2)]
        k_scg = [T(), T()]
        k_ktg = [T(), T()]

        def qsl(h, j):
            return t_qbuf[:, h * TPC + j * C: h * TPC + (j + 1) * C]

        def ksl(h, j):
            return t_kfm[:, h * TPC + j * C: h * TPC + (j + 1) * C]

        def vsl(h, j):
            return t_vtm[:, (j * HEADS + h) * HD:(j * HEADS + h + 1) * HD]
        for j in range(NCH):
            b.barrier(2)
            for hg in range(2):
                hs = [hg * 8 + i for i in range(8)]
                par = 0
                pkt, kkt = psb2[par], k_psb2[par]
                kvs = [k_vtm[j * 4 + h // 4] for h in hs]
                b.emit("pe", None, fns=[
                    lambda t, h=h, i=i, j=j, pkt=pkt: t.transpose(
                        pkt[:, i * C:(i + 1) * C], ksl(h, j), t_identb[:, :])
                    for i, h in enumerate(hs)],
                    reads=[k_kfm[h] for h in hs] + [k_consts], writes=[kkt])
                b.emit("act", None, fns=[
                    lambda s, h=h, i=i, pkt=pkt, par=par: s.activation(
                        t_ktg[par][:, i * C:(i + 1) * C], pkt[:, i * C:(i + 1) * C],
                        AF.Copy, scale=t_dkcol[:, h:h + 1])
                    for i, h in enumerate(hs)],
                    reads=[kkt, k_consts], writes=[k_ktg[par]])
                b.emit("pe", None, fns=[
                    lambda t, h=h, i=i, j=j: t.matmul(
                        ps[i // 4][:, (i % 4) * C:(i % 4) * C + C],
                        ksl(h, j), qsl(h, j), start=True, stop=True,
                        skip_group_check=True)
                    for i, h in enumerate(hs)],
                    reads=[k_kfm[h] for h in hs] + [k_qbuf[h] for h in hs],
                    writes=[k_ps[0], k_ps[1]])
                b.emit("dve", None, fns=[
                    lambda v, h=h, i=i, par=par: v.tensor_tensor(
                        t_scg[par][:, i * C:(i + 1) * C],
                        ps[i // 4][:, (i % 4) * C:(i % 4) * C + C],
                        t_dt[:, h * C:(h + 1) * C], ALU.mult)
                    for i, h in enumerate(hs)],
                    reads=[k_ps[0], k_ps[1], k_dt], writes=[k_scg[par]])
                b.emit("dve", None, fns=[
                    lambda v, h=h, j=j: v.tensor_tensor(
                        qsl(h, j), qsl(h, j), t_dqr[:, h * C:(h + 1) * C], ALU.mult)
                    for h in hs],
                    reads=[k_dqr], writes=[k_qbuf[h] for h in hs])
                ofns = []
                for i, h in enumerate(hs):
                    osl = lambda i=i: ps[2 + i // 4][:, (i % 4) * C:(i % 4) * C + C]
                    if j == 0:
                        ofns.append(lambda t, h=h, i=i, j=j, osl=osl, par=par: t.matmul(
                            osl(), vsl(h, j), t_scg[par][:, i * C:(i + 1) * C],
                            start=True, stop=True, skip_group_check=True))
                    else:
                        ofns.append(lambda t, h=h, i=i, j=j, osl=osl, par=par: t.matmul(
                            osl(), vsl(h, j), t_scg[par][:, i * C:(i + 1) * C],
                            start=True, stop=False, skip_group_check=True))
                        ofns.append(lambda t, h=h, i=i, j=j, osl=osl: t.matmul(
                            osl(), t_Sb[:, h * HD:(h + 1) * HD], qsl(h, j),
                            start=False, stop=True, skip_group_check=True))
                b.emit("pe", None, fns=ofns,
                    reads=kvs + [k_scg[par]] + [k_Sb[h] for h in hs]
                    + [k_qbuf[h] for h in hs],
                    writes=[k_ps[2], k_ps[3]])
                b.emit("act", None, fns=[
                    lambda s, h=h, i=i, j=j: s.copy(
                        t_obuf[:, h * TPC + j * C: h * TPC + (j + 1) * C],
                        ps[2 + i // 4][:, (i % 4) * C:(i % 4) * C + C])
                    for i, h in enumerate(hs)],
                    reads=[k_ps[2], k_ps[3]], writes=[k_obuf[h] for h in hs])
                b.emit("pe", None, fns=[
                    lambda t, h=h, i=i, j=j, par=par: t.matmul(
                        ps[4 + i // 4][:, (i % 4) * C:(i % 4) * C + C],
                        t_ktg[par][:, i * C:(i + 1) * C], vsl(h, j),
                        start=True, stop=True, skip_group_check=True)
                    for i, h in enumerate(hs)],
                    reads=[k_ktg[par]] + kvs, writes=[k_ps[4], k_ps[5]])
                sfns = []
                for i, h in enumerate(hs):
                    asl = lambda i=i: ps[4 + i // 4][:, (i % 4) * C:(i % 4) * C + C]
                    if j == 0:
                        sfns.append(lambda v, h=h, asl=asl: v.tensor_copy(
                            t_S[:, h * HD:(h + 1) * HD], asl()))
                    else:
                        sfns.append(lambda v, h=h, asl=asl: v.scalar_tensor_tensor(
                            t_S[:, h * HD:(h + 1) * HD],
                            t_S[:, h * HD:(h + 1) * HD],
                            t_lamcol[:, h:h + 1], asl(), ALU.mult, ALU.add))
                b.emit("dve", None, fns=sfns,
                       reads=[k_ps[4], k_ps[5], k_consts],
                       writes=[k_S[h] for h in hs])
                if j < NCH - 1:
                    b.emit("act", None, fns=[
                        lambda s, h=h: s.copy(
                            t_Sb[:, h * HD:(h + 1) * HD],
                            t_S[:, h * HD:(h + 1) * HD])
                        for h in hs],
                        reads=[k_S[h] for h in hs],
                        writes=[k_Sb[h] for h in hs])
        tap("o_scan", lambda: t_obuf[:, :], k_obuf, [128, HEADS * TPC], BF16)
        tap("s_loc", lambda: t_S[:, :], k_S, [128, HEADS * HD], F32)
        esV.close()
        esA2.close()

        b.barrier()
        esC = ExitStack()
        t_G = [esC.enter_context(nc.sbuf_tensor(f"G{i}", [128, HEADS * HD], F32))
               for i in range(3)]
        t_tmpA = esC.enter_context(nc.sbuf_tensor("tmpA", [128, HD], F32))
        t_tmpB = esC.enter_context(nc.sbuf_tensor("tmpB", [128, HD], F32))
        t_Sin = esC.enter_context(nc.sbuf_tensor("Sin", [128, HEADS * HD], BF16))
        t_qt2 = [esC.enter_context(nc.sbuf_tensor(f"qt{i}", [128, C], BF16))
                 for i in range(2)]
        t_gsa = esC.enter_context(nc.sbuf_tensor("gsa", [128, HEADS * TPC], BF16))
        t_sq = esC.enter_context(nc.sbuf_tensor("sqg", [128, TPC], BF16))
        t_rstd = esC.enter_context(nc.sbuf_tensor("rstdg", [16, TPC], F32))
        t_rstdb = esC.enter_context(nc.sbuf_tensor("rstdgb", [16, TPC], BF16))
        k_G = [T() for _ in range(3)]
        k_tmpA, k_tmpB, k_Sin = T(), T(), T()
        k_qt2 = [T(), T()]
        k_gsa, k_sq, k_rstd, k_rstdb = T(), T(), T(), T()
        k_ccin, k_ccout = T(), T()

        b.emit("sync", lambda g: g.dma_start(
            out=d_ccin.ap().rearrange("(h p) e -> p h e", p=128),
            in_=t_S.ap().rearrange("p (h e) -> p h e", h=HEADS)),
            reads=list(k_S), writes=[k_ccin], dma=True, dkey="ccin")

        e = b.eng["pool"]
        b._waits(e, [k_ccin.w])
        cc_sem = b.cc_sem
        e["script"].append(lambda g, s=cc_sem: g.collective_compute(
            "AllGather", ALU.bypass, replica_groups=[list(range(NCORES))],
            ins=[d_ccin.ap().opt()], outs=[d_ccout.ap().opt()]).then_inc(s, 1))
        k_ccout.w = (cc_sem, 1)

        b.emit("pool", lambda g: g.dma_start(
            out=t_stg[0][:, :EMBED], in_=d_wg2.ap()),
            writes=[k_stg[0]], dma=True, dkey="stg0")
        k_wg2v = k_stg[0]
        for h in range(HEADS):
            for half in range(2):
                bank = (h * 2 + half) % 4
                b.emit("pe", lambda t, h=h, half=half, bank=bank: t.matmul(
                    ps[bank][:, :], t_stg[0][:, h * 128:(h + 1) * 128],
                    t_g1[:, half * 512:half * 512 + 512], start=True, stop=True),
                    reads=[k_wg2v, k_g1], writes=[k_ps[bank]])
                b.emit("act", lambda s, h=h, half=half, bank=bank: s.activation(
                    t_gsa[:, h * TPC + half * 512: h * TPC + half * 512 + 512],
                    ps[bank][:, :], AF.Sigmoid),
                    reads=[k_ps[bank]], writes=[k_gsa])

        dsem = e["dsems"]["g"][0]

        def fetch_G(g, s):
            with g.register("gofs") as gofs:
                for i in range(3):
                    g.reg_load(gofs, t_poff[:1, i:i + 1])
                    g.dma_start(
                        out=t_G[i].ap().rearrange("p (h e) -> p h e", h=HEADS),
                        in_=bass.AP(d_ccout, gofs,
                                    [[128, 128], [128 * 128, HEADS], [1, 128]]),
                    ).then_inc(s, 16)
        b._waits(e, [k_ccout.w, k_consts.w])
        e["dsems"]["g"][1] += 48
        g_tok = (dsem, e["dsems"]["g"][1])
        e["script"].append(lambda g, s=dsem: fetch_G(g, s))
        for kk in k_G:
            kk.w = g_tok

        for h in range(HEADS):
            b.emit("dve", lambda v, h=h: v.tensor_scalar(
                t_tmpA[:, :], t_G[0][:, h * HD:(h + 1) * HD],
                t_wcol3[:, h:h + 1], None, ALU.mult),
                reads=[k_G[0], k_consts], writes=[k_tmpA])
            b.emit("dve", lambda v, h=h: v.scalar_tensor_tensor(
                t_tmpB[:, :], t_G[1][:, h * HD:(h + 1) * HD],
                t_wcol3[:, HEADS + h:HEADS + h + 1], t_tmpA[:, :],
                ALU.mult, ALU.add),
                reads=[k_G[1], k_consts, k_tmpA], writes=[k_tmpB])
            b.emit("dve", lambda v, h=h: v.scalar_tensor_tensor(
                t_tmpA[:, :], t_G[2][:, h * HD:(h + 1) * HD],
                t_wcol3[:, 2 * HEADS + h:2 * HEADS + h + 1], t_tmpB[:, :],
                ALU.mult, ALU.add),
                reads=[k_G[2], k_consts, k_tmpB], writes=[k_tmpA])
            b.emit("act", lambda s, h=h: s.copy(
                t_Sin[:, h * HD:(h + 1) * HD], t_tmpA[:, :]),
                reads=[k_tmpA], writes=[k_Sin])

        lj_np = lam_C[:, None] ** np.arange(NCH)[None, :]
        pc = 0
        for h in range(HEADS):
            for j in range(NCH):
                if lj_np[h, j] < 1e-30:
                    continue
                par = pc % 2
                pc += 1
                b.emit("dve", lambda v, h=h, j=j, par=par: v.tensor_scalar(
                    t_qt2[par][:, :],
                    t_qbuf[:, h * TPC + j * C: h * TPC + (j + 1) * C],
                    t_lamj[:, h * NCH + j: h * NCH + j + 1], None, ALU.mult),
                    reads=[k_qbuf[h], k_consts], writes=[k_qt2[par]])
                b.emit("pe", lambda t, h=h, par=par: t.matmul(
                    ps[par][:, 0:C], t_Sin[:, h * HD:(h + 1) * HD],
                    t_qt2[par][:, :], start=True, stop=True),
                    reads=[k_Sin, k_qt2[par]], writes=[k_ps[par]])
                b.emit("dve", lambda v, h=h, j=j, par=par: v.tensor_tensor(
                    t_obuf[:, h * TPC + j * C: h * TPC + (j + 1) * C],
                    t_obuf[:, h * TPC + j * C: h * TPC + (j + 1) * C],
                    ps[par][:, 0:C], ALU.add),
                    reads=[k_ps[par]], writes=[k_obuf[h]])
        tap("o_corr", lambda: t_obuf[:, :], k_obuf, [128, HEADS * TPC], BF16)
        tap("sin_t", lambda: t_Sin[:, :], [k_Sin], [128, HEADS * HD], BF16)
        for h in range(HEADS):
            b.emit("dve", lambda v, h=h: v.tensor_tensor(
                t_obuf[:, h * TPC:(h + 1) * TPC],
                t_obuf[:, h * TPC:(h + 1) * TPC],
                t_gsa[:, h * TPC:(h + 1) * TPC], ALU.mult),
                reads=[k_gsa], writes=[k_obuf[h]])
            b.emit("act", lambda s, h=h: s.activation(
                t_sq[:, :], t_obuf[:, h * TPC:(h + 1) * TPC], AF.Square),
                reads=[k_obuf[h]], writes=[k_sq])
            for half in range(2):
                b.emit("pe", lambda t, h=h, half=half: t.matmul(
                    ps[2 + half][0:16, :], t_ohb[:, h * 16:(h + 1) * 16],
                    t_sq[:, half * 512:half * 512 + 512],
                    start=(h == 0), stop=(h == HEADS - 1)),
                    reads=[k_sq, k_consts], writes=[k_ps[2 + half]])
        tap("xg", lambda: t_obuf[:, :], k_obuf, [128, HEADS * TPC], BF16)
        for half in range(2):
            b.emit("act", lambda s, half=half: s.activation(
                t_rstd[:, half * 512:half * 512 + 512], ps[2 + half][0:16, :],
                AF.Sqrt, bias=t_cvec[0:16, 1:2], scale=t_cvec[0:16, 0:1]),
                reads=[k_ps[2 + half], k_cvec], writes=[k_rstd])
        b.emit("dve", lambda v: v.reciprocal(t_rstd[:, :], t_rstd[:, :]),
               reads=[], writes=[k_rstd])
        b.emit("act", lambda s: s.copy(t_rstdb[:, :], t_rstd[:, :]),
               reads=[k_rstd], writes=[k_rstdb])
        tap("rstd", lambda: t_rstd[:, :], [k_rstd], [16, TPC], F32)
        for h in range(HEADS):
            for half in range(2):
                bank = half
                b.emit("pe", lambda t, h=h, half=half, bank=bank: t.matmul(
                    ps[bank][:, :], t_rsel[:, h * 128:(h + 1) * 128],
                    t_rstdb[:, half * 512:half * 512 + 512],
                    start=True, stop=True),
                    reads=[k_rstdb, k_consts], writes=[k_ps[bank]])
                b.emit("dve", lambda v, h=h, half=half, bank=bank:
                       v.scalar_tensor_tensor(
                           t_obuf[:, h * TPC + half * 512: h * TPC + half * 512 + 512],
                           t_obuf[:, h * TPC + half * 512: h * TPC + half * 512 + 512],
                           t_nwcol[:, h:h + 1], ps[bank][:, :],
                           ALU.mult, ALU.mult),
                       reads=[k_ps[bank], k_consts], writes=[k_obuf[h]])
        tap("o_n", lambda: t_obuf[:, :], k_obuf, [128, HEADS * TPC], BF16)
        esC.close()
        b.barrier()

        b.barrier()
        t_f = sb("f_acc", [128, KC * TPC], F32, side="right")
        k_f = T()
        b.emit("sync", lambda g: g.dma_start(
            out=t_f.ap().rearrange("p (kc t) -> p kc t", kc=KC),
            in_=d_xt.ap().rearrange("(kc p) t -> p kc t", p=128)),
            writes=[k_f], dma=True, dkey="xf")
        for m in range(KC):
            sg = m % 2
            b.emit("pool", lambda g, m=m, sg=sg: g.dma_start(
                out=t_stg[sg][:, :EMBED], in_=d_wo.ap()[m, :, :]),
                writes=[k_stg[sg]], dma=True, dkey=f"stg{sg}")
            for half in range(2):
                bank = half
                mm_sweep(bank,
                         lambda kci, sg=sg: t_stg[sg][:, kci * 128:(kci + 1) * 128],
                         lambda kci, half=half: t_obuf[:, kci * TPC + half * 512:
                                                       kci * TPC + half * 512 + 512],
                         KC, tuple([k_stg[sg]] + k_obuf))
                b.emit("dve", lambda v, m=m, half=half, bank=bank: v.tensor_tensor(
                    t_f[:, m * TPC + half * 512: m * TPC + half * 512 + 512],
                    t_f[:, m * TPC + half * 512: m * TPC + half * 512 + 512],
                    ps[bank][:, :], ALU.add),
                    reads=[k_ps[bank]], writes=[k_f])
        tap("r1", lambda: t_f[:, :], [k_f], [128, KC * TPC], F32)
        esA.close()

        def layer_norm(t_x, k_x, sq_scope_name):
            esL = ExitStack()
            t_sqf = esL.enter_context(nc.sbuf_tensor(
                f"sqf_{sq_scope_name}", [128, TPC], F32))
            t_st = esL.enter_context(nc.sbuf_tensor(
                f"st_{sq_scope_name}", [1, 4 * TPC], F32))
            k_sqf, k_st = T(), T()
            for kci in range(KC):
                b.emit("act", lambda s, kci=kci: s.activation(
                    t_sqf[:, :], t_x[:, kci * TPC:(kci + 1) * TPC], AF.Square),
                    reads=[k_x], writes=[k_sqf])
                for half in range(2):
                    b.emit("pe", lambda t, kci=kci, half=half: t.matmul(
                        ps[4 + half][0:1, :], t_onesf[:, :],
                        t_x[:, kci * TPC + half * 512: kci * TPC + half * 512 + 512],
                        start=(kci == 0), stop=(kci == KC - 1)),
                        reads=[k_x, k_consts], writes=[k_ps[4 + half]])
                    b.emit("pe", lambda t, kci=kci, half=half: t.matmul(
                        ps[2 + half][0:1, :], t_onesf[:, :],
                        t_sqf[:, half * 512:half * 512 + 512],
                        start=(kci == 0), stop=(kci == KC - 1)),
                        reads=[k_sqf, k_consts], writes=[k_ps[2 + half]])
            for half in range(2):
                b.emit("dve", lambda v, half=half: v.tensor_scalar(
                    t_st[0:1, half * 512:half * 512 + 512],
                    ps[4 + half][0:1, :], t_cvec[0:1, 2:3], None, ALU.mult),
                    reads=[k_ps[4 + half], k_cvec], writes=[k_st])
                b.emit("dve", lambda v, half=half: v.tensor_scalar(
                    t_st[0:1, TPC + half * 512: TPC + half * 512 + 512],
                    ps[2 + half][0:1, :], t_cvec[0:1, 2:3], None, ALU.mult),
                    reads=[k_ps[2 + half], k_cvec], writes=[k_st])
            b.emit("dve", lambda v: v.tensor_tensor(
                t_st[0:1, 2 * TPC:3 * TPC], t_st[0:1, 0:TPC], t_st[0:1, 0:TPC],
                ALU.mult), reads=[], writes=[k_st])
            b.emit("dve", lambda v: v.tensor_tensor(
                t_st[0:1, 2 * TPC:3 * TPC], t_st[0:1, TPC:2 * TPC],
                t_st[0:1, 2 * TPC:3 * TPC], ALU.subtract),
                reads=[], writes=[k_st])
            b.emit("act", lambda s: s.activation(
                t_st[0:1, 2 * TPC:3 * TPC], t_st[0:1, 2 * TPC:3 * TPC],
                AF.Sqrt, bias=t_cvec[0:1, 3:4]),
                reads=[k_cvec], writes=[k_st])
            b.emit("dve", lambda v: v.reciprocal(
                t_st[0:1, 2 * TPC:3 * TPC], t_st[0:1, 2 * TPC:3 * TPC]),
                reads=[], writes=[k_st])
            b.emit("dve", lambda v: v.tensor_tensor(
                t_st[0:1, 3 * TPC:4 * TPC], t_st[0:1, 0:TPC],
                t_st[0:1, 2 * TPC:3 * TPC], ALU.mult),
                reads=[], writes=[k_st])
            for half in range(2):
                b.emit("pe", lambda t, half=half: t.matmul(
                    ps[4 + half][:, :], t_onesrow[:, :],
                    t_st[0:1, 2 * TPC + half * 512: 2 * TPC + half * 512 + 512],
                    start=True, stop=True),
                    reads=[k_st, k_consts], writes=[k_ps[4 + half]])
                b.emit("pe", lambda t, half=half: t.matmul(
                    ps[2 + half][:, :], t_onesrow[:, :],
                    t_st[0:1, 3 * TPC + half * 512: 3 * TPC + half * 512 + 512],
                    start=True, stop=True),
                    reads=[k_st, k_consts], writes=[k_ps[2 + half]])
            for kci in range(KC):
                for half in range(2):
                    sl = lambda kci=kci, half=half: t_x[
                        :, kci * TPC + half * 512: kci * TPC + half * 512 + 512]
                    b.emit("dve", lambda v, sl=sl, half=half: v.tensor_tensor(
                        sl(), sl(), ps[4 + half][:, :], ALU.mult),
                        reads=[k_ps[4 + half]], writes=[k_x])
                    b.emit("dve", lambda v, sl=sl, half=half: v.tensor_tensor(
                        sl(), sl(), ps[2 + half][:, :], ALU.subtract),
                        reads=[k_ps[2 + half]], writes=[k_x])
            esL.close()

        b.barrier()
        layer_norm(t_f, k_f, "ln1")
        tap("h1", lambda: t_f[:, :], [k_f], [128, KC * TPC], F32)

        esF = ExitStack()
        t_h1b = esF.enter_context(nc.sbuf_tensor("h1b", [128, KC * TPC], BF16))
        t_lsb = esF.enter_context(nc.sbuf_tensor("lsb", [128, NE], F32))
        t_ysb = esF.enter_context(nc.sbuf_tensor("ysb", [128, NE], F32))
        t_msk = esF.enter_context(nc.sbuf_tensor("msk", [128, NE], F32))
        t_red = esF.enter_context(nc.sbuf_tensor("red", [128, 4], F32))
        t_ctm = esF.enter_context(nc.sbuf_tensor("ctm", [128, NE], BF16))
        t_cT = esF.enter_context(nc.sbuf_tensor("cT", [16, TPC], BF16))
        k_h1b, k_lsb, k_ysb, k_msk, k_red, k_ctm, k_cT = (
            T(), T(), T(), T(), T(), T(), T())
        b.emit("pool", lambda g: g.dma_start(out=t_h1b[:, :], in_=t_f[:, :]),
               reads=[k_f], writes=[k_h1b], dma=True, dkey="misc")

        tap("h1b", lambda: t_h1b[:, :], [k_h1b], [128, KC * TPC], BF16)
        for tc in range(NCH):
            fns = []
            for kci in range(KC):
                fns.append(lambda t, kci=kci, tc=tc: t.matmul(
                    ps[0][:, 0:NE],
                    t_f[:, kci * TPC + tc * 128: kci * TPC + tc * 128 + 128],
                    t_rwt[:, kci * NE:(kci + 1) * NE],
                    start=(kci == 0), stop=(kci == KC - 1)))
            b.emit("pe", None, fns=fns, reads=[k_f, k_rwt], writes=[k_ps[0]])
            b.emit("act", lambda s: s.copy(t_lsb[:, :], ps[0][:, 0:NE]),
                   reads=[k_ps[0]], writes=[k_lsb])
            b.emit("dve", lambda v: v.tensor_tensor(
                t_ysb[:, :], t_lsb[:, :], t_biasrep[:, :], ALU.add),
                reads=[k_lsb, k_consts], writes=[k_ysb])
            b.emit("dve", lambda v: v.reduce_max(
                t_red[:, 0:1], t_ysb[:, :], axis=AXX),
                reads=[k_ysb], writes=[k_red])
            b.emit("dve", lambda v: v.tensor_single_scalar(
                t_msk[:, :], t_ysb[:, :], t_red[:, 0:1], ALU.is_ge),
                reads=[k_ysb, k_red], writes=[k_msk])
            b.emit("dve", lambda v: v.scalar_tensor_tensor(
                t_ysb[:, :], t_msk[:, :], t_cvec[:, 4:5], t_ysb[:, :],
                ALU.mult, ALU.add),
                reads=[k_msk, k_cvec], writes=[k_ysb])
            b.emit("dve", lambda v: v.reduce_max(
                t_red[:, 1:2], t_ysb[:, :], axis=AXX),
                reads=[k_ysb], writes=[k_red])
            b.emit("dve", lambda v: v.tensor_single_scalar(
                t_ysb[:, :], t_ysb[:, :], t_red[:, 1:2], ALU.is_ge),
                reads=[k_red], writes=[k_ysb])
            b.emit("dve", lambda v: v.tensor_tensor(
                t_msk[:, :], t_msk[:, :], t_ysb[:, :], ALU.add),
                reads=[k_ysb], writes=[k_msk])
            b.emit("dve", lambda v: v.reduce_max(
                t_red[:, 2:3], t_lsb[:, :], axis=AXX),
                reads=[k_lsb], writes=[k_red])
            b.emit("dve", lambda v: v.tensor_single_scalar(
                t_lsb[:, :], t_lsb[:, :], t_red[:, 2:3], ALU.subtract),
                reads=[k_red], writes=[k_lsb])
            b.emit("act", lambda s: s.activation(
                t_lsb[:, :], t_lsb[:, :], AF.Exp),
                reads=[], writes=[k_lsb])
            b.emit("dve", lambda v: v.tensor_tensor(
                t_lsb[:, :], t_lsb[:, :], t_msk[:, :], ALU.mult),
                reads=[k_msk], writes=[k_lsb])
            b.emit("dve", lambda v: v.reduce_sum(
                t_red[:, 3:4], t_lsb[:, :], axis=AXX),
                reads=[k_lsb], writes=[k_red])
            b.emit("dve", lambda v: v.reciprocal(
                t_red[:, 3:4], t_red[:, 3:4]), reads=[], writes=[k_red])
            b.emit("dve", lambda v: v.tensor_scalar(
                t_ctm[:, :], t_lsb[:, :], t_red[:, 3:4], None, ALU.mult),
                reads=[k_lsb, k_red], writes=[k_ctm])
            if tc == 7:
                tap("rlogit", lambda: t_lsb[:, :], [k_lsb], [128, NE], F32)
                tap("rmsk", lambda: t_msk[:, :], [k_msk], [128, NE], F32)
                tap("rctm", lambda: t_ctm[:, :], [k_ctm], [128, NE], BF16)
            b.emit("pe", lambda t: t.transpose(
                psb[0:NE, 0:128], t_ctm[:, :], t_identb[:, :]),
                reads=[k_ctm, k_consts], writes=[k_psb])
            b.emit("act", lambda s, tc=tc: s.copy(
                t_cT[:, tc * 128:(tc + 1) * 128], psb[0:NE, 0:128]),
                reads=[k_psb], writes=[k_cT])

        tap("cT", lambda: t_cT[:, :], [k_cT], [16, TPC], BF16)

        b.barrier()
        t_usil = esF.enter_context(nc.sbuf_tensor("usil", [128, EKC * TPC], BF16))
        t_hb = [esF.enter_context(nc.sbuf_tensor(f"hb{i}", [128, EKC * TPC], BF16))
                for i in range(2)]
        t_cb = esF.enter_context(nc.sbuf_tensor("cb", [128, TPC], BF16))
        k_usil, k_cb = T(), T()
        k_hb = [T(), T()]

        for pr in range(NPAIR):
            b.barrier(2)
            for ei in range(2):
                e_idx = pr * 2 + ei
                for wgt, dram_w, is_u in ((0, d_wue, True), (1, d_wge, False)):
                    for mp in range(0, EKC, 2):
                        sg = (mp // 2) % 2
                        b.emit("pool", lambda g, e_idx=e_idx, mp=mp, sg=sg,
                               dram_w=dram_w: g.dma_start(
                                   out=t_stg[sg][:, 0:2 * EMBED].rearrange(
                                       "p (m f) -> p m f", m=2),
                                   in_=dram_w.ap()[e_idx, mp:mp + 2, :, :]
                                   .rearrange("m p f -> p m f")),
                               writes=[k_stg[sg]], dma=True, dkey=f"stg{sg}")
                        for mi in range(2):
                            m = mp + mi
                            for half in range(2):
                                bank = half
                                mm_sweep(
                                    bank,
                                    lambda kci, sg=sg, mi=mi:
                                    t_stg[sg][:, mi * EMBED + kci * 128:
                                              mi * EMBED + (kci + 1) * 128],
                                    lambda kci, half=half:
                                    t_h1b[:, kci * TPC + half * 512:
                                          kci * TPC + half * 512 + 512],
                                    KC, (k_stg[sg], k_h1b))
                                dst = lambda m=m, half=half: (
                                    m * TPC + half * 512, m * TPC + half * 512 + 512)
                                if is_u:
                                    b.emit("act", lambda s, d=dst(), bank=bank:
                                           s.activation(t_usil[:, d[0]:d[1]],
                                                        ps[bank][:, :], AF.Silu),
                                           reads=[k_ps[bank]], writes=[k_usil])
                                else:
                                    b.emit("dve", lambda v, d=dst(), bank=bank,
                                           ei=ei: v.tensor_tensor(
                                               t_hb[ei][:, d[0]:d[1]],
                                               ps[bank][:, :],
                                               t_usil[:, d[0]:d[1]], ALU.mult),
                                           reads=[k_ps[bank], k_usil],
                                           writes=[k_hb[ei]])
                if pr == 0 and ei == 0:
                    tap("usil0", lambda: t_usil[:, :], [k_usil],
                        [128, EKC * TPC], BF16)
                    tap("hb0raw", lambda: t_hb[0][:, :], [k_hb[0]],
                        [128, EKC * TPC], BF16)
                if e_idx < NE:
                    for half in range(2):
                        bank = 2 + half
                        b.emit("pe", lambda t, e_idx=e_idx, half=half, bank=bank:
                               t.matmul(ps[bank][:, :],
                                        t_rsel[:, e_idx * 128:(e_idx + 1) * 128],
                                        t_cT[:, half * 512:half * 512 + 512],
                                        start=True, stop=True),
                               reads=[k_cT, k_consts], writes=[k_ps[bank]])
                        b.emit("act", lambda s, half=half, bank=bank: s.copy(
                            t_cb[:, half * 512:half * 512 + 512],
                            ps[bank][:, :]),
                            reads=[k_ps[bank]], writes=[k_cb])
                    for m in range(EKC):
                        b.emit("dve", lambda v, m=m, ei=ei: v.tensor_tensor(
                            t_hb[ei][:, m * TPC:(m + 1) * TPC],
                            t_hb[ei][:, m * TPC:(m + 1) * TPC],
                            t_cb[:, :], ALU.mult),
                            reads=[k_cb], writes=[k_hb[ei]])
            if pr == 0:
                tap("hb0", lambda: t_hb[0][:, :], [k_hb[0]],
                    [128, EKC * TPC], BF16)
            for mp in range(0, 16, 2):
                sg = (mp // 2) % 2
                b.emit("pool", lambda g, pr=pr, mp=mp, sg=sg: g.dma_start(
                    out=t_stg[sg][:, 0:4096].rearrange("p (m i f) -> p m i f",
                                                       m=2, i=2),
                    in_=d_wdp.ap()[pr, mp:mp + 2, :, :, :]
                    .rearrange("m i p f -> p m i f")),
                    writes=[k_stg[sg]], dma=True, dkey=f"stg{sg}")
                for mi in range(2):
                    m = mp + mi
                    for half in range(2):
                        bank = half
                        fns = []
                        for ei in range(2):
                            for ke in range(EKC):
                                fns.append(lambda t, sg=sg, mi=mi, ei=ei, ke=ke,
                                           half=half, bank=bank: t.matmul(
                                               ps[bank][:, :],
                                               t_stg[sg][:, (mi * 2 + ei) * 1024 +
                                                         ke * 128:
                                                         (mi * 2 + ei) * 1024 +
                                                         (ke + 1) * 128],
                                               t_hb[ei][:, ke * TPC + half * 512:
                                                        ke * TPC + half * 512 + 512],
                                               start=(ei == 0 and ke == 0),
                                               stop=(ei == 1 and ke == EKC - 1)))
                        b.emit("pe", None, fns=fns,
                               reads=[k_stg[sg], k_hb[0], k_hb[1]],
                               writes=[k_ps[bank]])
                        b.emit("dve", lambda v, m=m, half=half, bank=bank:
                               v.tensor_tensor(
                                   t_f[:, m * TPC + half * 512:
                                       m * TPC + half * 512 + 512],
                                   t_f[:, m * TPC + half * 512:
                                       m * TPC + half * 512 + 512],
                                   ps[bank][:, :], ALU.add),
                               reads=[k_ps[bank]], writes=[k_f])
        tap("f", lambda: t_f[:, :], [k_f], [128, KC * TPC], F32)
        esF.close()

        b.barrier()
        layer_norm(t_f, k_f, "ln2")

        out_tok = b.emit("sync", lambda g: g.dma_start(
            out=d_out.ap().rearrange("(kc p) t -> p kc t", p=128),
            in_=t_f.ap().rearrange("p (kc t) -> p kc t", kc=KC)),
            reads=[k_f], dma=True, dkey="out")
        b.raw("sync", lambda g, s=out_tok[0], v=out_tok[1]: g.wait_ge(s, v))

        with nc.Block() as block:
            b.run_blocks(block)
    return nc


_NC_CACHE = None
_TAP_RESULTS = {}


def kernel(**inputs):
    global _NC_CACHE
    x = np.asarray(inputs["L0_attn_x"], np.float32)
    shared = host_weights(inputs)
    if _NC_CACHE is None:
        _NC_CACHE = build_program()
    nc = _NC_CACHE
    in_maps = []
    for c in range(NCORES):
        bi, cb = c // 4, c % 4
        xt = np.ascontiguousarray(x[bi, cb * TPC:(cb + 1) * TPC, :].T)
        m = dict(shared)
        m["xt"] = xt
        m.update(host_constants(c))
        in_maps.append(m)
    res = run_bass_kernel_spmd(nc, in_maps, core_ids=list(range(NCORES)))
    outs = []
    for c in range(NCORES):
        outs.append(res.results[c]["out_t"].T)
        for kk, vv in res.results[c].items():
            if kk.startswith("tap_"):
                _TAP_RESULTS.setdefault(kk, []).append(np.asarray(vv))
    full = np.concatenate(outs, axis=0).reshape(2, 4096, EMBED)
    return full.astype(np.float32)


# version 07
# speedup vs baseline: 1.2070x; 1.0034x over previous
import numpy as np
import ml_dtypes
import concourse.bass as bass
import concourse.mybir as mybir
from concourse.bass_utils import run_bass_kernel_spmd
from contextlib import ExitStack

F32 = mybir.dt.float32
BF16 = mybir.dt.bfloat16
I32 = mybir.dt.int32
AF = mybir.ActivationFunctionType
ALU = mybir.AluOpType
AXX = mybir.AxisListType.X

EMBED = 2048
HEADS = 16
HD = 128
C = 128
KC = 16
TPC = 1024
NCH = 8
NCORES = 8
NE = 16
NEX = 18
NPAIR = 9
EKC = 8
EPS_G = 1e-6
EPS_LN = 1e-5
BASE = 10000.0

_DEBUG_TAPS = set()
_PARANOID = 1

BF = ml_dtypes.bfloat16


def _log_lam():
    h = np.arange(HEADS, dtype=np.float64)
    return -(2.0 ** (-8.0 * (h + 1.0) / HEADS))


def host_constants(core):
    ll = _log_lam()
    cb = core % 4
    t0 = cb * TPC
    half = HD // 2
    theta = BASE ** (-2.0 * np.arange(half, dtype=np.float64) / HD)
    pos = np.arange(t0, t0 + TPC, dtype=np.float64)
    ang = pos[None, :] * np.concatenate([theta, theta])[:, None]
    cosb = np.cos(ang).astype(BF)
    sinb = np.sin(ang).astype(BF)

    i = np.arange(C, dtype=np.float64)
    diff = i[:, None] - i[None, :]
    dmask = np.where(diff[None] >= 0, np.exp(ll[:, None, None] * diff[None]), 0.0)
    dt_c = np.transpose(dmask, (0, 2, 1)).transpose(1, 0, 2).reshape(C, HEADS * C)
    dt_c = np.ascontiguousarray(dt_c).astype(BF)
    dq = np.exp(ll[:, None] * (i[None, :] + 1.0))
    dqr = np.broadcast_to(dq[None], (C, HEADS, C)).reshape(C, HEADS * C)
    dqr = np.ascontiguousarray(dqr).astype(BF)
    dkcol = np.exp(ll[None, :] * (C - 1.0 - i[:, None])).astype(np.float32)
    lam_C = np.exp(ll * C)
    lamcol = np.broadcast_to(lam_C[None, :], (C, HEADS)).astype(np.float32)
    lamcol = np.ascontiguousarray(lamcol)
    lj = lam_C[:, None] ** np.arange(NCH)[None, :]
    lj = np.where(lj < 1e-30, 0.0, lj)
    lamj = np.broadcast_to(lj.reshape(1, -1), (C, HEADS * NCH)).astype(np.float32)
    lamj = np.ascontiguousarray(lamj)

    identb = np.eye(128, dtype=BF)
    perm = np.zeros((128, 128), np.float32)
    for m in range(64):
        perm[m + 64, m] = -1.0
        perm[m, m + 64] = 1.0
    permb = perm.astype(BF)
    ohb = np.zeros((128, HEADS * 16), np.float32)
    for h in range(HEADS):
        ohb[:, h * 16 + h] = 1.0
    ohb = ohb.astype(BF)
    rsel = np.zeros((16, HEADS * 128), np.float32)
    for h in range(HEADS):
        rsel[h, h * 128:(h + 1) * 128] = 1.0
    rsel = rsel.astype(BF)
    onesf = np.ones((128, 1), np.float32)
    onesrow = np.ones((1, 128), np.float32)
    cvec = np.zeros((128, 8), np.float32)
    cvec[:, 0] = 1.0 / HD
    cvec[:, 1] = EPS_G
    cvec[:, 2] = 1.0 / EMBED
    cvec[:, 3] = EPS_LN
    cvec[:, 4] = -1e9
    cvec[:, 5] = 1.0
    lam_B = np.exp(ll * C * NCH)
    wcol3 = np.zeros((128, 3 * HEADS), np.float32)
    poff = np.zeros((1, 4), np.int32)
    for i3 in range(3):
        src = cb - 1 - i3
        r = (core - 1 - i3) % NCORES
        poff[0, i3] = r * EMBED * 128
        if src >= 0:
            w = lam_B ** i3
            w = np.where(w < 1e-30, 0.0, w)
            wcol3[:, i3 * HEADS:(i3 + 1) * HEADS] = w[None, :].astype(np.float32)
    return dict(cosb=cosb, sinb=sinb, dt_c=dt_c, dqr=dqr, dkcol=dkcol,
                lamcol=lamcol, lamj=lamj, identb=identb, permb=permb, ohb=ohb,
                rsel=rsel, onesf=onesf, onesrow=onesrow, cvec=cvec,
                wcol3=wcol3, poff=poff)


def host_weights(inp):
    def mpanels(W, n_in_kc, n_m):
        return np.ascontiguousarray(
            W.reshape(n_in_kc, 128, n_m, 128).transpose(2, 1, 0, 3)
            .reshape(n_m, 128, n_in_kc * 128))

    out = {}
    out["wq"] = mpanels(np.asarray(inp["L0_attn_Wq"], np.float32), KC, 16)
    out["wk"] = mpanels(np.asarray(inp["L0_attn_Wk"], np.float32), KC, 16)
    out["wo"] = mpanels(np.asarray(inp["L0_attn_Wo"], np.float32), KC, 16)
    wv = np.asarray(inp["L0_attn_Wv"], np.float32)
    out["wv"] = np.ascontiguousarray(
        wv.reshape(EMBED, 4, 512).transpose(1, 0, 2))
    out["wg1"] = mpanels(np.asarray(inp["L0_attn_Wg1"], np.float32), KC, 1)[0]
    out["wg2"] = np.asarray(inp["L0_attn_Wg2"], np.float32)
    out["rwt"] = np.ascontiguousarray(
        np.asarray(inp["L0_ffn_router_w"], np.float32).T.reshape(KC, 128, NE))
    out["biasrep"] = np.ascontiguousarray(np.broadcast_to(
        np.asarray(inp["L0_ffn_router_bias"], np.float32)[None, :], (128, NE)))
    out["nwcol"] = np.ascontiguousarray(
        np.asarray(inp["L0_attn_norm_weight"], np.float32).reshape(HEADS, 128).T)

    wu = np.asarray(inp["L0_ffn_Wu"], np.float32)
    wg = np.asarray(inp["L0_ffn_Wg"], np.float32)
    su = np.asarray(inp["L0_ffn_Su"], np.float32)[0]
    sg = np.asarray(inp["L0_ffn_Sg"], np.float32)[0]
    def upanels(W):
        return mpanels(W, KC, EKC)
    wue = np.empty((NEX, EKC, 128, EMBED), np.float32)
    wge = np.empty((NEX, EKC, 128, EMBED), np.float32)
    for e in range(NE):
        wue[e] = upanels(wu[e])
        wge[e] = upanels(wg[e])
    wue[16] = upanels(su[:, :1024]); wue[17] = upanels(su[:, 1024:])
    wge[16] = upanels(sg[:, :1024]); wge[17] = upanels(sg[:, 1024:])
    out["wue"] = np.ascontiguousarray(wue)
    out["wge"] = np.ascontiguousarray(wge)

    wd = np.asarray(inp["L0_ffn_Wd"], np.float32)
    sd = np.asarray(inp["L0_ffn_Sd"], np.float32)[0]
    def dpanels(W):
        return np.ascontiguousarray(
            W.reshape(EKC, 128, 16, 128).transpose(2, 1, 0, 3)
            .reshape(16, 128, EKC * 128))
    wdx = np.empty((NEX, 16, 128, EKC * 128), np.float32)
    for e in range(NE):
        wdx[e] = dpanels(wd[e])
    wdx[16] = dpanels(sd[:1024, :])
    wdx[17] = dpanels(sd[1024:, :])
    out["wdp"] = np.ascontiguousarray(
        wdx.reshape(NPAIR, 2, 16, 128, EKC * 128).transpose(0, 2, 1, 3, 4))
    return out


class Trk:
    __slots__ = ("w", "rs")

    def __init__(self):
        self.w = None
        self.rs = []


class B:

    ENGS = ("pe", "dve", "act", "pool", "sync")

    def __init__(self, nc):
        self.nc = nc
        self.eng = {}
        self.sems = {}

    DKEYS = {"pool": ("stg0", "stg1", "xbf", "g", "misc"),
             "sync": ("main", "rwt", "cos", "sin", "dqr", "dtm",
                      "ccin", "xf", "out", "g")}

    def init_sems(self, es):
        for name in self.ENGS:
            sem = es.enter_context(self.nc.semaphore(f"s_{name}"))
            d = dict(sem=sem, cnt=0, waited={}, script=[], dsems={})
            for key in self.DKEYS.get(name, ()):
                d["dsems"][key] = [
                    es.enter_context(self.nc.semaphore(f"s_{name}_{key}")), 0]
            self.eng[name] = d
        self.cc_sem = es.enter_context(self.nc.semaphore("s_cc"))

    def _waits(self, e, deps, skip_self=False):
        need = {}
        for (sem, val) in deps:
            if skip_self and sem is e["sem"]:
                continue
            if need.get(sem, 0) < val:
                need[sem] = val
        for sem, val in sorted(need.items(), key=lambda kv: str(kv[0].name)):
            if e["waited"].get(sem, 0) < val:
                e["waited"][sem] = val
                e["script"].append(
                    lambda eng, s=sem, v=val: eng.wait_ge(s, v))

    def emit(self, en, fn, reads=(), writes=(), dma=False, fns=None,
             dkey="main"):
        e = self.eng[en]
        deps = []
        for t in reads:
            if t.w:
                deps.append(t.w)
        for t in writes:
            if t.w:
                deps.append(t.w)
            deps.extend(t.rs)
        self._waits(e, deps, skip_self=(en == "pe"))
        if dma:
            ds = e["dsems"][dkey]
            ds[1] += 16
            sem, cnt = ds[0], ds[1]
            amt = 16
        else:
            e["cnt"] += 1
            sem, cnt = e["sem"], e["cnt"]
            amt = 1
        if fns is not None:
            for f in fns[:-1]:
                e["script"].append(lambda eng, f=f: f(eng))
            e["script"].append(
                lambda eng, f=fns[-1], s=sem, a=amt: f(eng).then_inc(s, a))
        else:
            e["script"].append(
                lambda eng, f=fn, s=sem, a=amt: f(eng).then_inc(s, a))
        tok = (sem, cnt)
        for t in reads:
            t.rs.append(tok)
        for t in writes:
            t.w = tok
            t.rs = []
        return tok

    def raw(self, en, fn):
        self.eng[en]["script"].append(fn)

    def barrier(self, level=1):
        import kernel as _k
        if _PARANOID < level:
            return
        sems = []
        for name in self.ENGS:
            e = self.eng[name]
            if e["cnt"]:
                sems.append((e["sem"], e["cnt"]))
            for key, sc in e["dsems"].items():
                if sc[1]:
                    sems.append((sc[0], sc[1]))
        for name in self.ENGS:
            e = self.eng[name]
            self._waits(e, [t for t in sems if t[0] is not e["sem"]])

    def run_blocks(self, block):
        block.tensor(lambda eng: [f(eng) for f in self.eng["pe"]["script"]])
        block.vector(lambda eng: [f(eng) for f in self.eng["dve"]["script"]])
        block.scalar(lambda eng: [f(eng) for f in self.eng["act"]["script"]])
        block.gpsimd(lambda eng: [f(eng) for f in self.eng["pool"]["script"]])
        block.sync(lambda eng: [f(eng) for f in self.eng["sync"]["script"]])


def build_program():
    nc = bass.Bass("TRN2", target_bir_lowering=False)
    ll = _log_lam()
    lam_C = np.exp(ll * C)

    d_xt = nc.dram_tensor("xt", [EMBED, TPC], F32, kind="ExternalInput")
    d_wq = nc.dram_tensor("wq", [16, 128, EMBED], F32, kind="ExternalInput")
    d_wk = nc.dram_tensor("wk", [16, 128, EMBED], F32, kind="ExternalInput")
    d_wo = nc.dram_tensor("wo", [16, 128, EMBED], F32, kind="ExternalInput")
    d_wv = nc.dram_tensor("wv", [4, EMBED, 512], F32, kind="ExternalInput")
    d_wg1 = nc.dram_tensor("wg1", [128, EMBED], F32, kind="ExternalInput")
    d_wg2 = nc.dram_tensor("wg2", [128, EMBED], F32, kind="ExternalInput")
    d_rwt = nc.dram_tensor("rwt", [KC, 128, NE], F32, kind="ExternalInput")
    d_biasrep = nc.dram_tensor("biasrep", [128, NE], F32, kind="ExternalInput")
    d_nwcol = nc.dram_tensor("nwcol", [128, HEADS], F32, kind="ExternalInput")
    d_wue = nc.dram_tensor("wue", [NEX, EKC, 128, EMBED], F32, kind="ExternalInput")
    d_wge = nc.dram_tensor("wge", [NEX, EKC, 128, EMBED], F32, kind="ExternalInput")
    d_wdp = nc.dram_tensor("wdp", [NPAIR, 16, 2, 128, EKC * 128], F32,
                           kind="ExternalInput")
    d_cosb = nc.dram_tensor("cosb", [128, TPC], BF16, kind="ExternalInput")
    d_sinb = nc.dram_tensor("sinb", [128, TPC], BF16, kind="ExternalInput")
    d_dt = nc.dram_tensor("dt_c", [128, HEADS * C], BF16, kind="ExternalInput")
    d_dqr = nc.dram_tensor("dqr", [128, HEADS * C], BF16, kind="ExternalInput")
    d_dkcol = nc.dram_tensor("dkcol", [128, HEADS], F32, kind="ExternalInput")
    d_lamcol = nc.dram_tensor("lamcol", [128, HEADS], F32, kind="ExternalInput")
    d_lamj = nc.dram_tensor("lamj", [128, HEADS * NCH], F32, kind="ExternalInput")
    d_identb = nc.dram_tensor("identb", [128, 128], BF16, kind="ExternalInput")
    d_permb = nc.dram_tensor("permb", [128, 128], BF16, kind="ExternalInput")
    d_ohb = nc.dram_tensor("ohb", [128, HEADS * 16], BF16, kind="ExternalInput")
    d_rsel = nc.dram_tensor("rsel", [16, HEADS * 128], BF16, kind="ExternalInput")
    d_onesf = nc.dram_tensor("onesf", [128, 1], F32, kind="ExternalInput")
    d_onesrow = nc.dram_tensor("onesrow", [1, 128], F32, kind="ExternalInput")
    d_cvec = nc.dram_tensor("cvec", [128, 8], F32, kind="ExternalInput")
    d_wcol3 = nc.dram_tensor("wcol3", [128, 3 * HEADS], F32, kind="ExternalInput")
    d_poff = nc.dram_tensor("poff", [1, 4], I32, kind="ExternalInput")
    d_out = nc.dram_tensor("out_t", [EMBED, TPC], F32, kind="ExternalOutput")
    d_ccin = nc.dram_tensor("cc_in", [EMBED, 128], F32)
    d_ccout = nc.dram_tensor("cc_out", [NCORES * EMBED, 128], F32)

    b = B(nc)

    def tap(name, ap_fn, reads, shape, dtype):
        if name not in _DEBUG_TAPS:
            return
        d = nc.dram_tensor("tap_" + name, shape, dtype, kind="ExternalOutput")
        b.emit("sync", lambda g, d=d, a=ap_fn(): g.dma_start(out=d.ap(), in_=a),
               reads=list(reads), dma=True, dkey="out")
    es = ExitStack()
    with es:
        b.init_sems(es)
        sb = lambda name, shape, dt, **kw: es.enter_context(
            nc.sbuf_tensor(name, shape, dt, **kw))
        t_cvec = sb("cvec_s", [128, 8], F32)
        t_identb = sb("identb_s", [128, 128], BF16)
        t_permb = sb("permb_s", [128, 128], BF16)
        t_ohb = sb("ohb_s", [128, HEADS * 16], BF16)
        t_rsel = sb("rsel_s", [16, HEADS * 128], BF16)
        t_onesf = sb("onesf_s", [128, 1], F32)
        t_onesrow = sb("onesrow_s", [1, 128], F32)
        t_biasrep = sb("biasrep_s", [128, NE], F32)
        t_nwcol = sb("nwcol_s", [128, HEADS], F32)
        t_dkcol = sb("dkcol_s", [128, HEADS], F32)
        t_lamcol = sb("lamcol_s", [128, HEADS], F32)
        t_lamj = sb("lamj_s", [128, HEADS * NCH], F32)
        t_wcol3 = sb("wcol3_s", [128, 3 * HEADS], F32)
        t_poff = sb("poff_s", [1, 4], I32)
        t_rwt = sb("rwt_s", [128, KC * NE], F32)
        t_g1 = sb("g1_s", [128, TPC], BF16)
        t_stg = [sb(f"stg{i}", [128, 4096], BF16) for i in range(2)]
        ps = [es.enter_context(nc.psum_tensor(f"ps{i}", [128, 512], F32))
              for i in range(6)]
        psb2 = [es.enter_context(nc.psum_tensor(f"psb{i}", [128, 1024], BF16))
                for i in range(2)]
        psb = psb2[0]

        T = lambda: Trk()
        k_cvec, k_consts = T(), T()
        k_rwt, k_g1 = T(), T()
        k_stg = [T(), T()]
        k_ps = [T() for _ in range(6)]
        k_psb2 = [T(), T()]
        k_psb = k_psb2[0]

        def ld(dst_ap, src_ap, trk, queue="sync", dkey="main"):
            b.emit(queue, lambda g, d=dst_ap, s=src_ap: g.dma_start(out=d, in_=s),
                   writes=[trk], dma=True, dkey=dkey)

        ld(t_cvec[:, :], d_cvec.ap(), k_cvec)
        for dst, src in ((t_identb, d_identb), (t_permb, d_permb),
                         (t_ohb, d_ohb), (t_rsel, d_rsel), (t_onesf, d_onesf),
                         (t_onesrow, d_onesrow), (t_biasrep, d_biasrep),
                         (t_nwcol, d_nwcol), (t_dkcol, d_dkcol),
                         (t_lamcol, d_lamcol), (t_lamj, d_lamj),
                         (t_wcol3, d_wcol3), (t_poff, d_poff)):
            ld(dst[:, :], src.ap(), k_consts)
        ld(t_rwt.ap().rearrange("p (kc n) -> p kc n", kc=KC),
           d_rwt.ap().rearrange("kc p n -> p kc n"), k_rwt, dkey="rwt")

        b.barrier()
        esA = ExitStack()
        t_obuf = esA.enter_context(nc.sbuf_tensor("obuf", [128, HEADS * TPC], BF16))
        t_qbuf = esA.enter_context(nc.sbuf_tensor("qbuf", [128, HEADS * TPC], BF16))
        t_S = esA.enter_context(nc.sbuf_tensor("S_f", [128, HEADS * HD], F32))
        t_Sb = esA.enter_context(nc.sbuf_tensor("S_b", [128, HEADS * HD], BF16))
        k_obuf = [T() for _ in range(HEADS)]
        k_qbuf = [T() for _ in range(HEADS)]
        k_S = [T() for _ in range(HEADS)]
        k_Sb = [T() for _ in range(HEADS)]
        k_dqr, k_dt = T(), T()

        esA2 = ExitStack()
        t_kfm = esA2.enter_context(nc.sbuf_tensor("kfm", [128, HEADS * TPC], BF16))
        t_scb = esA2.enter_context(nc.sbuf_tensor("scb", [128, C], BF16))
        t_ktb = esA2.enter_context(nc.sbuf_tensor("ktb", [128, C], BF16))
        esX = ExitStack()
        t_xbf = esX.enter_context(nc.sbuf_tensor("xbf", [128, KC * TPC], BF16,
                                                 side="right"))
        k_xbf = T()
        k_kfm = [T() for _ in range(HEADS)]
        k_vtm = [T() for _ in range(NCH * 4)]
        k_scb, k_ktb = T(), T()

        ld(t_xbf.ap().rearrange("p (kc t) -> p kc t", kc=KC),
           d_xt.ap().rearrange("(kc p) t -> p kc t", p=128), k_xbf, queue="pool",
           dkey="xbf")

        esR = ExitStack()
        t_cos = esR.enter_context(nc.sbuf_tensor("cosb_s", [128, TPC], BF16, side="right"))
        t_sin = esR.enter_context(nc.sbuf_tensor("sinb_s", [128, TPC], BF16, side="right"))
        t_qsil = esR.enter_context(nc.sbuf_tensor("qsil", [128, TPC], BF16, side="right"))
        t_rt1 = esR.enter_context(nc.sbuf_tensor("rt1", [128, TPC], BF16, side="right"))
        t_rt2 = esR.enter_context(nc.sbuf_tensor("rt2", [128, TPC], BF16, side="right"))
        k_cos, k_sin, k_qsil, k_rt1, k_rt2 = T(), T(), T(), T(), T()
        ld(t_cos[:, :], d_cosb.ap(), k_cos, dkey="cos")
        ld(t_sin[:, :], d_sinb.ap(), k_sin, dkey="sin")

        def mm_sweep(bank, lhs_fn, rhs_fn, n_k, reads):
            fns = []
            for kci in range(n_k):
                fns.append(lambda t, l=lhs_fn(kci), r=rhs_fn(kci),
                           st=(kci == 0), sp=(kci == n_k - 1), bk=bank:
                           t.matmul(ps[bk][:, :], l, r, start=st, stop=sp))
            return b.emit("pe", None, fns=fns, reads=list(reads),
                          writes=[k_ps[bank]])

        def qk_phase(dram_w, dest, k_dest):
            for m in range(HEADS):
                sg = m % 2
                b.emit("pool", lambda g, m=m, sg=sg: g.dma_start(
                    out=t_stg[sg][:, :EMBED], in_=dram_w.ap()[m, :, :]),
                    writes=[k_stg[sg]], dma=True, dkey=f"stg{sg}")
                for half in range(2):
                    bank = half
                    mm_sweep(bank,
                             lambda kci, sg=sg: t_stg[sg][:, kci * 128:(kci + 1) * 128],
                             lambda kci, half=half: t_xbf[:, kci * TPC + half * 512:
                                                          kci * TPC + half * 512 + 512],
                             KC, (k_stg[sg], k_xbf))
                    b.emit("act", lambda s, bank=bank, half=half: s.activation(
                        t_qsil[:, half * 512:half * 512 + 512], ps[bank][:, :],
                        AF.Silu), reads=[k_ps[bank]], writes=[k_qsil])
                    b.emit("pe", lambda t, half=half: t.matmul(
                        ps[2 + half][:, :], t_permb[:, :],
                        t_qsil[:, half * 512:half * 512 + 512],
                        start=True, stop=True),
                        reads=[k_qsil, k_consts], writes=[k_ps[2 + half]])
                b.emit("dve", lambda v, m=m: v.tensor_tensor(
                    t_rt1[:, :], t_qsil[:, :], t_cos[:, :], ALU.mult),
                    reads=[k_qsil, k_cos], writes=[k_rt1])
                for half in range(2):
                    b.emit("dve", lambda v, half=half: v.tensor_tensor(
                        t_rt2[:, half * 512:half * 512 + 512],
                        ps[2 + half][:, :],
                        t_sin[:, half * 512:half * 512 + 512], ALU.mult),
                        reads=[k_ps[2 + half], k_sin], writes=[k_rt2])
                b.emit("dve", lambda v, m=m: v.tensor_tensor(
                    dest[:, m * TPC:(m + 1) * TPC], t_rt1[:, :], t_rt2[:, :],
                    ALU.add), reads=[k_rt1, k_rt2], writes=[k_dest[m]])

        qk_phase(d_wq, t_qbuf, k_qbuf)
        b.barrier()
        qk_phase(d_wk, t_kfm, k_kfm)
        b.barrier()
        esR.close()
        tap("qk_q", lambda: t_qbuf[:, :], k_qbuf, [128, HEADS * TPC], BF16)
        tap("qk_k", lambda: t_kfm[:, :], k_kfm, [128, HEADS * TPC], BF16)

        b.emit("pool", lambda g: g.dma_start(
            out=t_stg[0][:, :EMBED], in_=d_wg1.ap()), writes=[k_stg[0]],
            dma=True, dkey="stg0")
        for half in range(2):
            mm_sweep(half,
                     lambda kci: t_stg[0][:, kci * 128:(kci + 1) * 128],
                     lambda kci, half=half: t_xbf[:, kci * TPC + half * 512:
                                                  kci * TPC + half * 512 + 512],
                     KC, (k_stg[0], k_xbf))
            b.emit("act", lambda s, half=half: s.copy(
                t_g1[:, half * 512:half * 512 + 512], ps[half][:, :]),
                reads=[k_ps[half]], writes=[k_g1])

        tap("g1", lambda: t_g1[:, :], [k_g1], [128, TPC], BF16)
        esV = ExitStack()
        t_vtm = esV.enter_context(nc.sbuf_tensor("vtm", [128, NCH * HEADS * HD],
                                                 BF16))
        for nf in range(4):
            for kh in range(2):
                b.emit("pool", lambda g, nf=nf, kh=kh: g.dma_start(
                    out=t_stg[kh].ap().rearrange("p (kc f) -> p kc f", kc=8),
                    in_=d_wv.ap()[nf, kh * 1024:(kh + 1) * 1024, :]
                    .rearrange("(kc p) f -> p kc f", p=128)),
                    writes=[k_stg[kh]], dma=True, dkey=f"stg{kh}")
            for tc in range(NCH):
                bank = tc % 2
                fns = []
                for kci in range(KC):
                    kh, kloc = kci // 8, kci % 8
                    fns.append(lambda t, kci=kci, kh=kh, kloc=kloc, tc=tc,
                               bank=bank: t.matmul(
                                   ps[bank][:, :],
                                   t_xbf[:, kci * TPC + tc * 128:
                                         kci * TPC + tc * 128 + 128],
                                   t_stg[kh][:, kloc * 512:(kloc + 1) * 512],
                                   start=(kci == 0), stop=(kci == KC - 1)))
                b.emit("pe", None, fns=fns,
                       reads=[k_stg[0], k_stg[1], k_xbf], writes=[k_ps[bank]])
                b.emit("act", lambda s, tc=tc, nf=nf, bank=bank: s.copy(
                    t_vtm[:, (tc * HEADS + nf * 4) * HD:
                          (tc * HEADS + nf * 4) * HD + 512], ps[bank][:, :]),
                    reads=[k_ps[bank]], writes=[k_vtm[tc * 4 + nf]])
        esX.close()
        t_dqr = esV.enter_context(nc.sbuf_tensor("dqr_s", [128, HEADS * C], BF16))
        t_dt = esV.enter_context(nc.sbuf_tensor("dt_s", [128, HEADS * C], BF16))
        ld(t_dqr[:, :], d_dqr.ap(), k_dqr, dkey="dqr")
        ld(t_dt[:, :], d_dt.ap(), k_dt, dkey="dtm")

        b.barrier()
        tap("v", lambda: t_vtm[:, :], k_vtm, [128, NCH * HEADS * HD], BF16)
        t_scg = [esV.enter_context(nc.sbuf_tensor(f"scg{i}", [128, 8 * C], BF16))
                 for i in range(2)]
        t_ktg = [esV.enter_context(nc.sbuf_tensor(f"ktg{i}", [128, 8 * C], BF16))
                 for i in range(2)]
        k_scg = [T(), T()]
        k_ktg = [T(), T()]

        def qsl(h, j):
            return t_qbuf[:, h * TPC + j * C: h * TPC + (j + 1) * C]

        def ksl(h, j):
            return t_kfm[:, h * TPC + j * C: h * TPC + (j + 1) * C]

        def vsl(h, j):
            return t_vtm[:, (j * HEADS + h) * HD:(j * HEADS + h + 1) * HD]

        def scan_front(g):
            j, hg = g // 2, g % 2
            hs = [hg * 8 + i for i in range(8)]
            par = g % 2
            pkt, kkt = psb2[par], k_psb2[par]
            b.emit("pe", None, fns=[
                lambda t, h=h, i=i, j=j, pkt=pkt: t.transpose(
                    pkt[:, i * C:(i + 1) * C], ksl(h, j), t_identb[:, :])
                for i, h in enumerate(hs)],
                reads=[k_kfm[h] for h in hs] + [k_consts], writes=[kkt])
            b.emit("act", None, fns=[
                lambda s, h=h, i=i, pkt=pkt, par=par: s.activation(
                    t_ktg[par][:, i * C:(i + 1) * C], pkt[:, i * C:(i + 1) * C],
                    AF.Copy, scale=t_dkcol[:, h:h + 1])
                for i, h in enumerate(hs)],
                reads=[kkt, k_consts], writes=[k_ktg[par]])
            b.emit("pe", None, fns=[
                lambda t, h=h, i=i, j=j: t.matmul(
                    ps[i // 4][:, (i % 4) * C:(i % 4) * C + C],
                    ksl(h, j), qsl(h, j), start=True, stop=True,
                    skip_group_check=True)
                for i, h in enumerate(hs)],
                reads=[k_kfm[h] for h in hs] + [k_qbuf[h] for h in hs],
                writes=[k_ps[0], k_ps[1]])
            b.emit("dve", None, fns=[
                lambda v, h=h, i=i, par=par: v.tensor_tensor(
                    t_scg[par][:, i * C:(i + 1) * C],
                    ps[i // 4][:, (i % 4) * C:(i % 4) * C + C],
                    t_dt[:, h * C:(h + 1) * C], ALU.mult)
                for i, h in enumerate(hs)],
                reads=[k_ps[0], k_ps[1], k_dt], writes=[k_scg[par]])
            b.emit("dve", None, fns=[
                lambda v, h=h, j=j: v.tensor_tensor(
                    qsl(h, j), qsl(h, j), t_dqr[:, h * C:(h + 1) * C], ALU.mult)
                for h in hs],
                reads=[k_dqr], writes=[k_qbuf[h] for h in hs])

        def scan_back(g):
            j, hg = g // 2, g % 2
            hs = [hg * 8 + i for i in range(8)]
            par = g % 2
            kvs = [k_vtm[j * 4 + h // 4] for h in hs]
            ofns = []
            for i, h in enumerate(hs):
                osl = lambda i=i: ps[2 + i // 4][:, (i % 4) * C:(i % 4) * C + C]
                if j == 0:
                    ofns.append(lambda t, h=h, i=i, j=j, osl=osl, par=par:
                                t.matmul(osl(), vsl(h, j),
                                         t_scg[par][:, i * C:(i + 1) * C],
                                         start=True, stop=True,
                                         skip_group_check=True))
                else:
                    ofns.append(lambda t, h=h, i=i, j=j, osl=osl, par=par:
                                t.matmul(osl(), vsl(h, j),
                                         t_scg[par][:, i * C:(i + 1) * C],
                                         start=True, stop=False,
                                         skip_group_check=True))
                    ofns.append(lambda t, h=h, i=i, j=j, osl=osl: t.matmul(
                        osl(), t_Sb[:, h * HD:(h + 1) * HD], qsl(h, j),
                        start=False, stop=True, skip_group_check=True))
            b.emit("pe", None, fns=ofns,
                   reads=kvs + [k_scg[par]] + [k_Sb[h] for h in hs]
                   + [k_qbuf[h] for h in hs],
                   writes=[k_ps[2], k_ps[3]])
            b.emit("act", None, fns=[
                lambda s, h=h, i=i, j=j: s.copy(
                    t_obuf[:, h * TPC + j * C: h * TPC + (j + 1) * C],
                    ps[2 + i // 4][:, (i % 4) * C:(i % 4) * C + C])
                for i, h in enumerate(hs)],
                reads=[k_ps[2], k_ps[3]], writes=[k_obuf[h] for h in hs])
            b.emit("pe", None, fns=[
                lambda t, h=h, i=i, j=j, par=par: t.matmul(
                    ps[4 + i // 4][:, (i % 4) * C:(i % 4) * C + C],
                    t_ktg[par][:, i * C:(i + 1) * C], vsl(h, j),
                    start=True, stop=True, skip_group_check=True)
                for i, h in enumerate(hs)],
                reads=[k_ktg[par]] + kvs, writes=[k_ps[4], k_ps[5]])
            sfns = []
            for i, h in enumerate(hs):
                asl = lambda i=i: ps[4 + i // 4][:, (i % 4) * C:(i % 4) * C + C]
                if j == 0:
                    sfns.append(lambda v, h=h, asl=asl: v.tensor_copy(
                        t_S[:, h * HD:(h + 1) * HD], asl()))
                else:
                    sfns.append(lambda v, h=h, asl=asl: v.scalar_tensor_tensor(
                        t_S[:, h * HD:(h + 1) * HD], t_S[:, h * HD:(h + 1) * HD],
                        t_lamcol[:, h:h + 1], asl(), ALU.mult, ALU.add))
            b.emit("dve", None, fns=sfns,
                   reads=[k_ps[4], k_ps[5], k_consts],
                   writes=[k_S[h] for h in hs])
            if j < NCH - 1:
                b.emit("act", None, fns=[
                    lambda s, h=h: s.copy(
                        t_Sb[:, h * HD:(h + 1) * HD],
                        t_S[:, h * HD:(h + 1) * HD])
                    for h in hs],
                    reads=[k_S[h] for h in hs],
                    writes=[k_Sb[h] for h in hs])

        for g in range(2 * NCH + 1):
            if g < 2 * NCH:
                scan_front(g)
            if g >= 1:
                scan_back(g - 1)
        tap("o_scan", lambda: t_obuf[:, :], k_obuf, [128, HEADS * TPC], BF16)
        tap("s_loc", lambda: t_S[:, :], k_S, [128, HEADS * HD], F32)
        esV.close()
        esA2.close()

        b.barrier()
        esC = ExitStack()
        t_G = [esC.enter_context(nc.sbuf_tensor(f"G{i}", [128, HEADS * HD], F32))
               for i in range(3)]
        t_tmpA = esC.enter_context(nc.sbuf_tensor("tmpA", [128, HD], F32))
        t_tmpB = esC.enter_context(nc.sbuf_tensor("tmpB", [128, HD], F32))
        t_Sin = esC.enter_context(nc.sbuf_tensor("Sin", [128, HEADS * HD], BF16))
        t_qt2 = [esC.enter_context(nc.sbuf_tensor(f"qt{i}", [128, C], BF16))
                 for i in range(2)]
        t_gsa = esC.enter_context(nc.sbuf_tensor("gsa", [128, HEADS * TPC], BF16))
        t_sq = esC.enter_context(nc.sbuf_tensor("sqg", [128, TPC], BF16))
        t_rstd = esC.enter_context(nc.sbuf_tensor("rstdg", [16, TPC], F32))
        t_rstdb = esC.enter_context(nc.sbuf_tensor("rstdgb", [16, TPC], BF16))
        k_G = [T() for _ in range(3)]
        k_tmpA, k_tmpB, k_Sin = T(), T(), T()
        k_qt2 = [T(), T()]
        k_gsa, k_sq, k_rstd, k_rstdb = T(), T(), T(), T()
        k_ccin, k_ccout = T(), T()

        b.emit("sync", lambda g: g.dma_start(
            out=d_ccin.ap().rearrange("(h p) e -> p h e", p=128),
            in_=t_S.ap().rearrange("p (h e) -> p h e", h=HEADS)),
            reads=list(k_S), writes=[k_ccin], dma=True, dkey="ccin")

        e = b.eng["pool"]
        b._waits(e, [k_ccin.w])
        cc_sem = b.cc_sem
        e["script"].append(lambda g, s=cc_sem: g.collective_compute(
            "AllGather", ALU.bypass, replica_groups=[list(range(NCORES))],
            ins=[d_ccin.ap().opt()], outs=[d_ccout.ap().opt()]).then_inc(s, 1))
        k_ccout.w = (cc_sem, 1)

        b.emit("pool", lambda g: g.dma_start(
            out=t_stg[0][:, :EMBED], in_=d_wg2.ap()),
            writes=[k_stg[0]], dma=True, dkey="stg0")
        k_wg2v = k_stg[0]
        for h in range(HEADS):
            for half in range(2):
                bank = (h * 2 + half) % 4
                b.emit("pe", lambda t, h=h, half=half, bank=bank: t.matmul(
                    ps[bank][:, :], t_stg[0][:, h * 128:(h + 1) * 128],
                    t_g1[:, half * 512:half * 512 + 512], start=True, stop=True),
                    reads=[k_wg2v, k_g1], writes=[k_ps[bank]])
                b.emit("act", lambda s, h=h, half=half, bank=bank: s.activation(
                    t_gsa[:, h * TPC + half * 512: h * TPC + half * 512 + 512],
                    ps[bank][:, :], AF.Sigmoid),
                    reads=[k_ps[bank]], writes=[k_gsa])

        esy = b.eng["sync"]
        dsem = esy["dsems"]["g"][0] if "g" in esy["dsems"] else None
        if dsem is None:
            raise RuntimeError("missing sync g key")

        def fetch_G(g, s):
            with g.register("gofs") as gofs:
                for i in range(3):
                    g.reg_load(gofs, t_poff[:1, i:i + 1])
                    g.dma_start(
                        out=t_G[i].ap().rearrange("p (h e) -> p h e", h=HEADS),
                        in_=bass.AP(d_ccout, gofs,
                                    [[128, 128], [128 * 128, HEADS], [1, 128]]),
                    ).then_inc(s, 16)
        b._waits(esy, [k_ccout.w, k_consts.w])
        esy["dsems"]["g"][1] += 48
        g_tok = (dsem, esy["dsems"]["g"][1])
        esy["script"].append(lambda g, s=dsem: fetch_G(g, s))
        for kk in k_G:
            kk.w = g_tok

        for h in range(HEADS):
            b.emit("dve", lambda v, h=h: v.tensor_scalar(
                t_tmpA[:, :], t_G[0][:, h * HD:(h + 1) * HD],
                t_wcol3[:, h:h + 1], None, ALU.mult),
                reads=[k_G[0], k_consts], writes=[k_tmpA])
            b.emit("dve", lambda v, h=h: v.scalar_tensor_tensor(
                t_tmpB[:, :], t_G[1][:, h * HD:(h + 1) * HD],
                t_wcol3[:, HEADS + h:HEADS + h + 1], t_tmpA[:, :],
                ALU.mult, ALU.add),
                reads=[k_G[1], k_consts, k_tmpA], writes=[k_tmpB])
            b.emit("dve", lambda v, h=h: v.scalar_tensor_tensor(
                t_tmpA[:, :], t_G[2][:, h * HD:(h + 1) * HD],
                t_wcol3[:, 2 * HEADS + h:2 * HEADS + h + 1], t_tmpB[:, :],
                ALU.mult, ALU.add),
                reads=[k_G[2], k_consts, k_tmpB], writes=[k_tmpA])
            b.emit("act", lambda s, h=h: s.copy(
                t_Sin[:, h * HD:(h + 1) * HD], t_tmpA[:, :]),
                reads=[k_tmpA], writes=[k_Sin])

        lj_np = lam_C[:, None] ** np.arange(NCH)[None, :]
        pc = 0
        for h in range(HEADS):
            for j in range(NCH):
                if lj_np[h, j] < 1e-30:
                    continue
                par = pc % 2
                pc += 1
                b.emit("dve", lambda v, h=h, j=j, par=par: v.tensor_scalar(
                    t_qt2[par][:, :],
                    t_qbuf[:, h * TPC + j * C: h * TPC + (j + 1) * C],
                    t_lamj[:, h * NCH + j: h * NCH + j + 1], None, ALU.mult),
                    reads=[k_qbuf[h], k_consts], writes=[k_qt2[par]])
                b.emit("pe", lambda t, h=h, par=par: t.matmul(
                    ps[par][:, 0:C], t_Sin[:, h * HD:(h + 1) * HD],
                    t_qt2[par][:, :], start=True, stop=True),
                    reads=[k_Sin, k_qt2[par]], writes=[k_ps[par]])
                b.emit("dve", lambda v, h=h, j=j, par=par: v.tensor_tensor(
                    t_obuf[:, h * TPC + j * C: h * TPC + (j + 1) * C],
                    t_obuf[:, h * TPC + j * C: h * TPC + (j + 1) * C],
                    ps[par][:, 0:C], ALU.add),
                    reads=[k_ps[par]], writes=[k_obuf[h]])
        tap("o_corr", lambda: t_obuf[:, :], k_obuf, [128, HEADS * TPC], BF16)
        tap("sin_t", lambda: t_Sin[:, :], [k_Sin], [128, HEADS * HD], BF16)
        for h in range(HEADS):
            b.emit("dve", lambda v, h=h: v.tensor_tensor(
                t_obuf[:, h * TPC:(h + 1) * TPC],
                t_obuf[:, h * TPC:(h + 1) * TPC],
                t_gsa[:, h * TPC:(h + 1) * TPC], ALU.mult),
                reads=[k_gsa], writes=[k_obuf[h]])
            b.emit("act", lambda s, h=h: s.activation(
                t_sq[:, :], t_obuf[:, h * TPC:(h + 1) * TPC], AF.Square),
                reads=[k_obuf[h]], writes=[k_sq])
            for half in range(2):
                b.emit("pe", lambda t, h=h, half=half: t.matmul(
                    ps[2 + half][0:16, :], t_ohb[:, h * 16:(h + 1) * 16],
                    t_sq[:, half * 512:half * 512 + 512],
                    start=(h == 0), stop=(h == HEADS - 1)),
                    reads=[k_sq, k_consts], writes=[k_ps[2 + half]])
        tap("xg", lambda: t_obuf[:, :], k_obuf, [128, HEADS * TPC], BF16)
        for half in range(2):
            b.emit("act", lambda s, half=half: s.activation(
                t_rstd[:, half * 512:half * 512 + 512], ps[2 + half][0:16, :],
                AF.Sqrt, bias=t_cvec[0:16, 1:2], scale=t_cvec[0:16, 0:1]),
                reads=[k_ps[2 + half], k_cvec], writes=[k_rstd])
        b.emit("dve", lambda v: v.reciprocal(t_rstd[:, :], t_rstd[:, :]),
               reads=[], writes=[k_rstd])
        b.emit("act", lambda s: s.copy(t_rstdb[:, :], t_rstd[:, :]),
               reads=[k_rstd], writes=[k_rstdb])
        tap("rstd", lambda: t_rstd[:, :], [k_rstd], [16, TPC], F32)
        for h in range(HEADS):
            for half in range(2):
                bank = half
                b.emit("pe", lambda t, h=h, half=half, bank=bank: t.matmul(
                    ps[bank][:, :], t_rsel[:, h * 128:(h + 1) * 128],
                    t_rstdb[:, half * 512:half * 512 + 512],
                    start=True, stop=True),
                    reads=[k_rstdb, k_consts], writes=[k_ps[bank]])
                b.emit("dve", lambda v, h=h, half=half, bank=bank:
                       v.scalar_tensor_tensor(
                           t_obuf[:, h * TPC + half * 512: h * TPC + half * 512 + 512],
                           t_obuf[:, h * TPC + half * 512: h * TPC + half * 512 + 512],
                           t_nwcol[:, h:h + 1], ps[bank][:, :],
                           ALU.mult, ALU.mult),
                       reads=[k_ps[bank], k_consts], writes=[k_obuf[h]])
        tap("o_n", lambda: t_obuf[:, :], k_obuf, [128, HEADS * TPC], BF16)
        esC.close()
        b.barrier()

        b.barrier()
        t_f = sb("f_acc", [128, KC * TPC], F32, side="right")
        k_f = T()
        b.emit("sync", lambda g: g.dma_start(
            out=t_f.ap().rearrange("p (kc t) -> p kc t", kc=KC),
            in_=d_xt.ap().rearrange("(kc p) t -> p kc t", p=128)),
            writes=[k_f], dma=True, dkey="xf")
        for m in range(KC):
            sg = m % 2
            b.emit("pool", lambda g, m=m, sg=sg: g.dma_start(
                out=t_stg[sg][:, :EMBED], in_=d_wo.ap()[m, :, :]),
                writes=[k_stg[sg]], dma=True, dkey=f"stg{sg}")
            for half in range(2):
                bank = half
                mm_sweep(bank,
                         lambda kci, sg=sg: t_stg[sg][:, kci * 128:(kci + 1) * 128],
                         lambda kci, half=half: t_obuf[:, kci * TPC + half * 512:
                                                       kci * TPC + half * 512 + 512],
                         KC, tuple([k_stg[sg]] + k_obuf))
                b.emit("dve", lambda v, m=m, half=half, bank=bank: v.tensor_tensor(
                    t_f[:, m * TPC + half * 512: m * TPC + half * 512 + 512],
                    t_f[:, m * TPC + half * 512: m * TPC + half * 512 + 512],
                    ps[bank][:, :], ALU.add),
                    reads=[k_ps[bank]], writes=[k_f])
        tap("r1", lambda: t_f[:, :], [k_f], [128, KC * TPC], F32)
        esA.close()

        def layer_norm(t_x, k_x, sq_scope_name):
            esL = ExitStack()
            t_sqf = esL.enter_context(nc.sbuf_tensor(
                f"sqf_{sq_scope_name}", [128, TPC], F32))
            t_st = esL.enter_context(nc.sbuf_tensor(
                f"st_{sq_scope_name}", [1, 4 * TPC], F32))
            k_sqf, k_st = T(), T()
            for kci in range(KC):
                b.emit("act", lambda s, kci=kci: s.activation(
                    t_sqf[:, :], t_x[:, kci * TPC:(kci + 1) * TPC], AF.Square),
                    reads=[k_x], writes=[k_sqf])
                for half in range(2):
                    b.emit("pe", lambda t, kci=kci, half=half: t.matmul(
                        ps[4 + half][0:1, :], t_onesf[:, :],
                        t_x[:, kci * TPC + half * 512: kci * TPC + half * 512 + 512],
                        start=(kci == 0), stop=(kci == KC - 1)),
                        reads=[k_x, k_consts], writes=[k_ps[4 + half]])
                    b.emit("pe", lambda t, kci=kci, half=half: t.matmul(
                        ps[2 + half][0:1, :], t_onesf[:, :],
                        t_sqf[:, half * 512:half * 512 + 512],
                        start=(kci == 0), stop=(kci == KC - 1)),
                        reads=[k_sqf, k_consts], writes=[k_ps[2 + half]])
            for half in range(2):
                b.emit("dve", lambda v, half=half: v.tensor_scalar(
                    t_st[0:1, half * 512:half * 512 + 512],
                    ps[4 + half][0:1, :], t_cvec[0:1, 2:3], None, ALU.mult),
                    reads=[k_ps[4 + half], k_cvec], writes=[k_st])
                b.emit("dve", lambda v, half=half: v.tensor_scalar(
                    t_st[0:1, TPC + half * 512: TPC + half * 512 + 512],
                    ps[2 + half][0:1, :], t_cvec[0:1, 2:3], None, ALU.mult),
                    reads=[k_ps[2 + half], k_cvec], writes=[k_st])
            b.emit("dve", lambda v: v.tensor_tensor(
                t_st[0:1, 2 * TPC:3 * TPC], t_st[0:1, 0:TPC], t_st[0:1, 0:TPC],
                ALU.mult), reads=[], writes=[k_st])
            b.emit("dve", lambda v: v.tensor_tensor(
                t_st[0:1, 2 * TPC:3 * TPC], t_st[0:1, TPC:2 * TPC],
                t_st[0:1, 2 * TPC:3 * TPC], ALU.subtract),
                reads=[], writes=[k_st])
            b.emit("act", lambda s: s.activation(
                t_st[0:1, 2 * TPC:3 * TPC], t_st[0:1, 2 * TPC:3 * TPC],
                AF.Sqrt, bias=t_cvec[0:1, 3:4]),
                reads=[k_cvec], writes=[k_st])
            b.emit("dve", lambda v: v.reciprocal(
                t_st[0:1, 2 * TPC:3 * TPC], t_st[0:1, 2 * TPC:3 * TPC]),
                reads=[], writes=[k_st])
            b.emit("dve", lambda v: v.tensor_tensor(
                t_st[0:1, 3 * TPC:4 * TPC], t_st[0:1, 0:TPC],
                t_st[0:1, 2 * TPC:3 * TPC], ALU.mult),
                reads=[], writes=[k_st])
            for half in range(2):
                b.emit("pe", lambda t, half=half: t.matmul(
                    ps[4 + half][:, :], t_onesrow[:, :],
                    t_st[0:1, 2 * TPC + half * 512: 2 * TPC + half * 512 + 512],
                    start=True, stop=True),
                    reads=[k_st, k_consts], writes=[k_ps[4 + half]])
                b.emit("pe", lambda t, half=half: t.matmul(
                    ps[2 + half][:, :], t_onesrow[:, :],
                    t_st[0:1, 3 * TPC + half * 512: 3 * TPC + half * 512 + 512],
                    start=True, stop=True),
                    reads=[k_st, k_consts], writes=[k_ps[2 + half]])
            for kci in range(KC):
                for half in range(2):
                    sl = lambda kci=kci, half=half: t_x[
                        :, kci * TPC + half * 512: kci * TPC + half * 512 + 512]
                    b.emit("dve", lambda v, sl=sl, half=half: v.tensor_tensor(
                        sl(), sl(), ps[4 + half][:, :], ALU.mult),
                        reads=[k_ps[4 + half]], writes=[k_x])
                    b.emit("dve", lambda v, sl=sl, half=half: v.tensor_tensor(
                        sl(), sl(), ps[2 + half][:, :], ALU.subtract),
                        reads=[k_ps[2 + half]], writes=[k_x])
            esL.close()

        b.barrier()
        layer_norm(t_f, k_f, "ln1")
        tap("h1", lambda: t_f[:, :], [k_f], [128, KC * TPC], F32)

        esF = ExitStack()
        t_h1b = esF.enter_context(nc.sbuf_tensor("h1b", [128, KC * TPC], BF16))
        t_lsb = esF.enter_context(nc.sbuf_tensor("lsb", [128, NE], F32))
        t_ysb = esF.enter_context(nc.sbuf_tensor("ysb", [128, NE], F32))
        t_msk = esF.enter_context(nc.sbuf_tensor("msk", [128, NE], F32))
        t_red = esF.enter_context(nc.sbuf_tensor("red", [128, 4], F32))
        t_ctm = esF.enter_context(nc.sbuf_tensor("ctm", [128, NE], BF16))
        t_cT = esF.enter_context(nc.sbuf_tensor("cT", [16, TPC], BF16))
        k_h1b, k_lsb, k_ysb, k_msk, k_red, k_ctm, k_cT = (
            T(), T(), T(), T(), T(), T(), T())
        b.emit("pool", lambda g: g.dma_start(out=t_h1b[:, :], in_=t_f[:, :]),
               reads=[k_f], writes=[k_h1b], dma=True, dkey="misc")

        tap("h1b", lambda: t_h1b[:, :], [k_h1b], [128, KC * TPC], BF16)
        for tc in range(NCH):
            fns = []
            for kci in range(KC):
                fns.append(lambda t, kci=kci, tc=tc: t.matmul(
                    ps[0][:, 0:NE],
                    t_f[:, kci * TPC + tc * 128: kci * TPC + tc * 128 + 128],
                    t_rwt[:, kci * NE:(kci + 1) * NE],
                    start=(kci == 0), stop=(kci == KC - 1)))
            b.emit("pe", None, fns=fns, reads=[k_f, k_rwt], writes=[k_ps[0]])
            b.emit("act", lambda s: s.copy(t_lsb[:, :], ps[0][:, 0:NE]),
                   reads=[k_ps[0]], writes=[k_lsb])
            b.emit("dve", lambda v: v.tensor_tensor(
                t_ysb[:, :], t_lsb[:, :], t_biasrep[:, :], ALU.add),
                reads=[k_lsb, k_consts], writes=[k_ysb])
            b.emit("dve", lambda v: v.reduce_max(
                t_red[:, 0:1], t_ysb[:, :], axis=AXX),
                reads=[k_ysb], writes=[k_red])
            b.emit("dve", lambda v: v.tensor_single_scalar(
                t_msk[:, :], t_ysb[:, :], t_red[:, 0:1], ALU.is_ge),
                reads=[k_ysb, k_red], writes=[k_msk])
            b.emit("dve", lambda v: v.scalar_tensor_tensor(
                t_ysb[:, :], t_msk[:, :], t_cvec[:, 4:5], t_ysb[:, :],
                ALU.mult, ALU.add),
                reads=[k_msk, k_cvec], writes=[k_ysb])
            b.emit("dve", lambda v: v.reduce_max(
                t_red[:, 1:2], t_ysb[:, :], axis=AXX),
                reads=[k_ysb], writes=[k_red])
            b.emit("dve", lambda v: v.tensor_single_scalar(
                t_ysb[:, :], t_ysb[:, :], t_red[:, 1:2], ALU.is_ge),
                reads=[k_red], writes=[k_ysb])
            b.emit("dve", lambda v: v.tensor_tensor(
                t_msk[:, :], t_msk[:, :], t_ysb[:, :], ALU.add),
                reads=[k_ysb], writes=[k_msk])
            b.emit("dve", lambda v: v.reduce_max(
                t_red[:, 2:3], t_lsb[:, :], axis=AXX),
                reads=[k_lsb], writes=[k_red])
            b.emit("dve", lambda v: v.tensor_single_scalar(
                t_lsb[:, :], t_lsb[:, :], t_red[:, 2:3], ALU.subtract),
                reads=[k_red], writes=[k_lsb])
            b.emit("act", lambda s: s.activation(
                t_lsb[:, :], t_lsb[:, :], AF.Exp),
                reads=[], writes=[k_lsb])
            b.emit("dve", lambda v: v.tensor_tensor(
                t_lsb[:, :], t_lsb[:, :], t_msk[:, :], ALU.mult),
                reads=[k_msk], writes=[k_lsb])
            b.emit("dve", lambda v: v.reduce_sum(
                t_red[:, 3:4], t_lsb[:, :], axis=AXX),
                reads=[k_lsb], writes=[k_red])
            b.emit("dve", lambda v: v.reciprocal(
                t_red[:, 3:4], t_red[:, 3:4]), reads=[], writes=[k_red])
            b.emit("dve", lambda v: v.tensor_scalar(
                t_ctm[:, :], t_lsb[:, :], t_red[:, 3:4], None, ALU.mult),
                reads=[k_lsb, k_red], writes=[k_ctm])
            if tc == 7:
                tap("rlogit", lambda: t_lsb[:, :], [k_lsb], [128, NE], F32)
                tap("rmsk", lambda: t_msk[:, :], [k_msk], [128, NE], F32)
                tap("rctm", lambda: t_ctm[:, :], [k_ctm], [128, NE], BF16)
            b.emit("pe", lambda t: t.transpose(
                psb[0:NE, 0:128], t_ctm[:, :], t_identb[:, :]),
                reads=[k_ctm, k_consts], writes=[k_psb])
            b.emit("act", lambda s, tc=tc: s.copy(
                t_cT[:, tc * 128:(tc + 1) * 128], psb[0:NE, 0:128]),
                reads=[k_psb], writes=[k_cT])

        tap("cT", lambda: t_cT[:, :], [k_cT], [16, TPC], BF16)

        b.barrier()
        t_usil = esF.enter_context(nc.sbuf_tensor("usil", [128, EKC * TPC], BF16))
        t_hb = [esF.enter_context(nc.sbuf_tensor(f"hb{i}", [128, EKC * TPC], BF16))
                for i in range(2)]
        t_cb = esF.enter_context(nc.sbuf_tensor("cb", [128, TPC], BF16))
        k_usil, k_cb = T(), T()
        k_hb = [T(), T()]

        for pr in range(NPAIR):
            b.barrier(2)
            for ei in range(2):
                e_idx = pr * 2 + ei
                for wgt, dram_w, is_u in ((0, d_wue, True), (1, d_wge, False)):
                    for mp in range(0, EKC, 2):
                        sg = (mp // 2) % 2
                        b.emit("pool", lambda g, e_idx=e_idx, mp=mp, sg=sg,
                               dram_w=dram_w: g.dma_start(
                                   out=t_stg[sg][:, 0:2 * EMBED].rearrange(
                                       "p (m f) -> p m f", m=2),
                                   in_=dram_w.ap()[e_idx, mp:mp + 2, :, :]
                                   .rearrange("m p f -> p m f")),
                               writes=[k_stg[sg]], dma=True, dkey=f"stg{sg}")
                        for mi in range(2):
                            m = mp + mi
                            for half in range(2):
                                bank = half
                                mm_sweep(
                                    bank,
                                    lambda kci, sg=sg, mi=mi:
                                    t_stg[sg][:, mi * EMBED + kci * 128:
                                              mi * EMBED + (kci + 1) * 128],
                                    lambda kci, half=half:
                                    t_h1b[:, kci * TPC + half * 512:
                                          kci * TPC + half * 512 + 512],
                                    KC, (k_stg[sg], k_h1b))
                                dst = lambda m=m, half=half: (
                                    m * TPC + half * 512, m * TPC + half * 512 + 512)
                                if is_u:
                                    b.emit("act", lambda s, d=dst(), bank=bank:
                                           s.activation(t_usil[:, d[0]:d[1]],
                                                        ps[bank][:, :], AF.Silu),
                                           reads=[k_ps[bank]], writes=[k_usil])
                                else:
                                    b.emit("dve", lambda v, d=dst(), bank=bank,
                                           ei=ei: v.tensor_tensor(
                                               t_hb[ei][:, d[0]:d[1]],
                                               ps[bank][:, :],
                                               t_usil[:, d[0]:d[1]], ALU.mult),
                                           reads=[k_ps[bank], k_usil],
                                           writes=[k_hb[ei]])
                if pr == 0 and ei == 0:
                    tap("usil0", lambda: t_usil[:, :], [k_usil],
                        [128, EKC * TPC], BF16)
                    tap("hb0raw", lambda: t_hb[0][:, :], [k_hb[0]],
                        [128, EKC * TPC], BF16)
                if e_idx < NE:
                    for half in range(2):
                        bank = 2 + half
                        b.emit("pe", lambda t, e_idx=e_idx, half=half, bank=bank:
                               t.matmul(ps[bank][:, :],
                                        t_rsel[:, e_idx * 128:(e_idx + 1) * 128],
                                        t_cT[:, half * 512:half * 512 + 512],
                                        start=True, stop=True),
                               reads=[k_cT, k_consts], writes=[k_ps[bank]])
                        b.emit("act", lambda s, half=half, bank=bank: s.copy(
                            t_cb[:, half * 512:half * 512 + 512],
                            ps[bank][:, :]),
                            reads=[k_ps[bank]], writes=[k_cb])
                    for m in range(EKC):
                        b.emit("dve", lambda v, m=m, ei=ei: v.tensor_tensor(
                            t_hb[ei][:, m * TPC:(m + 1) * TPC],
                            t_hb[ei][:, m * TPC:(m + 1) * TPC],
                            t_cb[:, :], ALU.mult),
                            reads=[k_cb], writes=[k_hb[ei]])
            if pr == 0:
                tap("hb0", lambda: t_hb[0][:, :], [k_hb[0]],
                    [128, EKC * TPC], BF16)
            for mp in range(0, 16, 2):
                sg = (mp // 2) % 2
                b.emit("pool", lambda g, pr=pr, mp=mp, sg=sg: g.dma_start(
                    out=t_stg[sg][:, 0:4096].rearrange("p (m i f) -> p m i f",
                                                       m=2, i=2),
                    in_=d_wdp.ap()[pr, mp:mp + 2, :, :, :]
                    .rearrange("m i p f -> p m i f")),
                    writes=[k_stg[sg]], dma=True, dkey=f"stg{sg}")
                for mi in range(2):
                    m = mp + mi
                    for half in range(2):
                        bank = half
                        fns = []
                        for ei in range(2):
                            for ke in range(EKC):
                                fns.append(lambda t, sg=sg, mi=mi, ei=ei, ke=ke,
                                           half=half, bank=bank: t.matmul(
                                               ps[bank][:, :],
                                               t_stg[sg][:, (mi * 2 + ei) * 1024 +
                                                         ke * 128:
                                                         (mi * 2 + ei) * 1024 +
                                                         (ke + 1) * 128],
                                               t_hb[ei][:, ke * TPC + half * 512:
                                                        ke * TPC + half * 512 + 512],
                                               start=(ei == 0 and ke == 0),
                                               stop=(ei == 1 and ke == EKC - 1)))
                        b.emit("pe", None, fns=fns,
                               reads=[k_stg[sg], k_hb[0], k_hb[1]],
                               writes=[k_ps[bank]])
                        b.emit("dve", lambda v, m=m, half=half, bank=bank:
                               v.tensor_tensor(
                                   t_f[:, m * TPC + half * 512:
                                       m * TPC + half * 512 + 512],
                                   t_f[:, m * TPC + half * 512:
                                       m * TPC + half * 512 + 512],
                                   ps[bank][:, :], ALU.add),
                               reads=[k_ps[bank]], writes=[k_f])
        tap("f", lambda: t_f[:, :], [k_f], [128, KC * TPC], F32)
        esF.close()

        b.barrier()
        layer_norm(t_f, k_f, "ln2")

        out_tok = b.emit("sync", lambda g: g.dma_start(
            out=d_out.ap().rearrange("(kc p) t -> p kc t", p=128),
            in_=t_f.ap().rearrange("p (kc t) -> p kc t", kc=KC)),
            reads=[k_f], dma=True, dkey="out")
        b.raw("sync", lambda g, s=out_tok[0], v=out_tok[1]: g.wait_ge(s, v))

        with nc.Block() as block:
            b.run_blocks(block)
    return nc


_NC_CACHE = None
_TAP_RESULTS = {}


def kernel(**inputs):
    global _NC_CACHE
    x = np.asarray(inputs["L0_attn_x"], np.float32)
    shared = host_weights(inputs)
    if _NC_CACHE is None:
        _NC_CACHE = build_program()
    nc = _NC_CACHE
    in_maps = []
    for c in range(NCORES):
        bi, cb = c // 4, c % 4
        xt = np.ascontiguousarray(x[bi, cb * TPC:(cb + 1) * TPC, :].T)
        m = dict(shared)
        m["xt"] = xt
        m.update(host_constants(c))
        in_maps.append(m)
    res = run_bass_kernel_spmd(nc, in_maps, core_ids=list(range(NCORES)))
    outs = []
    for c in range(NCORES):
        outs.append(res.results[c]["out_t"].T)
        for kk, vv in res.results[c].items():
            if kk.startswith("tap_"):
                _TAP_RESULTS.setdefault(kk, []).append(np.asarray(vv))
    full = np.concatenate(outs, axis=0).reshape(2, 4096, EMBED)
    return full.astype(np.float32)


# version 11
# speedup vs baseline: 1.2078x; 1.0007x over previous
import numpy as np
import ml_dtypes
import concourse.bass as bass
import concourse.mybir as mybir
from concourse.bass_utils import run_bass_kernel_spmd
from contextlib import ExitStack

F32 = mybir.dt.float32
F32R = mybir.dt.float32r
BF16 = mybir.dt.bfloat16
I32 = mybir.dt.int32
AF = mybir.ActivationFunctionType
ALU = mybir.AluOpType
AXX = mybir.AxisListType.X

EMBED = 2048
HEADS = 16
HD = 128
C = 128
KC = 16
TPC = 1024
NCH = 8
NCORES = 8
NE = 16
NEX = 18
NPAIR = 9
EKC = 8
EPS_G = 1e-6
EPS_LN = 1e-5
BASE = 10000.0

_DEBUG_TAPS = set()
_PARANOID = 1

BF = ml_dtypes.bfloat16


def _log_lam():
    h = np.arange(HEADS, dtype=np.float64)
    return -(2.0 ** (-8.0 * (h + 1.0) / HEADS))


def host_constants(core):
    ll = _log_lam()
    cb = core % 4
    t0 = cb * TPC
    half = HD // 2
    theta = BASE ** (-2.0 * np.arange(half, dtype=np.float64) / HD)
    pos = np.arange(t0, t0 + TPC, dtype=np.float64)
    ang = pos[None, :] * np.concatenate([theta, theta])[:, None]
    cosb = np.cos(ang).astype(BF)
    sinb = np.sin(ang).astype(BF)

    i = np.arange(C, dtype=np.float64)
    diff = i[:, None] - i[None, :]
    dmask = np.where(diff[None] >= 0, np.exp(ll[:, None, None] * diff[None]), 0.0)
    dt_c = np.transpose(dmask, (0, 2, 1)).transpose(1, 0, 2).reshape(C, HEADS * C)
    dt_c = np.ascontiguousarray(dt_c).astype(BF)
    dq = np.exp(ll[:, None] * (i[None, :] + 1.0))
    dqr = np.broadcast_to(dq[None], (C, HEADS, C)).reshape(C, HEADS * C)
    dqr = np.ascontiguousarray(dqr).astype(BF)
    dkcol = np.exp(ll[None, :] * (C - 1.0 - i[:, None])).astype(np.float32)
    lam_C = np.exp(ll * C)
    lamcol = np.broadcast_to(lam_C[None, :], (C, HEADS)).astype(np.float32)
    lamcol = np.ascontiguousarray(lamcol)
    lj = lam_C[:, None] ** np.arange(NCH)[None, :]
    lj = np.where(lj < 1e-30, 0.0, lj)
    lamj = np.broadcast_to(lj.reshape(1, -1), (C, HEADS * NCH)).astype(np.float32)
    lamj = np.ascontiguousarray(lamj)

    identb = np.eye(128, dtype=BF)
    perm = np.zeros((128, 128), np.float32)
    for m in range(64):
        perm[m + 64, m] = -1.0
        perm[m, m + 64] = 1.0
    permb = perm.astype(BF)
    ohb = np.zeros((128, HEADS * 16), np.float32)
    for h in range(HEADS):
        ohb[:, h * 16 + h] = 1.0
    ohb = ohb.astype(BF)
    rsel = np.zeros((16, HEADS * 128), np.float32)
    for h in range(HEADS):
        rsel[h, h * 128:(h + 1) * 128] = 1.0
    rsel = rsel.astype(BF)
    onesf = np.ones((128, 1), np.float32)
    onesq = np.ones((128, 128), np.float32)
    onesrow = np.ones((1, 128), np.float32)
    cvec = np.zeros((128, 8), np.float32)
    cvec[:, 0] = 1.0 / HD
    cvec[:, 1] = EPS_G
    cvec[:, 2] = 1.0 / EMBED
    cvec[:, 3] = EPS_LN
    cvec[:, 4] = -1e9
    cvec[:, 5] = 1.0
    lam_B = np.exp(ll * C * NCH)
    wcol3 = np.zeros((128, 3 * HEADS), np.float32)
    poff = np.zeros((1, 4), np.int32)
    for i3 in range(3):
        src = cb - 1 - i3
        r = (core - 1 - i3) % NCORES
        poff[0, i3] = r * EMBED * 128
        if src >= 0:
            w = lam_B ** i3
            w = np.where(w < 1e-30, 0.0, w)
            wcol3[:, i3 * HEADS:(i3 + 1) * HEADS] = w[None, :].astype(np.float32)
    return dict(cosb=cosb, sinb=sinb, dt_c=dt_c, dqr=dqr, dkcol=dkcol,
                lamcol=lamcol, lamj=lamj, identb=identb, permb=permb, ohb=ohb,
                rsel=rsel, onesf=onesf, onesq=onesq, onesrow=onesrow, cvec=cvec,
                wcol3=wcol3, poff=poff)


def host_weights(inp):
    def mpanels(W, n_in_kc, n_m):
        return np.ascontiguousarray(
            W.reshape(n_in_kc, 128, n_m, 128).transpose(2, 1, 0, 3)
            .reshape(n_m, 128, n_in_kc * 128))

    out = {}
    out["wq"] = mpanels(np.asarray(inp["L0_attn_Wq"], np.float32), KC, 16)
    out["wk"] = mpanels(np.asarray(inp["L0_attn_Wk"], np.float32), KC, 16)
    out["wo"] = mpanels(np.asarray(inp["L0_attn_Wo"], np.float32), KC, 16)
    wv = np.asarray(inp["L0_attn_Wv"], np.float32)
    out["wv"] = np.ascontiguousarray(
        wv.reshape(EMBED, 4, 512).transpose(1, 0, 2))
    out["wg1"] = mpanels(np.asarray(inp["L0_attn_Wg1"], np.float32), KC, 1)[0]
    out["wg2"] = np.asarray(inp["L0_attn_Wg2"], np.float32)
    out["rwt"] = np.ascontiguousarray(
        np.asarray(inp["L0_ffn_router_w"], np.float32).T.reshape(KC, 128, NE))
    out["biasrep"] = np.ascontiguousarray(np.broadcast_to(
        np.asarray(inp["L0_ffn_router_bias"], np.float32)[None, :], (128, NE)))
    out["nwcol"] = np.ascontiguousarray(
        np.asarray(inp["L0_attn_norm_weight"], np.float32).reshape(HEADS, 128).T)

    wu = np.asarray(inp["L0_ffn_Wu"], np.float32)
    wg = np.asarray(inp["L0_ffn_Wg"], np.float32)
    su = np.asarray(inp["L0_ffn_Su"], np.float32)[0]
    sg = np.asarray(inp["L0_ffn_Sg"], np.float32)[0]
    def upanels(W):
        return mpanels(W, KC, EKC)
    wue = np.empty((NEX, EKC, 128, EMBED), np.float32)
    wge = np.empty((NEX, EKC, 128, EMBED), np.float32)
    for e in range(NE):
        wue[e] = upanels(wu[e])
        wge[e] = upanels(wg[e])
    wue[16] = upanels(su[:, :1024]); wue[17] = upanels(su[:, 1024:])
    wge[16] = upanels(sg[:, :1024]); wge[17] = upanels(sg[:, 1024:])
    out["wue"] = np.ascontiguousarray(wue)
    out["wge"] = np.ascontiguousarray(wge)

    wd = np.asarray(inp["L0_ffn_Wd"], np.float32)
    sd = np.asarray(inp["L0_ffn_Sd"], np.float32)[0]
    def dpanels(W):
        return np.ascontiguousarray(
            W.reshape(EKC, 128, 16, 128).transpose(2, 1, 0, 3)
            .reshape(16, 128, EKC * 128))
    wdx = np.empty((NEX, 16, 128, EKC * 128), np.float32)
    for e in range(NE):
        wdx[e] = dpanels(wd[e])
    wdx[16] = dpanels(sd[:1024, :])
    wdx[17] = dpanels(sd[1024:, :])
    out["wdp"] = np.ascontiguousarray(
        wdx.reshape(NPAIR, 2, 16, 128, EKC * 128).transpose(0, 2, 1, 3, 4))
    return out


class Trk:
    __slots__ = ("w", "rs")

    def __init__(self):
        self.w = None
        self.rs = []


class B:

    ENGS = ("pe", "dve", "act", "pool", "sync")

    def __init__(self, nc):
        self.nc = nc
        self.eng = {}
        self.sems = {}

    DKEYS = {"pool": ("stg0", "stg1", "xbf", "g", "misc"),
             "sync": ("main", "rwt", "cos", "sin", "dqr", "dtm",
                      "ccin", "xf", "out", "g")}

    def init_sems(self, es):
        for name in self.ENGS:
            sem = es.enter_context(self.nc.semaphore(f"s_{name}"))
            d = dict(sem=sem, cnt=0, waited={}, script=[], dsems={})
            for key in self.DKEYS.get(name, ()):
                d["dsems"][key] = [
                    es.enter_context(self.nc.semaphore(f"s_{name}_{key}")), 0]
            self.eng[name] = d
        self.cc_sem = es.enter_context(self.nc.semaphore("s_cc"))

    def _waits(self, e, deps, skip_self=False):
        need = {}
        for (sem, val) in deps:
            if skip_self and sem is e["sem"]:
                continue
            if need.get(sem, 0) < val:
                need[sem] = val
        for sem, val in sorted(need.items(), key=lambda kv: str(kv[0].name)):
            if e["waited"].get(sem, 0) < val:
                e["waited"][sem] = val
                e["script"].append(
                    lambda eng, s=sem, v=val: eng.wait_ge(s, v))

    def emit(self, en, fn, reads=(), writes=(), dma=False, fns=None,
             dkey="main"):
        e = self.eng[en]
        deps = []
        for t in reads:
            if t.w:
                deps.append(t.w)
        for t in writes:
            if t.w:
                deps.append(t.w)
            deps.extend(t.rs)
        self._waits(e, deps, skip_self=(en == "pe"))
        if dma:
            ds = e["dsems"][dkey]
            ds[1] += 16
            sem, cnt = ds[0], ds[1]
            amt = 16
        else:
            e["cnt"] += 1
            sem, cnt = e["sem"], e["cnt"]
            amt = 1
        if fns is not None:
            for f in fns[:-1]:
                e["script"].append(lambda eng, f=f: f(eng))
            e["script"].append(
                lambda eng, f=fns[-1], s=sem, a=amt: f(eng).then_inc(s, a))
        else:
            e["script"].append(
                lambda eng, f=fn, s=sem, a=amt: f(eng).then_inc(s, a))
        tok = (sem, cnt)
        for t in reads:
            t.rs.append(tok)
        for t in writes:
            t.w = tok
            t.rs = []
        return tok

    def raw(self, en, fn):
        self.eng[en]["script"].append(fn)

    def barrier(self, level=1):
        import kernel as _k
        if _PARANOID < level:
            return
        sems = []
        for name in self.ENGS:
            e = self.eng[name]
            if e["cnt"]:
                sems.append((e["sem"], e["cnt"]))
            for key, sc in e["dsems"].items():
                if sc[1]:
                    sems.append((sc[0], sc[1]))
        for name in self.ENGS:
            e = self.eng[name]
            self._waits(e, [t for t in sems if t[0] is not e["sem"]])

    def run_blocks(self, block):
        block.tensor(lambda eng: [f(eng) for f in self.eng["pe"]["script"]])
        block.vector(lambda eng: [f(eng) for f in self.eng["dve"]["script"]])
        block.scalar(lambda eng: [f(eng) for f in self.eng["act"]["script"]])
        block.gpsimd(lambda eng: [f(eng) for f in self.eng["pool"]["script"]])
        block.sync(lambda eng: [f(eng) for f in self.eng["sync"]["script"]])


def build_program():
    nc = bass.Bass("TRN2", target_bir_lowering=False)
    ll = _log_lam()
    lam_C = np.exp(ll * C)

    d_xt = nc.dram_tensor("xt", [EMBED, TPC], F32, kind="ExternalInput")
    d_wq = nc.dram_tensor("wq", [16, 128, EMBED], F32, kind="ExternalInput")
    d_wk = nc.dram_tensor("wk", [16, 128, EMBED], F32, kind="ExternalInput")
    d_wo = nc.dram_tensor("wo", [16, 128, EMBED], F32, kind="ExternalInput")
    d_wv = nc.dram_tensor("wv", [4, EMBED, 512], F32, kind="ExternalInput")
    d_wg1 = nc.dram_tensor("wg1", [128, EMBED], F32, kind="ExternalInput")
    d_wg2 = nc.dram_tensor("wg2", [128, EMBED], F32, kind="ExternalInput")
    d_rwt = nc.dram_tensor("rwt", [KC, 128, NE], F32, kind="ExternalInput")
    d_biasrep = nc.dram_tensor("biasrep", [128, NE], F32, kind="ExternalInput")
    d_nwcol = nc.dram_tensor("nwcol", [128, HEADS], F32, kind="ExternalInput")
    d_wue = nc.dram_tensor("wue", [NEX, EKC, 128, EMBED], F32, kind="ExternalInput")
    d_wge = nc.dram_tensor("wge", [NEX, EKC, 128, EMBED], F32, kind="ExternalInput")
    d_wdp = nc.dram_tensor("wdp", [NPAIR, 16, 2, 128, EKC * 128], F32,
                           kind="ExternalInput")
    d_cosb = nc.dram_tensor("cosb", [128, TPC], BF16, kind="ExternalInput")
    d_sinb = nc.dram_tensor("sinb", [128, TPC], BF16, kind="ExternalInput")
    d_dt = nc.dram_tensor("dt_c", [128, HEADS * C], BF16, kind="ExternalInput")
    d_dqr = nc.dram_tensor("dqr", [128, HEADS * C], BF16, kind="ExternalInput")
    d_dkcol = nc.dram_tensor("dkcol", [128, HEADS], F32, kind="ExternalInput")
    d_lamcol = nc.dram_tensor("lamcol", [128, HEADS], F32, kind="ExternalInput")
    d_lamj = nc.dram_tensor("lamj", [128, HEADS * NCH], F32, kind="ExternalInput")
    d_identb = nc.dram_tensor("identb", [128, 128], BF16, kind="ExternalInput")
    d_permb = nc.dram_tensor("permb", [128, 128], BF16, kind="ExternalInput")
    d_ohb = nc.dram_tensor("ohb", [128, HEADS * 16], BF16, kind="ExternalInput")
    d_rsel = nc.dram_tensor("rsel", [16, HEADS * 128], BF16, kind="ExternalInput")
    d_onesf = nc.dram_tensor("onesf", [128, 1], F32, kind="ExternalInput")
    d_onesq = nc.dram_tensor("onesq", [128, 128], F32, kind="ExternalInput")
    d_onesrow = nc.dram_tensor("onesrow", [1, 128], F32, kind="ExternalInput")
    d_cvec = nc.dram_tensor("cvec", [128, 8], F32, kind="ExternalInput")
    d_wcol3 = nc.dram_tensor("wcol3", [128, 3 * HEADS], F32, kind="ExternalInput")
    d_poff = nc.dram_tensor("poff", [1, 4], I32, kind="ExternalInput")
    d_out = nc.dram_tensor("out_t", [EMBED, TPC], F32, kind="ExternalOutput")
    d_ccin = nc.dram_tensor("cc_in", [EMBED, 128], F32)
    d_ccout = nc.dram_tensor("cc_out", [NCORES * EMBED, 128], F32)

    b = B(nc)

    def tap(name, ap_fn, reads, shape, dtype):
        if name not in _DEBUG_TAPS:
            return
        d = nc.dram_tensor("tap_" + name, shape, dtype, kind="ExternalOutput")
        b.emit("sync", lambda g, d=d, a=ap_fn(): g.dma_start(out=d.ap(), in_=a),
               reads=list(reads), dma=True, dkey="out")
    es = ExitStack()
    with es:
        b.init_sems(es)
        sb = lambda name, shape, dt, **kw: es.enter_context(
            nc.sbuf_tensor(name, shape, dt, **kw))
        t_cvec = sb("cvec_s", [128, 8], F32)
        t_identb = sb("identb_s", [128, 128], BF16)
        t_permb = sb("permb_s", [128, 128], BF16)
        t_ohb = sb("ohb_s", [128, HEADS * 16], BF16)
        t_rsel = sb("rsel_s", [16, HEADS * 128], BF16)
        t_onesf = sb("onesf_s", [128, 1], F32)
        t_onesq = sb("onesq_s", [128, 128], F32)
        t_onesrow = sb("onesrow_s", [1, 128], F32)
        t_biasrep = sb("biasrep_s", [128, NE], F32)
        t_nwcol = sb("nwcol_s", [128, HEADS], F32)
        t_dkcol = sb("dkcol_s", [128, HEADS], F32)
        t_lamcol = sb("lamcol_s", [128, HEADS], F32)
        t_lamj = sb("lamj_s", [128, HEADS * NCH], F32)
        t_wcol3 = sb("wcol3_s", [128, 3 * HEADS], F32)
        t_poff = sb("poff_s", [1, 4], I32)
        t_rwt = sb("rwt_s", [128, KC * NE], F32)
        t_g1 = sb("g1_s", [128, TPC], BF16)
        t_stg = [sb(f"stg{i}", [128, 4096], BF16) for i in range(2)]
        ps = [es.enter_context(nc.psum_tensor(f"ps{i}", [128, 512], F32))
              for i in range(6)]
        psb2 = [es.enter_context(nc.psum_tensor(f"psb{i}", [128, 1024], BF16))
                for i in range(2)]
        psb = psb2[0]

        T = lambda: Trk()
        k_cvec, k_consts = T(), T()
        k_rwt, k_g1 = T(), T()
        k_stg = [T(), T()]
        k_ps = [T() for _ in range(6)]
        k_psb2 = [T(), T()]
        k_psb = k_psb2[0]

        def ld(dst_ap, src_ap, trk, queue="sync", dkey="main"):
            b.emit(queue, lambda g, d=dst_ap, s=src_ap: g.dma_start(out=d, in_=s),
                   writes=[trk], dma=True, dkey=dkey)

        ld(t_cvec[:, :], d_cvec.ap(), k_cvec)
        for dst, src in ((t_identb, d_identb), (t_permb, d_permb),
                         (t_ohb, d_ohb), (t_rsel, d_rsel), (t_onesf, d_onesf),
                         (t_onesq, d_onesq),
                         (t_onesrow, d_onesrow), (t_biasrep, d_biasrep),
                         (t_nwcol, d_nwcol), (t_dkcol, d_dkcol),
                         (t_lamcol, d_lamcol), (t_lamj, d_lamj),
                         (t_wcol3, d_wcol3), (t_poff, d_poff)):
            ld(dst[:, :], src.ap(), k_consts)
        ld(t_rwt.ap().rearrange("p (kc n) -> p kc n", kc=KC),
           d_rwt.ap().rearrange("kc p n -> p kc n"), k_rwt, dkey="rwt")

        b.barrier()
        esA = ExitStack()
        t_obuf = esA.enter_context(nc.sbuf_tensor("obuf", [128, HEADS * TPC], BF16))
        t_qbuf = esA.enter_context(nc.sbuf_tensor("qbuf", [128, HEADS * TPC], BF16))
        t_S = esA.enter_context(nc.sbuf_tensor("S_f", [128, HEADS * HD], F32))
        t_Sb = esA.enter_context(nc.sbuf_tensor("S_b", [128, HEADS * HD], BF16))
        k_obuf = [T() for _ in range(HEADS)]
        k_qbuf = [T() for _ in range(HEADS)]
        k_S = [T() for _ in range(HEADS)]
        k_Sb = [T() for _ in range(HEADS)]
        k_dqr, k_dt = T(), T()

        esA2 = ExitStack()
        t_kfm = esA2.enter_context(nc.sbuf_tensor("kfm", [128, HEADS * TPC], BF16))
        t_scb = esA2.enter_context(nc.sbuf_tensor("scb", [128, C], BF16))
        t_ktb = esA2.enter_context(nc.sbuf_tensor("ktb", [128, C], BF16))
        esX = ExitStack()
        t_xbf = esX.enter_context(nc.sbuf_tensor("xbf", [128, KC * TPC], BF16,
                                                 side="right"))
        k_xbf = T()
        k_kfm = [T() for _ in range(HEADS)]
        k_vtm = [T() for _ in range(NCH * 4)]
        k_scb, k_ktb = T(), T()

        ld(t_xbf.ap().rearrange("p (kc t) -> p kc t", kc=KC),
           d_xt.ap().rearrange("(kc p) t -> p kc t", p=128), k_xbf, queue="pool",
           dkey="xbf")

        esR = ExitStack()
        t_cos = esR.enter_context(nc.sbuf_tensor("cosb_s", [128, TPC], BF16, side="right"))
        t_sin = esR.enter_context(nc.sbuf_tensor("sinb_s", [128, TPC], BF16, side="right"))
        t_qsil = esR.enter_context(nc.sbuf_tensor("qsil", [128, TPC], BF16, side="right"))
        t_rt1 = esR.enter_context(nc.sbuf_tensor("rt1", [128, TPC], BF16, side="right"))
        t_rt2 = esR.enter_context(nc.sbuf_tensor("rt2", [128, TPC], BF16, side="right"))
        k_cos, k_sin, k_qsil, k_rt1, k_rt2 = T(), T(), T(), T(), T()
        ld(t_cos[:, :], d_cosb.ap(), k_cos, dkey="cos")
        ld(t_sin[:, :], d_sinb.ap(), k_sin, dkey="sin")

        def mm_sweep(bank, lhs_fn, rhs_fn, n_k, reads):
            fns = []
            for kci in range(n_k):
                fns.append(lambda t, l=lhs_fn(kci), r=rhs_fn(kci),
                           st=(kci == 0), sp=(kci == n_k - 1), bk=bank:
                           t.matmul(ps[bk][:, :], l, r, start=st, stop=sp))
            return b.emit("pe", None, fns=fns, reads=list(reads),
                          writes=[k_ps[bank]])

        def qk_phase(dram_w, dest, k_dest):
            for m in range(HEADS):
                sg = (m // 2) % 2
                moff = (m % 2) * EMBED
                if m % 2 == 0:
                    b.emit("pool", lambda g, m=m, sg=sg: g.dma_start(
                        out=t_stg[sg][:, 0:2 * EMBED].rearrange(
                            "p (m f) -> p m f", m=2),
                        in_=dram_w.ap()[m:m + 2, :, :].rearrange(
                            "m p f -> p m f")),
                        writes=[k_stg[sg]], dma=True, dkey=f"stg{sg}")
                for half in range(2):
                    bank = half
                    mm_sweep(bank,
                             lambda kci, sg=sg, moff=moff:
                             t_stg[sg][:, moff + kci * 128: moff + (kci + 1) * 128],
                             lambda kci, half=half: t_xbf[:, kci * TPC + half * 512:
                                                          kci * TPC + half * 512 + 512],
                             KC, (k_stg[sg], k_xbf))
                    b.emit("act", lambda s, bank=bank, half=half: s.activation(
                        t_qsil[:, half * 512:half * 512 + 512], ps[bank][:, :],
                        AF.Silu), reads=[k_ps[bank]], writes=[k_qsil])
                    b.emit("pe", lambda t, half=half: t.matmul(
                        ps[2 + half][:, :], t_permb[:, :],
                        t_qsil[:, half * 512:half * 512 + 512],
                        start=True, stop=True),
                        reads=[k_qsil, k_consts], writes=[k_ps[2 + half]])
                b.emit("dve", lambda v, m=m: v.tensor_tensor(
                    t_rt1[:, :], t_qsil[:, :], t_cos[:, :], ALU.mult),
                    reads=[k_qsil, k_cos], writes=[k_rt1])
                for half in range(2):
                    b.emit("dve", lambda v, half=half: v.tensor_tensor(
                        t_rt2[:, half * 512:half * 512 + 512],
                        ps[2 + half][:, :],
                        t_sin[:, half * 512:half * 512 + 512], ALU.mult),
                        reads=[k_ps[2 + half], k_sin], writes=[k_rt2])
                b.emit("dve", lambda v, m=m: v.tensor_tensor(
                    dest[:, m * TPC:(m + 1) * TPC], t_rt1[:, :], t_rt2[:, :],
                    ALU.add), reads=[k_rt1, k_rt2], writes=[k_dest[m]])

        qk_phase(d_wq, t_qbuf, k_qbuf)
        b.barrier()
        qk_phase(d_wk, t_kfm, k_kfm)
        b.barrier()
        esR.close()
        tap("qk_q", lambda: t_qbuf[:, :], k_qbuf, [128, HEADS * TPC], BF16)
        tap("qk_k", lambda: t_kfm[:, :], k_kfm, [128, HEADS * TPC], BF16)

        b.emit("pool", lambda g: g.dma_start(
            out=t_stg[0][:, :EMBED], in_=d_wg1.ap()), writes=[k_stg[0]],
            dma=True, dkey="stg0")
        for half in range(2):
            mm_sweep(half,
                     lambda kci: t_stg[0][:, kci * 128:(kci + 1) * 128],
                     lambda kci, half=half: t_xbf[:, kci * TPC + half * 512:
                                                  kci * TPC + half * 512 + 512],
                     KC, (k_stg[0], k_xbf))
            b.emit("act", lambda s, half=half: s.copy(
                t_g1[:, half * 512:half * 512 + 512], ps[half][:, :]),
                reads=[k_ps[half]], writes=[k_g1])

        tap("g1", lambda: t_g1[:, :], [k_g1], [128, TPC], BF16)
        esV = ExitStack()
        t_vtm = esV.enter_context(nc.sbuf_tensor("vtm", [128, NCH * HEADS * HD],
                                                 BF16))
        for nf in range(4):
            for kh in range(2):
                b.emit("pool", lambda g, nf=nf, kh=kh: g.dma_start(
                    out=t_stg[kh].ap().rearrange("p (kc f) -> p kc f", kc=8),
                    in_=d_wv.ap()[nf, kh * 1024:(kh + 1) * 1024, :]
                    .rearrange("(kc p) f -> p kc f", p=128)),
                    writes=[k_stg[kh]], dma=True, dkey=f"stg{kh}")
            for tc in range(NCH):
                bank = tc % 2
                fns = []
                for kci in range(KC):
                    kh, kloc = kci // 8, kci % 8
                    fns.append(lambda t, kci=kci, kh=kh, kloc=kloc, tc=tc,
                               bank=bank: t.matmul(
                                   ps[bank][:, :],
                                   t_xbf[:, kci * TPC + tc * 128:
                                         kci * TPC + tc * 128 + 128],
                                   t_stg[kh][:, kloc * 512:(kloc + 1) * 512],
                                   start=(kci == 0), stop=(kci == KC - 1)))
                b.emit("pe", None, fns=fns,
                       reads=[k_stg[0], k_stg[1], k_xbf], writes=[k_ps[bank]])
                b.emit("act", lambda s, tc=tc, nf=nf, bank=bank: s.copy(
                    t_vtm[:, (tc * HEADS + nf * 4) * HD:
                          (tc * HEADS + nf * 4) * HD + 512], ps[bank][:, :]),
                    reads=[k_ps[bank]], writes=[k_vtm[tc * 4 + nf]])
        esX.close()
        t_dqr = esV.enter_context(nc.sbuf_tensor("dqr_s", [128, HEADS * C], BF16))
        t_dt = esV.enter_context(nc.sbuf_tensor("dt_s", [128, HEADS * C], BF16))
        ld(t_dqr[:, :], d_dqr.ap(), k_dqr, dkey="dqr")
        ld(t_dt[:, :], d_dt.ap(), k_dt, dkey="dtm")

        b.barrier()
        tap("v", lambda: t_vtm[:, :], k_vtm, [128, NCH * HEADS * HD], BF16)
        t_scg = [esV.enter_context(nc.sbuf_tensor(f"scg{i}", [128, 8 * C], BF16))
                 for i in range(2)]
        t_ktg = [esV.enter_context(nc.sbuf_tensor(f"ktg{i}", [128, 8 * C], BF16))
                 for i in range(2)]
        k_scg = [T(), T()]
        k_ktg = [T(), T()]

        def qsl(h, j):
            return t_qbuf[:, h * TPC + j * C: h * TPC + (j + 1) * C]

        def ksl(h, j):
            return t_kfm[:, h * TPC + j * C: h * TPC + (j + 1) * C]

        def vsl(h, j):
            return t_vtm[:, (j * HEADS + h) * HD:(j * HEADS + h + 1) * HD]

        def scan_front(g):
            j, hg = g // 2, g % 2
            hs = [hg * 8 + i for i in range(8)]
            par = g % 2
            pkt, kkt = psb2[par], k_psb2[par]
            b.emit("pe", None, fns=[
                lambda t, h=h, i=i, j=j, pkt=pkt: t.transpose(
                    pkt[:, i * C:(i + 1) * C], ksl(h, j), t_identb[:, :])
                for i, h in enumerate(hs)],
                reads=[k_kfm[h] for h in hs] + [k_consts], writes=[kkt])
            b.emit("act", None, fns=[
                lambda s, h=h, i=i, pkt=pkt, par=par: s.activation(
                    t_ktg[par][:, i * C:(i + 1) * C], pkt[:, i * C:(i + 1) * C],
                    AF.Copy, scale=t_dkcol[:, h:h + 1])
                for i, h in enumerate(hs)],
                reads=[kkt, k_consts], writes=[k_ktg[par]])
            b.emit("pe", None, fns=[
                lambda t, h=h, i=i, j=j: t.matmul(
                    ps[i // 4][:, (i % 4) * C:(i % 4) * C + C],
                    ksl(h, j), qsl(h, j), start=True, stop=True,
                    skip_group_check=True)
                for i, h in enumerate(hs)],
                reads=[k_kfm[h] for h in hs] + [k_qbuf[h] for h in hs],
                writes=[k_ps[0], k_ps[1]])
            b.emit("dve", None, fns=[
                lambda v, h=h, i=i, par=par: v.tensor_tensor(
                    t_scg[par][:, i * C:(i + 1) * C],
                    ps[i // 4][:, (i % 4) * C:(i % 4) * C + C],
                    t_dt[:, h * C:(h + 1) * C], ALU.mult)
                for i, h in enumerate(hs)],
                reads=[k_ps[0], k_ps[1], k_dt], writes=[k_scg[par]])
            b.emit("dve", None, fns=[
                lambda v, h=h, j=j: v.tensor_tensor(
                    qsl(h, j), qsl(h, j), t_dqr[:, h * C:(h + 1) * C], ALU.mult)
                for h in hs],
                reads=[k_dqr], writes=[k_qbuf[h] for h in hs])

        def scan_back(g):
            j, hg = g // 2, g % 2
            hs = [hg * 8 + i for i in range(8)]
            par = g % 2
            kvs = [k_vtm[j * 4 + h // 4] for h in hs]
            ofns = []
            for i, h in enumerate(hs):
                osl = lambda i=i: ps[2 + i // 4][:, (i % 4) * C:(i % 4) * C + C]
                if j == 0:
                    ofns.append(lambda t, h=h, i=i, j=j, osl=osl, par=par:
                                t.matmul(osl(), vsl(h, j),
                                         t_scg[par][:, i * C:(i + 1) * C],
                                         start=True, stop=True,
                                         skip_group_check=True))
                else:
                    ofns.append(lambda t, h=h, i=i, j=j, osl=osl, par=par:
                                t.matmul(osl(), vsl(h, j),
                                         t_scg[par][:, i * C:(i + 1) * C],
                                         start=True, stop=False,
                                         skip_group_check=True))
                    ofns.append(lambda t, h=h, i=i, j=j, osl=osl: t.matmul(
                        osl(), t_Sb[:, h * HD:(h + 1) * HD], qsl(h, j),
                        start=False, stop=True, skip_group_check=True))
            b.emit("pe", None, fns=ofns,
                   reads=kvs + [k_scg[par]] + [k_Sb[h] for h in hs]
                   + [k_qbuf[h] for h in hs],
                   writes=[k_ps[2], k_ps[3]])
            b.emit("act", None, fns=[
                lambda s, h=h, i=i, j=j: s.copy(
                    t_obuf[:, h * TPC + j * C: h * TPC + (j + 1) * C],
                    ps[2 + i // 4][:, (i % 4) * C:(i % 4) * C + C])
                for i, h in enumerate(hs)],
                reads=[k_ps[2], k_ps[3]], writes=[k_obuf[h] for h in hs])
            b.emit("pe", None, fns=[
                lambda t, h=h, i=i, j=j, par=par: t.matmul(
                    ps[4 + i // 4][:, (i % 4) * C:(i % 4) * C + C],
                    t_ktg[par][:, i * C:(i + 1) * C], vsl(h, j),
                    start=True, stop=True, skip_group_check=True)
                for i, h in enumerate(hs)],
                reads=[k_ktg[par]] + kvs, writes=[k_ps[4], k_ps[5]])
            sfns = []
            for i, h in enumerate(hs):
                asl = lambda i=i: ps[4 + i // 4][:, (i % 4) * C:(i % 4) * C + C]
                if j == 0:
                    sfns.append(lambda v, h=h, asl=asl: v.tensor_copy(
                        t_S[:, h * HD:(h + 1) * HD], asl()))
                else:
                    sfns.append(lambda v, h=h, asl=asl: v.scalar_tensor_tensor(
                        t_S[:, h * HD:(h + 1) * HD], t_S[:, h * HD:(h + 1) * HD],
                        t_lamcol[:, h:h + 1], asl(), ALU.mult, ALU.add))
            b.emit("dve", None, fns=sfns,
                   reads=[k_ps[4], k_ps[5], k_consts],
                   writes=[k_S[h] for h in hs])
            if j < NCH - 1:
                b.emit("act", None, fns=[
                    lambda s, h=h: s.copy(
                        t_Sb[:, h * HD:(h + 1) * HD],
                        t_S[:, h * HD:(h + 1) * HD])
                    for h in hs],
                    reads=[k_S[h] for h in hs],
                    writes=[k_Sb[h] for h in hs])

        for g in range(2 * NCH + 1):
            if g < 2 * NCH:
                scan_front(g)
            if g >= 1:
                scan_back(g - 1)
        tap("o_scan", lambda: t_obuf[:, :], k_obuf, [128, HEADS * TPC], BF16)
        tap("s_loc", lambda: t_S[:, :], k_S, [128, HEADS * HD], F32)
        esV.close()
        esA2.close()

        b.barrier()
        esC = ExitStack()
        t_G = [esC.enter_context(nc.sbuf_tensor(f"G{i}", [128, HEADS * HD], F32))
               for i in range(3)]
        t_tmpA = esC.enter_context(nc.sbuf_tensor("tmpA", [128, HD], F32))
        t_tmpB = esC.enter_context(nc.sbuf_tensor("tmpB", [128, HD], F32))
        t_Sin = esC.enter_context(nc.sbuf_tensor("Sin", [128, HEADS * HD], BF16))
        t_qt2 = [esC.enter_context(nc.sbuf_tensor(f"qt{i}", [128, C], BF16))
                 for i in range(2)]
        t_gsa = esC.enter_context(nc.sbuf_tensor("gsa", [128, HEADS * TPC], BF16))
        t_sq = esC.enter_context(nc.sbuf_tensor("sqg", [128, TPC], BF16))
        t_rstd = esC.enter_context(nc.sbuf_tensor("rstdg", [16, TPC], F32))
        t_rstdb = esC.enter_context(nc.sbuf_tensor("rstdgb", [16, TPC], BF16))
        k_G = [T() for _ in range(3)]
        k_tmpA, k_tmpB, k_Sin = T(), T(), T()
        k_qt2 = [T(), T()]
        k_gsa, k_sq, k_rstd, k_rstdb = T(), T(), T(), T()
        k_ccin, k_ccout = T(), T()

        b.emit("sync", lambda g: g.dma_start(
            out=d_ccin.ap().rearrange("(h p) e -> p h e", p=128),
            in_=t_S.ap().rearrange("p (h e) -> p h e", h=HEADS)),
            reads=list(k_S), writes=[k_ccin], dma=True, dkey="ccin")

        e = b.eng["pool"]
        b._waits(e, [k_ccin.w])
        cc_sem = b.cc_sem
        e["script"].append(lambda g, s=cc_sem: g.collective_compute(
            "AllGather", ALU.bypass, replica_groups=[list(range(NCORES))],
            ins=[d_ccin.ap().opt()], outs=[d_ccout.ap().opt()]).then_inc(s, 1))
        k_ccout.w = (cc_sem, 1)

        b.emit("pool", lambda g: g.dma_start(
            out=t_stg[0][:, :EMBED], in_=d_wg2.ap()),
            writes=[k_stg[0]], dma=True, dkey="stg0")
        k_wg2v = k_stg[0]
        for h in range(HEADS):
            for half in range(2):
                bank = (h * 2 + half) % 4
                b.emit("pe", lambda t, h=h, half=half, bank=bank: t.matmul(
                    ps[bank][:, :], t_stg[0][:, h * 128:(h + 1) * 128],
                    t_g1[:, half * 512:half * 512 + 512], start=True, stop=True),
                    reads=[k_wg2v, k_g1], writes=[k_ps[bank]])
                b.emit("act", lambda s, h=h, half=half, bank=bank: s.activation(
                    t_gsa[:, h * TPC + half * 512: h * TPC + half * 512 + 512],
                    ps[bank][:, :], AF.Sigmoid),
                    reads=[k_ps[bank]], writes=[k_gsa])

        esy = b.eng["sync"]
        dsem = esy["dsems"]["g"][0] if "g" in esy["dsems"] else None
        if dsem is None:
            raise RuntimeError("missing sync g key")

        def fetch_G(g, s):
            with g.register("gofs") as gofs:
                for i in range(3):
                    g.reg_load(gofs, t_poff[:1, i:i + 1])
                    g.dma_start(
                        out=t_G[i].ap().rearrange("p (h e) -> p h e", h=HEADS),
                        in_=bass.AP(d_ccout, gofs,
                                    [[128, 128], [128 * 128, HEADS], [1, 128]]),
                    ).then_inc(s, 16)
        b._waits(esy, [k_ccout.w, k_consts.w])
        esy["dsems"]["g"][1] += 48
        g_tok = (dsem, esy["dsems"]["g"][1])
        esy["script"].append(lambda g, s=dsem: fetch_G(g, s))
        for kk in k_G:
            kk.w = g_tok

        for h in range(HEADS):
            b.emit("dve", lambda v, h=h: v.tensor_scalar(
                t_tmpA[:, :], t_G[0][:, h * HD:(h + 1) * HD],
                t_wcol3[:, h:h + 1], None, ALU.mult),
                reads=[k_G[0], k_consts], writes=[k_tmpA])
            b.emit("dve", lambda v, h=h: v.scalar_tensor_tensor(
                t_tmpB[:, :], t_G[1][:, h * HD:(h + 1) * HD],
                t_wcol3[:, HEADS + h:HEADS + h + 1], t_tmpA[:, :],
                ALU.mult, ALU.add),
                reads=[k_G[1], k_consts, k_tmpA], writes=[k_tmpB])
            b.emit("dve", lambda v, h=h: v.scalar_tensor_tensor(
                t_tmpA[:, :], t_G[2][:, h * HD:(h + 1) * HD],
                t_wcol3[:, 2 * HEADS + h:2 * HEADS + h + 1], t_tmpB[:, :],
                ALU.mult, ALU.add),
                reads=[k_G[2], k_consts, k_tmpB], writes=[k_tmpA])
            b.emit("act", lambda s, h=h: s.copy(
                t_Sin[:, h * HD:(h + 1) * HD], t_tmpA[:, :]),
                reads=[k_tmpA], writes=[k_Sin])

        lj_np = lam_C[:, None] ** np.arange(NCH)[None, :]
        pc = 0
        for h in range(HEADS):
            for j in range(NCH):
                if lj_np[h, j] < 1e-30:
                    continue
                par = pc % 2
                pc += 1
                b.emit("dve", lambda v, h=h, j=j, par=par: v.tensor_scalar(
                    t_qt2[par][:, :],
                    t_qbuf[:, h * TPC + j * C: h * TPC + (j + 1) * C],
                    t_lamj[:, h * NCH + j: h * NCH + j + 1], None, ALU.mult),
                    reads=[k_qbuf[h], k_consts], writes=[k_qt2[par]])
                b.emit("pe", lambda t, h=h, par=par: t.matmul(
                    ps[par][:, 0:C], t_Sin[:, h * HD:(h + 1) * HD],
                    t_qt2[par][:, :], start=True, stop=True),
                    reads=[k_Sin, k_qt2[par]], writes=[k_ps[par]])
                b.emit("dve", lambda v, h=h, j=j, par=par: v.tensor_tensor(
                    t_obuf[:, h * TPC + j * C: h * TPC + (j + 1) * C],
                    t_obuf[:, h * TPC + j * C: h * TPC + (j + 1) * C],
                    ps[par][:, 0:C], ALU.add),
                    reads=[k_ps[par]], writes=[k_obuf[h]])
        tap("o_corr", lambda: t_obuf[:, :], k_obuf, [128, HEADS * TPC], BF16)
        tap("sin_t", lambda: t_Sin[:, :], [k_Sin], [128, HEADS * HD], BF16)
        for h in range(HEADS):
            b.emit("dve", lambda v, h=h: v.tensor_tensor(
                t_obuf[:, h * TPC:(h + 1) * TPC],
                t_obuf[:, h * TPC:(h + 1) * TPC],
                t_gsa[:, h * TPC:(h + 1) * TPC], ALU.mult),
                reads=[k_gsa], writes=[k_obuf[h]])
            b.emit("act", lambda s, h=h: s.activation(
                t_sq[:, :], t_obuf[:, h * TPC:(h + 1) * TPC], AF.Square),
                reads=[k_obuf[h]], writes=[k_sq])
            for half in range(2):
                b.emit("pe", lambda t, h=h, half=half: t.matmul(
                    ps[2 + half][0:16, :], t_ohb[:, h * 16:(h + 1) * 16],
                    t_sq[:, half * 512:half * 512 + 512],
                    start=(h == 0), stop=(h == HEADS - 1)),
                    reads=[k_sq, k_consts], writes=[k_ps[2 + half]])
        tap("xg", lambda: t_obuf[:, :], k_obuf, [128, HEADS * TPC], BF16)
        for half in range(2):
            b.emit("act", lambda s, half=half: s.activation(
                t_rstd[:, half * 512:half * 512 + 512], ps[2 + half][0:16, :],
                AF.Sqrt, bias=t_cvec[0:16, 1:2], scale=t_cvec[0:16, 0:1]),
                reads=[k_ps[2 + half], k_cvec], writes=[k_rstd])
        b.emit("dve", lambda v: v.reciprocal(t_rstd[:, :], t_rstd[:, :]),
               reads=[], writes=[k_rstd])
        b.emit("act", lambda s: s.copy(t_rstdb[:, :], t_rstd[:, :]),
               reads=[k_rstd], writes=[k_rstdb])
        tap("rstd", lambda: t_rstd[:, :], [k_rstd], [16, TPC], F32)
        for h in range(HEADS):
            for half in range(2):
                bank = half
                b.emit("pe", lambda t, h=h, half=half, bank=bank: t.matmul(
                    ps[bank][:, :], t_rsel[:, h * 128:(h + 1) * 128],
                    t_rstdb[:, half * 512:half * 512 + 512],
                    start=True, stop=True),
                    reads=[k_rstdb, k_consts], writes=[k_ps[bank]])
                b.emit("dve", lambda v, h=h, half=half, bank=bank:
                       v.scalar_tensor_tensor(
                           t_obuf[:, h * TPC + half * 512: h * TPC + half * 512 + 512],
                           t_obuf[:, h * TPC + half * 512: h * TPC + half * 512 + 512],
                           t_nwcol[:, h:h + 1], ps[bank][:, :],
                           ALU.mult, ALU.mult),
                       reads=[k_ps[bank], k_consts], writes=[k_obuf[h]])
        tap("o_n", lambda: t_obuf[:, :], k_obuf, [128, HEADS * TPC], BF16)
        esC.close()
        b.barrier()

        b.barrier()
        t_f = sb("f_acc", [128, KC * TPC], F32, side="right")
        k_f = T()
        b.emit("sync", lambda g: g.dma_start(
            out=t_f.ap().rearrange("p (kc t) -> p kc t", kc=KC),
            in_=d_xt.ap().rearrange("(kc p) t -> p kc t", p=128)),
            writes=[k_f], dma=True, dkey="xf")
        for m in range(KC):
            sg = (m // 2) % 2
            moff = (m % 2) * EMBED
            if m % 2 == 0:
                b.emit("pool", lambda g, m=m, sg=sg: g.dma_start(
                    out=t_stg[sg][:, 0:2 * EMBED].rearrange("p (m f) -> p m f",
                                                            m=2),
                    in_=d_wo.ap()[m:m + 2, :, :].rearrange("m p f -> p m f")),
                    writes=[k_stg[sg]], dma=True, dkey=f"stg{sg}")
            for half in range(2):
                bank = half
                mm_sweep(bank,
                         lambda kci, sg=sg, moff=moff:
                         t_stg[sg][:, moff + kci * 128: moff + (kci + 1) * 128],
                         lambda kci, half=half: t_obuf[:, kci * TPC + half * 512:
                                                       kci * TPC + half * 512 + 512],
                         KC, tuple([k_stg[sg]] + k_obuf))
                b.emit("dve", lambda v, m=m, half=half, bank=bank: v.tensor_tensor(
                    t_f[:, m * TPC + half * 512: m * TPC + half * 512 + 512],
                    t_f[:, m * TPC + half * 512: m * TPC + half * 512 + 512],
                    ps[bank][:, :], ALU.add),
                    reads=[k_ps[bank]], writes=[k_f])
        tap("r1", lambda: t_f[:, :], [k_f], [128, KC * TPC], F32)
        esA.close()

        def layer_norm(t_x, k_x, sq_scope_name, kc_done=None):
            esL = ExitStack()
            t_sqf2 = [esL.enter_context(nc.sbuf_tensor(
                f"sqf{i}_{sq_scope_name}", [128, TPC], F32R)) for i in range(2)]
            t_xr2 = [esL.enter_context(nc.sbuf_tensor(
                f"xr{i}_{sq_scope_name}", [128, TPC], F32R)) for i in range(2)]
            t_oqr = esL.enter_context(nc.sbuf_tensor(
                f"oqr_{sq_scope_name}", [128, 128], F32R))
            t_st = esL.enter_context(nc.sbuf_tensor(
                f"st_{sq_scope_name}", [1, 4 * TPC], F32))
            k_sqf2 = [T(), T()]
            k_xr2 = [T(), T()]
            k_oqr = T()
            k_st = T()
            b.emit("pool", lambda g: g.tensor_copy(t_oqr[:, :], t_onesq[:, :]),
                   reads=[k_consts], writes=[k_oqr])
            for kci in range(KC):
                sqp = kci % 2
                b.emit("act", lambda s, kci=kci, sqp=sqp: s.activation(
                    t_sqf2[sqp][:, :], t_x[:, kci * TPC:(kci + 1) * TPC],
                    AF.Square), reads=[k_x], writes=[k_sqf2[sqp]])
                b.emit("pool", lambda g, kci=kci, sqp=sqp: g.tensor_copy(
                    t_xr2[sqp][:, :], t_x[:, kci * TPC:(kci + 1) * TPC]),
                    reads=[k_x], writes=[k_xr2[sqp]])
                fns = []
                for half in range(2):
                    fns.append(lambda t, kci=kci, half=half, sqp=sqp: t.matmul(
                        ps[4 + half][:, :], t_oqr[:, :],
                        t_xr2[sqp][:, half * 512:half * 512 + 512],
                        start=(kci == 0), stop=(kci == KC - 1),
                        skip_group_check=True))
                    fns.append(lambda t, kci=kci, half=half, sqp=sqp: t.matmul(
                        ps[2 + half][:, :], t_oqr[:, :],
                        t_sqf2[sqp][:, half * 512:half * 512 + 512],
                        start=(kci == 0), stop=(kci == KC - 1),
                        skip_group_check=True))
                b.emit("pe", None, fns=fns,
                       reads=[k_xr2[sqp], k_sqf2[sqp], k_oqr],
                       writes=[k_ps[2], k_ps[3], k_ps[4], k_ps[5]])
            for half in range(2):
                b.emit("dve", lambda v, half=half: v.tensor_scalar(
                    t_st[0:1, half * 512:half * 512 + 512],
                    ps[4 + half][0:1, :], t_cvec[0:1, 2:3], None, ALU.mult),
                    reads=[k_ps[4 + half], k_cvec], writes=[k_st])
                b.emit("dve", lambda v, half=half: v.tensor_scalar(
                    t_st[0:1, TPC + half * 512: TPC + half * 512 + 512],
                    ps[2 + half][0:1, :], t_cvec[0:1, 2:3], None, ALU.mult),
                    reads=[k_ps[2 + half], k_cvec], writes=[k_st])
            b.emit("dve", lambda v: v.tensor_tensor(
                t_st[0:1, 2 * TPC:3 * TPC], t_st[0:1, 0:TPC], t_st[0:1, 0:TPC],
                ALU.mult), reads=[], writes=[k_st])
            b.emit("dve", lambda v: v.tensor_tensor(
                t_st[0:1, 2 * TPC:3 * TPC], t_st[0:1, TPC:2 * TPC],
                t_st[0:1, 2 * TPC:3 * TPC], ALU.subtract),
                reads=[], writes=[k_st])
            b.emit("act", lambda s: s.activation(
                t_st[0:1, 2 * TPC:3 * TPC], t_st[0:1, 2 * TPC:3 * TPC],
                AF.Sqrt, bias=t_cvec[0:1, 3:4]),
                reads=[k_cvec], writes=[k_st])
            b.emit("dve", lambda v: v.reciprocal(
                t_st[0:1, 2 * TPC:3 * TPC], t_st[0:1, 2 * TPC:3 * TPC]),
                reads=[], writes=[k_st])
            b.emit("dve", lambda v: v.tensor_tensor(
                t_st[0:1, 3 * TPC:4 * TPC], t_st[0:1, 0:TPC],
                t_st[0:1, 2 * TPC:3 * TPC], ALU.mult),
                reads=[], writes=[k_st])
            for half in range(2):
                b.emit("pe", lambda t, half=half: t.matmul(
                    ps[4 + half][:, :], t_onesrow[:, :],
                    t_st[0:1, 2 * TPC + half * 512: 2 * TPC + half * 512 + 512],
                    start=True, stop=True),
                    reads=[k_st, k_consts], writes=[k_ps[4 + half]])
                b.emit("pe", lambda t, half=half: t.matmul(
                    ps[2 + half][:, :], t_onesrow[:, :],
                    t_st[0:1, 3 * TPC + half * 512: 3 * TPC + half * 512 + 512],
                    start=True, stop=True),
                    reads=[k_st, k_consts], writes=[k_ps[2 + half]])
            for kci in range(KC):
                for half in range(2):
                    sl = lambda kci=kci, half=half: t_x[
                        :, kci * TPC + half * 512: kci * TPC + half * 512 + 512]
                    b.emit("dve", lambda v, sl=sl, half=half: v.tensor_tensor(
                        sl(), sl(), ps[4 + half][:, :], ALU.mult),
                        reads=[k_ps[4 + half]], writes=[k_x])
                    b.emit("dve", lambda v, sl=sl, half=half: v.tensor_tensor(
                        sl(), sl(), ps[2 + half][:, :], ALU.subtract),
                        reads=[k_ps[2 + half]], writes=[k_x])
                if kc_done is not None:
                    kc_done(kci, k_x)
            esL.close()

        b.barrier()
        layer_norm(t_f, k_f, "ln1")
        tap("h1", lambda: t_f[:, :], [k_f], [128, KC * TPC], F32)

        esF = ExitStack()
        t_h1b = esF.enter_context(nc.sbuf_tensor("h1b", [128, KC * TPC], BF16))
        t_lsb = esF.enter_context(nc.sbuf_tensor("lsb", [128, NE], F32))
        t_ysb = esF.enter_context(nc.sbuf_tensor("ysb", [128, NE], F32))
        t_msk = esF.enter_context(nc.sbuf_tensor("msk", [128, NE], F32))
        t_red = esF.enter_context(nc.sbuf_tensor("red", [128, 4], F32))
        t_ctm = esF.enter_context(nc.sbuf_tensor("ctm", [128, NE], BF16))
        t_cT = esF.enter_context(nc.sbuf_tensor("cT", [16, TPC], BF16))
        k_h1b, k_lsb, k_ysb, k_msk, k_red, k_ctm, k_cT = (
            T(), T(), T(), T(), T(), T(), T())
        b.emit("pool", lambda g: g.dma_start(out=t_h1b[:, :], in_=t_f[:, :]),
               reads=[k_f], writes=[k_h1b], dma=True, dkey="misc")

        tap("h1b", lambda: t_h1b[:, :], [k_h1b], [128, KC * TPC], BF16)
        for tc in range(NCH):
            fns = []
            for kci in range(KC):
                fns.append(lambda t, kci=kci, tc=tc: t.matmul(
                    ps[0][:, 0:NE],
                    t_f[:, kci * TPC + tc * 128: kci * TPC + tc * 128 + 128],
                    t_rwt[:, kci * NE:(kci + 1) * NE],
                    start=(kci == 0), stop=(kci == KC - 1)))
            b.emit("pe", None, fns=fns, reads=[k_f, k_rwt], writes=[k_ps[0]])
            b.emit("act", lambda s: s.copy(t_lsb[:, :], ps[0][:, 0:NE]),
                   reads=[k_ps[0]], writes=[k_lsb])
            b.emit("dve", lambda v: v.tensor_tensor(
                t_ysb[:, :], t_lsb[:, :], t_biasrep[:, :], ALU.add),
                reads=[k_lsb, k_consts], writes=[k_ysb])
            b.emit("dve", lambda v: v.reduce_max(
                t_red[:, 0:1], t_ysb[:, :], axis=AXX),
                reads=[k_ysb], writes=[k_red])
            b.emit("dve", lambda v: v.tensor_single_scalar(
                t_msk[:, :], t_ysb[:, :], t_red[:, 0:1], ALU.is_ge),
                reads=[k_ysb, k_red], writes=[k_msk])
            b.emit("dve", lambda v: v.scalar_tensor_tensor(
                t_ysb[:, :], t_msk[:, :], t_cvec[:, 4:5], t_ysb[:, :],
                ALU.mult, ALU.add),
                reads=[k_msk, k_cvec], writes=[k_ysb])
            b.emit("dve", lambda v: v.reduce_max(
                t_red[:, 1:2], t_ysb[:, :], axis=AXX),
                reads=[k_ysb], writes=[k_red])
            b.emit("dve", lambda v: v.tensor_single_scalar(
                t_ysb[:, :], t_ysb[:, :], t_red[:, 1:2], ALU.is_ge),
                reads=[k_red], writes=[k_ysb])
            b.emit("dve", lambda v: v.tensor_tensor(
                t_msk[:, :], t_msk[:, :], t_ysb[:, :], ALU.add),
                reads=[k_ysb], writes=[k_msk])
            b.emit("dve", lambda v: v.reduce_max(
                t_red[:, 2:3], t_lsb[:, :], axis=AXX),
                reads=[k_lsb], writes=[k_red])
            b.emit("dve", lambda v: v.tensor_single_scalar(
                t_lsb[:, :], t_lsb[:, :], t_red[:, 2:3], ALU.subtract),
                reads=[k_red], writes=[k_lsb])
            b.emit("act", lambda s: s.activation(
                t_lsb[:, :], t_lsb[:, :], AF.Exp),
                reads=[], writes=[k_lsb])
            b.emit("dve", lambda v: v.tensor_tensor(
                t_lsb[:, :], t_lsb[:, :], t_msk[:, :], ALU.mult),
                reads=[k_msk], writes=[k_lsb])
            b.emit("dve", lambda v: v.reduce_sum(
                t_red[:, 3:4], t_lsb[:, :], axis=AXX),
                reads=[k_lsb], writes=[k_red])
            b.emit("dve", lambda v: v.reciprocal(
                t_red[:, 3:4], t_red[:, 3:4]), reads=[], writes=[k_red])
            b.emit("dve", lambda v: v.tensor_scalar(
                t_ctm[:, :], t_lsb[:, :], t_red[:, 3:4], None, ALU.mult),
                reads=[k_lsb, k_red], writes=[k_ctm])
            if tc == 7:
                tap("rlogit", lambda: t_lsb[:, :], [k_lsb], [128, NE], F32)
                tap("rmsk", lambda: t_msk[:, :], [k_msk], [128, NE], F32)
                tap("rctm", lambda: t_ctm[:, :], [k_ctm], [128, NE], BF16)
            b.emit("pe", lambda t: t.transpose(
                psb[0:NE, 0:128], t_ctm[:, :], t_identb[:, :]),
                reads=[k_ctm, k_consts], writes=[k_psb])
            b.emit("act", lambda s, tc=tc: s.copy(
                t_cT[:, tc * 128:(tc + 1) * 128], psb[0:NE, 0:128]),
                reads=[k_psb], writes=[k_cT])

        tap("cT", lambda: t_cT[:, :], [k_cT], [16, TPC], BF16)

        b.barrier()
        t_usil = esF.enter_context(nc.sbuf_tensor("usil", [128, EKC * TPC], BF16))
        t_hb = [esF.enter_context(nc.sbuf_tensor(f"hb{i}", [128, EKC * TPC], BF16))
                for i in range(2)]
        t_cb = esF.enter_context(nc.sbuf_tensor("cb", [128, TPC], BF16))
        k_usil, k_cb = T(), T()
        k_hb = [T(), T()]

        for pr in range(NPAIR):
            b.barrier(2)
            for ei in range(2):
                e_idx = pr * 2 + ei
                for wgt, dram_w, is_u in ((0, d_wue, True), (1, d_wge, False)):
                    for mp in range(0, EKC, 2):
                        sg = (mp // 2) % 2
                        b.emit("pool", lambda g, e_idx=e_idx, mp=mp, sg=sg,
                               dram_w=dram_w: g.dma_start(
                                   out=t_stg[sg][:, 0:2 * EMBED].rearrange(
                                       "p (m f) -> p m f", m=2),
                                   in_=dram_w.ap()[e_idx, mp:mp + 2, :, :]
                                   .rearrange("m p f -> p m f")),
                               writes=[k_stg[sg]], dma=True, dkey=f"stg{sg}")
                        for mi in range(2):
                            m = mp + mi
                            for half in range(2):
                                bank = half
                                mm_sweep(
                                    bank,
                                    lambda kci, sg=sg, mi=mi:
                                    t_stg[sg][:, mi * EMBED + kci * 128:
                                              mi * EMBED + (kci + 1) * 128],
                                    lambda kci, half=half:
                                    t_h1b[:, kci * TPC + half * 512:
                                          kci * TPC + half * 512 + 512],
                                    KC, (k_stg[sg], k_h1b))
                                dst = lambda m=m, half=half: (
                                    m * TPC + half * 512, m * TPC + half * 512 + 512)
                                if is_u:
                                    b.emit("act", lambda s, d=dst(), bank=bank:
                                           s.activation(t_usil[:, d[0]:d[1]],
                                                        ps[bank][:, :], AF.Silu),
                                           reads=[k_ps[bank]], writes=[k_usil])
                                else:
                                    b.emit("dve", lambda v, d=dst(), bank=bank,
                                           ei=ei: v.tensor_tensor(
                                               t_hb[ei][:, d[0]:d[1]],
                                               ps[bank][:, :],
                                               t_usil[:, d[0]:d[1]], ALU.mult),
                                           reads=[k_ps[bank], k_usil],
                                           writes=[k_hb[ei]])
                if pr == 0 and ei == 0:
                    tap("usil0", lambda: t_usil[:, :], [k_usil],
                        [128, EKC * TPC], BF16)
                    tap("hb0raw", lambda: t_hb[0][:, :], [k_hb[0]],
                        [128, EKC * TPC], BF16)
                if e_idx < NE:
                    for half in range(2):
                        bank = 2 + half
                        b.emit("pe", lambda t, e_idx=e_idx, half=half, bank=bank:
                               t.matmul(ps[bank][:, :],
                                        t_rsel[:, e_idx * 128:(e_idx + 1) * 128],
                                        t_cT[:, half * 512:half * 512 + 512],
                                        start=True, stop=True),
                               reads=[k_cT, k_consts], writes=[k_ps[bank]])
                        b.emit("act", lambda s, half=half, bank=bank: s.copy(
                            t_cb[:, half * 512:half * 512 + 512],
                            ps[bank][:, :]),
                            reads=[k_ps[bank]], writes=[k_cb])
                    for m in range(EKC):
                        b.emit("dve", lambda v, m=m, ei=ei: v.tensor_tensor(
                            t_hb[ei][:, m * TPC:(m + 1) * TPC],
                            t_hb[ei][:, m * TPC:(m + 1) * TPC],
                            t_cb[:, :], ALU.mult),
                            reads=[k_cb], writes=[k_hb[ei]])
            if pr == 0:
                tap("hb0", lambda: t_hb[0][:, :], [k_hb[0]],
                    [128, EKC * TPC], BF16)
            for mp in range(0, 16, 2):
                sg = (mp // 2) % 2
                b.emit("pool", lambda g, pr=pr, mp=mp, sg=sg: g.dma_start(
                    out=t_stg[sg][:, 0:4096].rearrange("p (m i f) -> p m i f",
                                                       m=2, i=2),
                    in_=d_wdp.ap()[pr, mp:mp + 2, :, :, :]
                    .rearrange("m i p f -> p m i f")),
                    writes=[k_stg[sg]], dma=True, dkey=f"stg{sg}")
                for mi in range(2):
                    m = mp + mi
                    for half in range(2):
                        bank = half
                        fns = []
                        for ei in range(2):
                            for ke in range(EKC):
                                fns.append(lambda t, sg=sg, mi=mi, ei=ei, ke=ke,
                                           half=half, bank=bank: t.matmul(
                                               ps[bank][:, :],
                                               t_stg[sg][:, (mi * 2 + ei) * 1024 +
                                                         ke * 128:
                                                         (mi * 2 + ei) * 1024 +
                                                         (ke + 1) * 128],
                                               t_hb[ei][:, ke * TPC + half * 512:
                                                        ke * TPC + half * 512 + 512],
                                               start=(ei == 0 and ke == 0),
                                               stop=(ei == 1 and ke == EKC - 1)))
                        b.emit("pe", None, fns=fns,
                               reads=[k_stg[sg], k_hb[0], k_hb[1]],
                               writes=[k_ps[bank]])
                        b.emit("dve", lambda v, m=m, half=half, bank=bank:
                               v.tensor_tensor(
                                   t_f[:, m * TPC + half * 512:
                                       m * TPC + half * 512 + 512],
                                   t_f[:, m * TPC + half * 512:
                                       m * TPC + half * 512 + 512],
                                   ps[bank][:, :], ALU.add),
                               reads=[k_ps[bank]], writes=[k_f])
        tap("f", lambda: t_f[:, :], [k_f], [128, KC * TPC], F32)
        esF.close()

        b.barrier()
        out_toks = []

        def emit_out_kc(kci, k_x):
            tok = b.emit("sync", lambda g, kci=kci: g.dma_start(
                out=d_out.ap()[kci * 128:(kci + 1) * 128, :],
                in_=t_f[:, kci * TPC:(kci + 1) * TPC]),
                reads=[k_x], dma=True, dkey="out")
            out_toks.append(tok)

        layer_norm(t_f, k_f, "ln2", kc_done=emit_out_kc)
        out_tok = out_toks[-1]
        b.raw("sync", lambda g, s=out_tok[0], v=out_tok[1]: g.wait_ge(s, v))

        with nc.Block() as block:
            b.run_blocks(block)
    return nc


_NC_CACHE = None
_TAP_RESULTS = {}


def kernel(**inputs):
    global _NC_CACHE
    x = np.asarray(inputs["L0_attn_x"], np.float32)
    shared = host_weights(inputs)
    if _NC_CACHE is None:
        _NC_CACHE = build_program()
    nc = _NC_CACHE
    in_maps = []
    for c in range(NCORES):
        bi, cb = c // 4, c % 4
        xt = np.ascontiguousarray(x[bi, cb * TPC:(cb + 1) * TPC, :].T)
        m = dict(shared)
        m["xt"] = xt
        m.update(host_constants(c))
        in_maps.append(m)
    res = run_bass_kernel_spmd(nc, in_maps, core_ids=list(range(NCORES)))
    outs = []
    for c in range(NCORES):
        outs.append(res.results[c]["out_t"].T)
        for kk, vv in res.results[c].items():
            if kk.startswith("tap_"):
                _TAP_RESULTS.setdefault(kk, []).append(np.asarray(vv))
    full = np.concatenate(outs, axis=0).reshape(2, 4096, EMBED)
    return full.astype(np.float32)


# version 12
# speedup vs baseline: 1.2241x; 1.0134x over previous
import numpy as np
import ml_dtypes
import concourse.bass as bass
import concourse.mybir as mybir
from concourse.bass_utils import run_bass_kernel_spmd
from contextlib import ExitStack

F32 = mybir.dt.float32
F32R = mybir.dt.float32r
BF16 = mybir.dt.bfloat16
I32 = mybir.dt.int32
AF = mybir.ActivationFunctionType
ALU = mybir.AluOpType
AXX = mybir.AxisListType.X

EMBED = 2048
HEADS = 16
HD = 128
C = 128
KC = 16
TPC = 1024
NCH = 8
NCORES = 8
NE = 16
NEX = 18
NPAIR = 9
EKC = 8
EPS_G = 1e-6
EPS_LN = 1e-5
BASE = 10000.0

_DEBUG_TAPS = set()
_PARANOID = 1

BF = ml_dtypes.bfloat16


def _log_lam():
    h = np.arange(HEADS, dtype=np.float64)
    return -(2.0 ** (-8.0 * (h + 1.0) / HEADS))


def host_constants(core):
    ll = _log_lam()
    cb = core % 4
    t0 = cb * TPC
    half = HD // 2
    theta = BASE ** (-2.0 * np.arange(half, dtype=np.float64) / HD)
    pos = np.arange(t0, t0 + TPC, dtype=np.float64)
    ang = pos[None, :] * np.concatenate([theta, theta])[:, None]
    cosb = np.cos(ang).astype(BF)
    sinb = np.sin(ang).astype(BF)

    i = np.arange(C, dtype=np.float64)
    diff = i[:, None] - i[None, :]
    dmask = np.where(diff[None] >= 0, np.exp(ll[:, None, None] * diff[None]), 0.0)
    dt_c = np.transpose(dmask, (0, 2, 1)).transpose(1, 0, 2).reshape(C, HEADS * C)
    dt_c = np.ascontiguousarray(dt_c).astype(BF)
    dq = np.exp(ll[:, None] * (i[None, :] + 1.0))
    dqr = np.broadcast_to(dq[None], (C, HEADS, C)).reshape(C, HEADS * C)
    dqr = np.ascontiguousarray(dqr).astype(BF)
    dkcol = np.exp(ll[None, :] * (C - 1.0 - i[:, None])).astype(np.float32)
    lam_C = np.exp(ll * C)
    lamcol = np.broadcast_to(lam_C[None, :], (C, HEADS)).astype(np.float32)
    lamcol = np.ascontiguousarray(lamcol)
    lj = lam_C[:, None] ** np.arange(NCH)[None, :]
    lj = np.where(lj < 1e-30, 0.0, lj)
    lamj = np.broadcast_to(lj.reshape(1, -1), (C, HEADS * NCH)).astype(np.float32)
    lamj = np.ascontiguousarray(lamj)

    identb = np.eye(128, dtype=BF)
    perm = np.zeros((128, 128), np.float32)
    for m in range(64):
        perm[m + 64, m] = -1.0
        perm[m, m + 64] = 1.0
    permb = perm.astype(BF)
    ohb = np.zeros((128, HEADS * 16), np.float32)
    for h in range(HEADS):
        ohb[:, h * 16 + h] = 1.0
    ohb = ohb.astype(BF)
    rsel = np.zeros((16, HEADS * 128), np.float32)
    for h in range(HEADS):
        rsel[h, h * 128:(h + 1) * 128] = 1.0
    rsel = rsel.astype(BF)
    onesf = np.ones((128, 1), np.float32)
    onesq = np.ones((128, 128), np.float32)
    onesrow = np.ones((1, 128), np.float32)
    cvec = np.zeros((128, 8), np.float32)
    cvec[:, 0] = 1.0 / HD
    cvec[:, 1] = EPS_G
    cvec[:, 2] = 1.0 / EMBED
    cvec[:, 3] = EPS_LN
    cvec[:, 4] = -1e9
    cvec[:, 5] = 1.0
    lam_B = np.exp(ll * C * NCH)
    wcol3 = np.zeros((128, 3 * HEADS), np.float32)
    poff = np.zeros((1, 4), np.int32)
    for i3 in range(3):
        src = cb - 1 - i3
        r = (core - 1 - i3) % NCORES
        poff[0, i3] = r * EMBED * 128
        if src >= 0:
            w = lam_B ** i3
            w = np.where(w < 1e-30, 0.0, w)
            wcol3[:, i3 * HEADS:(i3 + 1) * HEADS] = w[None, :].astype(np.float32)
    return dict(cosb=cosb, sinb=sinb, dt_c=dt_c, dqr=dqr, dkcol=dkcol,
                lamcol=lamcol, lamj=lamj, identb=identb, permb=permb, ohb=ohb,
                rsel=rsel, onesf=onesf, onesq=onesq, onesrow=onesrow, cvec=cvec,
                wcol3=wcol3, poff=poff)


def host_weights(inp):
    def mpanels(W, n_in_kc, n_m):
        return np.ascontiguousarray(
            W.reshape(n_in_kc, 128, n_m, 128).transpose(2, 1, 0, 3)
            .reshape(n_m, 128, n_in_kc * 128))

    out = {}
    out["wq"] = mpanels(np.asarray(inp["L0_attn_Wq"], np.float32), KC, 16)
    out["wk"] = mpanels(np.asarray(inp["L0_attn_Wk"], np.float32), KC, 16)
    out["wo"] = mpanels(np.asarray(inp["L0_attn_Wo"], np.float32), KC, 16)
    wv = np.asarray(inp["L0_attn_Wv"], np.float32)
    out["wv"] = np.ascontiguousarray(
        wv.reshape(EMBED, 4, 512).transpose(1, 0, 2))
    out["wg1"] = mpanels(np.asarray(inp["L0_attn_Wg1"], np.float32), KC, 1)[0]
    out["wg2"] = np.asarray(inp["L0_attn_Wg2"], np.float32)
    out["rwt"] = np.ascontiguousarray(
        np.asarray(inp["L0_ffn_router_w"], np.float32).T.reshape(KC, 128, NE))
    b8 = np.broadcast_to(
        np.asarray(inp["L0_ffn_router_bias"], np.float32)[None, None, :],
        (128, 8, NE)).reshape(128, 8 * NE)
    out["biasrep"] = np.ascontiguousarray(b8)
    out["nwcol"] = np.ascontiguousarray(
        np.asarray(inp["L0_attn_norm_weight"], np.float32).reshape(HEADS, 128).T)

    wu = np.asarray(inp["L0_ffn_Wu"], np.float32)
    wg = np.asarray(inp["L0_ffn_Wg"], np.float32)
    su = np.asarray(inp["L0_ffn_Su"], np.float32)[0]
    sg = np.asarray(inp["L0_ffn_Sg"], np.float32)[0]
    def upanels(W):
        return mpanels(W, KC, EKC)
    wue = np.empty((NEX, EKC, 128, EMBED), np.float32)
    wge = np.empty((NEX, EKC, 128, EMBED), np.float32)
    for e in range(NE):
        wue[e] = upanels(wu[e])
        wge[e] = upanels(wg[e])
    wue[16] = upanels(su[:, :1024]); wue[17] = upanels(su[:, 1024:])
    wge[16] = upanels(sg[:, :1024]); wge[17] = upanels(sg[:, 1024:])
    out["wue"] = np.ascontiguousarray(wue)
    out["wge"] = np.ascontiguousarray(wge)

    wd = np.asarray(inp["L0_ffn_Wd"], np.float32)
    sd = np.asarray(inp["L0_ffn_Sd"], np.float32)[0]
    def dpanels(W):
        return np.ascontiguousarray(
            W.reshape(EKC, 128, 16, 128).transpose(2, 1, 0, 3)
            .reshape(16, 128, EKC * 128))
    wdx = np.empty((NEX, 16, 128, EKC * 128), np.float32)
    for e in range(NE):
        wdx[e] = dpanels(wd[e])
    wdx[16] = dpanels(sd[:1024, :])
    wdx[17] = dpanels(sd[1024:, :])
    out["wdp"] = np.ascontiguousarray(
        wdx.reshape(NPAIR, 2, 16, 128, EKC * 128).transpose(0, 2, 1, 3, 4))
    return out


class Trk:
    __slots__ = ("w", "rs")

    def __init__(self):
        self.w = None
        self.rs = []


class B:

    ENGS = ("pe", "dve", "act", "pool", "sync")

    def __init__(self, nc):
        self.nc = nc
        self.eng = {}
        self.sems = {}

    DKEYS = {"pool": ("stg0", "stg1", "xbf", "g", "misc"),
             "sync": ("main", "rwt", "cos", "sin", "dqr", "dtm",
                      "ccin", "xf", "out", "g")}

    def init_sems(self, es):
        for name in self.ENGS:
            sem = es.enter_context(self.nc.semaphore(f"s_{name}"))
            d = dict(sem=sem, cnt=0, waited={}, script=[], dsems={})
            for key in self.DKEYS.get(name, ()):
                d["dsems"][key] = [
                    es.enter_context(self.nc.semaphore(f"s_{name}_{key}")), 0]
            self.eng[name] = d
        self.cc_sem = es.enter_context(self.nc.semaphore("s_cc"))

    def _waits(self, e, deps, skip_self=False):
        need = {}
        for (sem, val) in deps:
            if skip_self and sem is e["sem"]:
                continue
            if need.get(sem, 0) < val:
                need[sem] = val
        for sem, val in sorted(need.items(), key=lambda kv: str(kv[0].name)):
            if e["waited"].get(sem, 0) < val:
                e["waited"][sem] = val
                e["script"].append(
                    lambda eng, s=sem, v=val: eng.wait_ge(s, v))

    def emit(self, en, fn, reads=(), writes=(), dma=False, fns=None,
             dkey="main"):
        e = self.eng[en]
        deps = []
        for t in reads:
            if t.w:
                deps.append(t.w)
        for t in writes:
            if t.w:
                deps.append(t.w)
            deps.extend(t.rs)
        self._waits(e, deps, skip_self=(en == "pe"))
        if dma:
            ds = e["dsems"][dkey]
            ds[1] += 16
            sem, cnt = ds[0], ds[1]
            amt = 16
        else:
            e["cnt"] += 1
            sem, cnt = e["sem"], e["cnt"]
            amt = 1
        if fns is not None:
            for f in fns[:-1]:
                e["script"].append(lambda eng, f=f: f(eng))
            e["script"].append(
                lambda eng, f=fns[-1], s=sem, a=amt: f(eng).then_inc(s, a))
        else:
            e["script"].append(
                lambda eng, f=fn, s=sem, a=amt: f(eng).then_inc(s, a))
        tok = (sem, cnt)
        for t in reads:
            t.rs.append(tok)
        for t in writes:
            t.w = tok
            t.rs = []
        return tok

    def raw(self, en, fn):
        self.eng[en]["script"].append(fn)

    def barrier(self, level=1):
        import kernel as _k
        if _PARANOID < level:
            return
        sems = []
        for name in self.ENGS:
            e = self.eng[name]
            if e["cnt"]:
                sems.append((e["sem"], e["cnt"]))
            for key, sc in e["dsems"].items():
                if sc[1]:
                    sems.append((sc[0], sc[1]))
        for name in self.ENGS:
            e = self.eng[name]
            self._waits(e, [t for t in sems if t[0] is not e["sem"]])

    def run_blocks(self, block):
        block.tensor(lambda eng: [f(eng) for f in self.eng["pe"]["script"]])
        block.vector(lambda eng: [f(eng) for f in self.eng["dve"]["script"]])
        block.scalar(lambda eng: [f(eng) for f in self.eng["act"]["script"]])
        block.gpsimd(lambda eng: [f(eng) for f in self.eng["pool"]["script"]])
        block.sync(lambda eng: [f(eng) for f in self.eng["sync"]["script"]])


def build_program():
    nc = bass.Bass("TRN2", target_bir_lowering=False)
    ll = _log_lam()
    lam_C = np.exp(ll * C)

    d_xt = nc.dram_tensor("xt", [EMBED, TPC], F32, kind="ExternalInput")
    d_wq = nc.dram_tensor("wq", [16, 128, EMBED], F32, kind="ExternalInput")
    d_wk = nc.dram_tensor("wk", [16, 128, EMBED], F32, kind="ExternalInput")
    d_wo = nc.dram_tensor("wo", [16, 128, EMBED], F32, kind="ExternalInput")
    d_wv = nc.dram_tensor("wv", [4, EMBED, 512], F32, kind="ExternalInput")
    d_wg1 = nc.dram_tensor("wg1", [128, EMBED], F32, kind="ExternalInput")
    d_wg2 = nc.dram_tensor("wg2", [128, EMBED], F32, kind="ExternalInput")
    d_rwt = nc.dram_tensor("rwt", [KC, 128, NE], F32, kind="ExternalInput")
    d_biasrep = nc.dram_tensor("biasrep", [128, 8 * NE], F32, kind="ExternalInput")
    d_nwcol = nc.dram_tensor("nwcol", [128, HEADS], F32, kind="ExternalInput")
    d_wue = nc.dram_tensor("wue", [NEX, EKC, 128, EMBED], F32, kind="ExternalInput")
    d_wge = nc.dram_tensor("wge", [NEX, EKC, 128, EMBED], F32, kind="ExternalInput")
    d_wdp = nc.dram_tensor("wdp", [NPAIR, 16, 2, 128, EKC * 128], F32,
                           kind="ExternalInput")
    d_cosb = nc.dram_tensor("cosb", [128, TPC], BF16, kind="ExternalInput")
    d_sinb = nc.dram_tensor("sinb", [128, TPC], BF16, kind="ExternalInput")
    d_dt = nc.dram_tensor("dt_c", [128, HEADS * C], BF16, kind="ExternalInput")
    d_dqr = nc.dram_tensor("dqr", [128, HEADS * C], BF16, kind="ExternalInput")
    d_dkcol = nc.dram_tensor("dkcol", [128, HEADS], F32, kind="ExternalInput")
    d_lamcol = nc.dram_tensor("lamcol", [128, HEADS], F32, kind="ExternalInput")
    d_lamj = nc.dram_tensor("lamj", [128, HEADS * NCH], F32, kind="ExternalInput")
    d_identb = nc.dram_tensor("identb", [128, 128], BF16, kind="ExternalInput")
    d_permb = nc.dram_tensor("permb", [128, 128], BF16, kind="ExternalInput")
    d_ohb = nc.dram_tensor("ohb", [128, HEADS * 16], BF16, kind="ExternalInput")
    d_rsel = nc.dram_tensor("rsel", [16, HEADS * 128], BF16, kind="ExternalInput")
    d_onesf = nc.dram_tensor("onesf", [128, 1], F32, kind="ExternalInput")
    d_onesq = nc.dram_tensor("onesq", [128, 128], F32, kind="ExternalInput")
    d_onesrow = nc.dram_tensor("onesrow", [1, 128], F32, kind="ExternalInput")
    d_cvec = nc.dram_tensor("cvec", [128, 8], F32, kind="ExternalInput")
    d_wcol3 = nc.dram_tensor("wcol3", [128, 3 * HEADS], F32, kind="ExternalInput")
    d_poff = nc.dram_tensor("poff", [1, 4], I32, kind="ExternalInput")
    d_out = nc.dram_tensor("out_t", [EMBED, TPC], F32, kind="ExternalOutput")
    d_ccin = nc.dram_tensor("cc_in", [EMBED, 128], F32)
    d_ccout = nc.dram_tensor("cc_out", [NCORES * EMBED, 128], F32)

    b = B(nc)

    def tap(name, ap_fn, reads, shape, dtype):
        if name not in _DEBUG_TAPS:
            return
        d = nc.dram_tensor("tap_" + name, shape, dtype, kind="ExternalOutput")
        b.emit("sync", lambda g, d=d, a=ap_fn(): g.dma_start(out=d.ap(), in_=a),
               reads=list(reads), dma=True, dkey="out")
    es = ExitStack()
    with es:
        b.init_sems(es)
        sb = lambda name, shape, dt, **kw: es.enter_context(
            nc.sbuf_tensor(name, shape, dt, **kw))
        t_cvec = sb("cvec_s", [128, 8], F32)
        t_identb = sb("identb_s", [128, 128], BF16)
        t_permb = sb("permb_s", [128, 128], BF16)
        t_ohb = sb("ohb_s", [128, HEADS * 16], BF16)
        t_rsel = sb("rsel_s", [16, HEADS * 128], BF16)
        t_onesf = sb("onesf_s", [128, 1], F32)
        t_onesq = sb("onesq_s", [128, 128], F32)
        t_onesrow = sb("onesrow_s", [1, 128], F32)
        t_biasrep = sb("biasrep_s", [128, 8 * NE], F32)
        t_nwcol = sb("nwcol_s", [128, HEADS], F32)
        t_dkcol = sb("dkcol_s", [128, HEADS], F32)
        t_lamcol = sb("lamcol_s", [128, HEADS], F32)
        t_lamj = sb("lamj_s", [128, HEADS * NCH], F32)
        t_wcol3 = sb("wcol3_s", [128, 3 * HEADS], F32)
        t_poff = sb("poff_s", [1, 4], I32)
        t_rwt = sb("rwt_s", [128, KC * NE], F32)
        t_g1 = sb("g1_s", [128, TPC], BF16)
        t_stg = [sb(f"stg{i}", [128, 4096], BF16) for i in range(2)]
        ps = [es.enter_context(nc.psum_tensor(f"ps{i}", [128, 512], F32))
              for i in range(6)]
        psb2 = [es.enter_context(nc.psum_tensor(f"psb{i}", [128, 1024], BF16))
                for i in range(2)]
        psb = psb2[0]

        T = lambda: Trk()
        k_cvec, k_consts = T(), T()
        k_rwt, k_g1 = T(), T()
        k_stg = [T(), T()]
        k_ps = [T() for _ in range(6)]
        k_psb2 = [T(), T()]
        k_psb = k_psb2[0]

        def ld(dst_ap, src_ap, trk, queue="sync", dkey="main"):
            b.emit(queue, lambda g, d=dst_ap, s=src_ap: g.dma_start(out=d, in_=s),
                   writes=[trk], dma=True, dkey=dkey)

        ld(t_cvec[:, :], d_cvec.ap(), k_cvec)
        for dst, src in ((t_identb, d_identb), (t_permb, d_permb),
                         (t_ohb, d_ohb), (t_rsel, d_rsel), (t_onesf, d_onesf),
                         (t_onesq, d_onesq),
                         (t_onesrow, d_onesrow), (t_biasrep, d_biasrep),
                         (t_nwcol, d_nwcol), (t_dkcol, d_dkcol),
                         (t_lamcol, d_lamcol), (t_lamj, d_lamj),
                         (t_wcol3, d_wcol3), (t_poff, d_poff)):
            ld(dst[:, :], src.ap(), k_consts)
        ld(t_rwt.ap().rearrange("p (kc n) -> p kc n", kc=KC),
           d_rwt.ap().rearrange("kc p n -> p kc n"), k_rwt, dkey="rwt")

        b.barrier()
        esA = ExitStack()
        t_obuf = esA.enter_context(nc.sbuf_tensor("obuf", [128, HEADS * TPC], BF16))
        t_qbuf = esA.enter_context(nc.sbuf_tensor("qbuf", [128, HEADS * TPC], BF16))
        t_S = esA.enter_context(nc.sbuf_tensor("S_f", [128, HEADS * HD], F32))
        t_Sb = esA.enter_context(nc.sbuf_tensor("S_b", [128, HEADS * HD], BF16))
        k_obuf = [T() for _ in range(HEADS)]
        k_qbuf = [T() for _ in range(HEADS)]
        k_S = [T() for _ in range(HEADS)]
        k_Sb = [T() for _ in range(HEADS)]
        k_dqr, k_dt = T(), T()

        esA2 = ExitStack()
        t_kfm = esA2.enter_context(nc.sbuf_tensor("kfm", [128, HEADS * TPC], BF16))
        t_scb = esA2.enter_context(nc.sbuf_tensor("scb", [128, C], BF16))
        t_ktb = esA2.enter_context(nc.sbuf_tensor("ktb", [128, C], BF16))
        esX = ExitStack()
        t_xbf = esX.enter_context(nc.sbuf_tensor("xbf", [128, KC * TPC], BF16,
                                                 side="right"))
        k_xbf = T()
        k_kfm = [T() for _ in range(HEADS)]
        k_vtm = [T() for _ in range(NCH * 4)]
        k_scb, k_ktb = T(), T()

        ld(t_xbf.ap().rearrange("p (kc t) -> p kc t", kc=KC),
           d_xt.ap().rearrange("(kc p) t -> p kc t", p=128), k_xbf, queue="pool",
           dkey="xbf")

        esR = ExitStack()
        t_cos = esR.enter_context(nc.sbuf_tensor("cosb_s", [128, TPC], BF16, side="right"))
        t_sin = esR.enter_context(nc.sbuf_tensor("sinb_s", [128, TPC], BF16, side="right"))
        t_qsil = esR.enter_context(nc.sbuf_tensor("qsil", [128, TPC], BF16, side="right"))
        t_rt1 = esR.enter_context(nc.sbuf_tensor("rt1", [128, TPC], BF16, side="right"))
        t_rt2 = esR.enter_context(nc.sbuf_tensor("rt2", [128, TPC], BF16, side="right"))
        k_cos, k_sin, k_qsil, k_rt1, k_rt2 = T(), T(), T(), T(), T()
        ld(t_cos[:, :], d_cosb.ap(), k_cos, dkey="cos")
        ld(t_sin[:, :], d_sinb.ap(), k_sin, dkey="sin")

        def mm_sweep(bank, lhs_fn, rhs_fn, n_k, reads):
            fns = []
            for kci in range(n_k):
                fns.append(lambda t, l=lhs_fn(kci), r=rhs_fn(kci),
                           st=(kci == 0), sp=(kci == n_k - 1), bk=bank:
                           t.matmul(ps[bk][:, :], l, r, start=st, stop=sp))
            return b.emit("pe", None, fns=fns, reads=list(reads),
                          writes=[k_ps[bank]])

        def qk_phase(dram_w, dest, k_dest):
            for m in range(HEADS):
                sg = (m // 2) % 2
                moff = (m % 2) * EMBED
                if m % 2 == 0:
                    b.emit("pool", lambda g, m=m, sg=sg: g.dma_start(
                        out=t_stg[sg][:, 0:2 * EMBED].rearrange(
                            "p (m f) -> p m f", m=2),
                        in_=dram_w.ap()[m:m + 2, :, :].rearrange(
                            "m p f -> p m f")),
                        writes=[k_stg[sg]], dma=True, dkey=f"stg{sg}")
                for half in range(2):
                    bank = half
                    mm_sweep(bank,
                             lambda kci, sg=sg, moff=moff:
                             t_stg[sg][:, moff + kci * 128: moff + (kci + 1) * 128],
                             lambda kci, half=half: t_xbf[:, kci * TPC + half * 512:
                                                          kci * TPC + half * 512 + 512],
                             KC, (k_stg[sg], k_xbf))
                    b.emit("act", lambda s, bank=bank, half=half: s.activation(
                        t_qsil[:, half * 512:half * 512 + 512], ps[bank][:, :],
                        AF.Silu), reads=[k_ps[bank]], writes=[k_qsil])
                    b.emit("pe", lambda t, half=half: t.matmul(
                        ps[2 + half][:, :], t_permb[:, :],
                        t_qsil[:, half * 512:half * 512 + 512],
                        start=True, stop=True),
                        reads=[k_qsil, k_consts], writes=[k_ps[2 + half]])
                b.emit("dve", lambda v, m=m: v.tensor_tensor(
                    t_rt1[:, :], t_qsil[:, :], t_cos[:, :], ALU.mult),
                    reads=[k_qsil, k_cos], writes=[k_rt1])
                for half in range(2):
                    b.emit("dve", lambda v, half=half: v.tensor_tensor(
                        t_rt2[:, half * 512:half * 512 + 512],
                        ps[2 + half][:, :],
                        t_sin[:, half * 512:half * 512 + 512], ALU.mult),
                        reads=[k_ps[2 + half], k_sin], writes=[k_rt2])
                b.emit("dve", lambda v, m=m: v.tensor_tensor(
                    dest[:, m * TPC:(m + 1) * TPC], t_rt1[:, :], t_rt2[:, :],
                    ALU.add), reads=[k_rt1, k_rt2], writes=[k_dest[m]])

        qk_phase(d_wq, t_qbuf, k_qbuf)
        b.barrier()
        qk_phase(d_wk, t_kfm, k_kfm)
        b.barrier()
        esR.close()
        tap("qk_q", lambda: t_qbuf[:, :], k_qbuf, [128, HEADS * TPC], BF16)
        tap("qk_k", lambda: t_kfm[:, :], k_kfm, [128, HEADS * TPC], BF16)

        b.emit("pool", lambda g: g.dma_start(
            out=t_stg[0][:, :EMBED], in_=d_wg1.ap()), writes=[k_stg[0]],
            dma=True, dkey="stg0")
        for half in range(2):
            mm_sweep(half,
                     lambda kci: t_stg[0][:, kci * 128:(kci + 1) * 128],
                     lambda kci, half=half: t_xbf[:, kci * TPC + half * 512:
                                                  kci * TPC + half * 512 + 512],
                     KC, (k_stg[0], k_xbf))
            b.emit("act", lambda s, half=half: s.copy(
                t_g1[:, half * 512:half * 512 + 512], ps[half][:, :]),
                reads=[k_ps[half]], writes=[k_g1])

        tap("g1", lambda: t_g1[:, :], [k_g1], [128, TPC], BF16)
        esV = ExitStack()
        t_vtm = esV.enter_context(nc.sbuf_tensor("vtm", [128, NCH * HEADS * HD],
                                                 BF16))
        for nf in range(4):
            for kh in range(2):
                b.emit("pool", lambda g, nf=nf, kh=kh: g.dma_start(
                    out=t_stg[kh].ap().rearrange("p (kc f) -> p kc f", kc=8),
                    in_=d_wv.ap()[nf, kh * 1024:(kh + 1) * 1024, :]
                    .rearrange("(kc p) f -> p kc f", p=128)),
                    writes=[k_stg[kh]], dma=True, dkey=f"stg{kh}")
            for tc in range(NCH):
                bank = tc % 2
                fns = []
                for kci in range(KC):
                    kh, kloc = kci // 8, kci % 8
                    fns.append(lambda t, kci=kci, kh=kh, kloc=kloc, tc=tc,
                               bank=bank: t.matmul(
                                   ps[bank][:, :],
                                   t_xbf[:, kci * TPC + tc * 128:
                                         kci * TPC + tc * 128 + 128],
                                   t_stg[kh][:, kloc * 512:(kloc + 1) * 512],
                                   start=(kci == 0), stop=(kci == KC - 1)))
                b.emit("pe", None, fns=fns,
                       reads=[k_stg[0], k_stg[1], k_xbf], writes=[k_ps[bank]])
                b.emit("act", lambda s, tc=tc, nf=nf, bank=bank: s.copy(
                    t_vtm[:, (tc * HEADS + nf * 4) * HD:
                          (tc * HEADS + nf * 4) * HD + 512], ps[bank][:, :]),
                    reads=[k_ps[bank]], writes=[k_vtm[tc * 4 + nf]])
        esX.close()
        t_dqr = esV.enter_context(nc.sbuf_tensor("dqr_s", [128, HEADS * C], BF16))
        t_dt = esV.enter_context(nc.sbuf_tensor("dt_s", [128, HEADS * C], BF16))
        ld(t_dqr[:, :], d_dqr.ap(), k_dqr, dkey="dqr")
        ld(t_dt[:, :], d_dt.ap(), k_dt, dkey="dtm")

        b.barrier()
        tap("v", lambda: t_vtm[:, :], k_vtm, [128, NCH * HEADS * HD], BF16)
        t_scg = [esV.enter_context(nc.sbuf_tensor(f"scg{i}", [128, 8 * C], BF16))
                 for i in range(2)]
        t_ktg = [esV.enter_context(nc.sbuf_tensor(f"ktg{i}", [128, 8 * C], BF16))
                 for i in range(2)]
        k_scg = [T(), T()]
        k_ktg = [T(), T()]

        def qsl(h, j):
            return t_qbuf[:, h * TPC + j * C: h * TPC + (j + 1) * C]

        def ksl(h, j):
            return t_kfm[:, h * TPC + j * C: h * TPC + (j + 1) * C]

        def vsl(h, j):
            return t_vtm[:, (j * HEADS + h) * HD:(j * HEADS + h + 1) * HD]

        def scan_front(g):
            j, hg = g // 2, g % 2
            hs = [hg * 8 + i for i in range(8)]
            par = g % 2
            pkt, kkt = psb2[par], k_psb2[par]
            b.emit("pe", None, fns=[
                lambda t, h=h, i=i, j=j, pkt=pkt: t.transpose(
                    pkt[:, i * C:(i + 1) * C], ksl(h, j), t_identb[:, :])
                for i, h in enumerate(hs)],
                reads=[k_kfm[h] for h in hs] + [k_consts], writes=[kkt])
            b.emit("act", None, fns=[
                lambda s, h=h, i=i, pkt=pkt, par=par: s.activation(
                    t_ktg[par][:, i * C:(i + 1) * C], pkt[:, i * C:(i + 1) * C],
                    AF.Copy, scale=t_dkcol[:, h:h + 1])
                for i, h in enumerate(hs)],
                reads=[kkt, k_consts], writes=[k_ktg[par]])
            b.emit("pe", None, fns=[
                lambda t, h=h, i=i, j=j: t.matmul(
                    ps[i // 4][:, (i % 4) * C:(i % 4) * C + C],
                    ksl(h, j), qsl(h, j), start=True, stop=True,
                    skip_group_check=True)
                for i, h in enumerate(hs)],
                reads=[k_kfm[h] for h in hs] + [k_qbuf[h] for h in hs],
                writes=[k_ps[0], k_ps[1]])
            b.emit("dve", None, fns=[
                lambda v, h=h, i=i, par=par: v.tensor_tensor(
                    t_scg[par][:, i * C:(i + 1) * C],
                    ps[i // 4][:, (i % 4) * C:(i % 4) * C + C],
                    t_dt[:, h * C:(h + 1) * C], ALU.mult)
                for i, h in enumerate(hs)],
                reads=[k_ps[0], k_ps[1], k_dt], writes=[k_scg[par]])
            b.emit("dve", None, fns=[
                lambda v, h=h, j=j: v.tensor_tensor(
                    qsl(h, j), qsl(h, j), t_dqr[:, h * C:(h + 1) * C], ALU.mult)
                for h in hs],
                reads=[k_dqr], writes=[k_qbuf[h] for h in hs])

        def scan_back(g):
            j, hg = g // 2, g % 2
            hs = [hg * 8 + i for i in range(8)]
            par = g % 2
            kvs = [k_vtm[j * 4 + h // 4] for h in hs]
            ofns = []
            for i, h in enumerate(hs):
                osl = lambda i=i: ps[2 + i // 4][:, (i % 4) * C:(i % 4) * C + C]
                if j == 0:
                    ofns.append(lambda t, h=h, i=i, j=j, osl=osl, par=par:
                                t.matmul(osl(), vsl(h, j),
                                         t_scg[par][:, i * C:(i + 1) * C],
                                         start=True, stop=True,
                                         skip_group_check=True))
                else:
                    ofns.append(lambda t, h=h, i=i, j=j, osl=osl, par=par:
                                t.matmul(osl(), vsl(h, j),
                                         t_scg[par][:, i * C:(i + 1) * C],
                                         start=True, stop=False,
                                         skip_group_check=True))
                    ofns.append(lambda t, h=h, i=i, j=j, osl=osl: t.matmul(
                        osl(), t_Sb[:, h * HD:(h + 1) * HD], qsl(h, j),
                        start=False, stop=True, skip_group_check=True))
            b.emit("pe", None, fns=ofns,
                   reads=kvs + [k_scg[par]] + [k_Sb[h] for h in hs]
                   + [k_qbuf[h] for h in hs],
                   writes=[k_ps[2], k_ps[3]])
            b.emit("act", None, fns=[
                lambda s, h=h, i=i, j=j: s.copy(
                    t_obuf[:, h * TPC + j * C: h * TPC + (j + 1) * C],
                    ps[2 + i // 4][:, (i % 4) * C:(i % 4) * C + C])
                for i, h in enumerate(hs)],
                reads=[k_ps[2], k_ps[3]], writes=[k_obuf[h] for h in hs])
            b.emit("pe", None, fns=[
                lambda t, h=h, i=i, j=j, par=par: t.matmul(
                    ps[4 + i // 4][:, (i % 4) * C:(i % 4) * C + C],
                    t_ktg[par][:, i * C:(i + 1) * C], vsl(h, j),
                    start=True, stop=True, skip_group_check=True)
                for i, h in enumerate(hs)],
                reads=[k_ktg[par]] + kvs, writes=[k_ps[4], k_ps[5]])
            sfns = []
            for i, h in enumerate(hs):
                asl = lambda i=i: ps[4 + i // 4][:, (i % 4) * C:(i % 4) * C + C]
                if j == 0:
                    sfns.append(lambda v, h=h, asl=asl: v.tensor_copy(
                        t_S[:, h * HD:(h + 1) * HD], asl()))
                else:
                    sfns.append(lambda v, h=h, asl=asl: v.scalar_tensor_tensor(
                        t_S[:, h * HD:(h + 1) * HD], t_S[:, h * HD:(h + 1) * HD],
                        t_lamcol[:, h:h + 1], asl(), ALU.mult, ALU.add))
            b.emit("dve", None, fns=sfns,
                   reads=[k_ps[4], k_ps[5], k_consts],
                   writes=[k_S[h] for h in hs])
            if j < NCH - 1:
                b.emit("act", None, fns=[
                    lambda s, h=h: s.copy(
                        t_Sb[:, h * HD:(h + 1) * HD],
                        t_S[:, h * HD:(h + 1) * HD])
                    for h in hs],
                    reads=[k_S[h] for h in hs],
                    writes=[k_Sb[h] for h in hs])

        for g in range(2 * NCH + 1):
            if g < 2 * NCH:
                scan_front(g)
            if g >= 1:
                scan_back(g - 1)
        tap("o_scan", lambda: t_obuf[:, :], k_obuf, [128, HEADS * TPC], BF16)
        tap("s_loc", lambda: t_S[:, :], k_S, [128, HEADS * HD], F32)
        esV.close()
        esA2.close()

        b.barrier()
        esC = ExitStack()
        t_G = [esC.enter_context(nc.sbuf_tensor(f"G{i}", [128, HEADS * HD], F32))
               for i in range(3)]
        t_tmpA = esC.enter_context(nc.sbuf_tensor("tmpA", [128, HD], F32))
        t_tmpB = esC.enter_context(nc.sbuf_tensor("tmpB", [128, HD], F32))
        t_Sin = esC.enter_context(nc.sbuf_tensor("Sin", [128, HEADS * HD], BF16))
        t_qt2 = [esC.enter_context(nc.sbuf_tensor(f"qt{i}", [128, C], BF16))
                 for i in range(2)]
        t_gsa = esC.enter_context(nc.sbuf_tensor("gsa", [128, HEADS * TPC], BF16))
        t_sq = esC.enter_context(nc.sbuf_tensor("sqg", [128, TPC], BF16))
        t_rstd = esC.enter_context(nc.sbuf_tensor("rstdg", [16, TPC], F32))
        t_rstdb = esC.enter_context(nc.sbuf_tensor("rstdgb", [16, TPC], BF16))
        k_G = [T() for _ in range(3)]
        k_tmpA, k_tmpB, k_Sin = T(), T(), T()
        k_qt2 = [T(), T()]
        k_gsa, k_sq, k_rstd, k_rstdb = T(), T(), T(), T()
        k_ccin, k_ccout = T(), T()

        b.emit("sync", lambda g: g.dma_start(
            out=d_ccin.ap().rearrange("(h p) e -> p h e", p=128),
            in_=t_S.ap().rearrange("p (h e) -> p h e", h=HEADS)),
            reads=list(k_S), writes=[k_ccin], dma=True, dkey="ccin")

        e = b.eng["pool"]
        b._waits(e, [k_ccin.w])
        cc_sem = b.cc_sem
        e["script"].append(lambda g, s=cc_sem: g.collective_compute(
            "AllGather", ALU.bypass, replica_groups=[list(range(NCORES))],
            ins=[d_ccin.ap().opt()], outs=[d_ccout.ap().opt()]).then_inc(s, 1))
        k_ccout.w = (cc_sem, 1)

        b.emit("pool", lambda g: g.dma_start(
            out=t_stg[0][:, :EMBED], in_=d_wg2.ap()),
            writes=[k_stg[0]], dma=True, dkey="stg0")
        k_wg2v = k_stg[0]
        for h in range(HEADS):
            for half in range(2):
                bank = (h * 2 + half) % 4
                b.emit("pe", lambda t, h=h, half=half, bank=bank: t.matmul(
                    ps[bank][:, :], t_stg[0][:, h * 128:(h + 1) * 128],
                    t_g1[:, half * 512:half * 512 + 512], start=True, stop=True),
                    reads=[k_wg2v, k_g1], writes=[k_ps[bank]])
                b.emit("act", lambda s, h=h, half=half, bank=bank: s.activation(
                    t_gsa[:, h * TPC + half * 512: h * TPC + half * 512 + 512],
                    ps[bank][:, :], AF.Sigmoid),
                    reads=[k_ps[bank]], writes=[k_gsa])

        esy = b.eng["sync"]
        dsem = esy["dsems"]["g"][0] if "g" in esy["dsems"] else None
        if dsem is None:
            raise RuntimeError("missing sync g key")

        def fetch_G(g, s):
            with g.register("gofs") as gofs:
                for i in range(3):
                    g.reg_load(gofs, t_poff[:1, i:i + 1])
                    g.dma_start(
                        out=t_G[i].ap().rearrange("p (h e) -> p h e", h=HEADS),
                        in_=bass.AP(d_ccout, gofs,
                                    [[128, 128], [128 * 128, HEADS], [1, 128]]),
                    ).then_inc(s, 16)
        b._waits(esy, [k_ccout.w, k_consts.w])
        esy["dsems"]["g"][1] += 48
        g_tok = (dsem, esy["dsems"]["g"][1])
        esy["script"].append(lambda g, s=dsem: fetch_G(g, s))
        for kk in k_G:
            kk.w = g_tok

        for h in range(HEADS):
            b.emit("dve", lambda v, h=h: v.tensor_scalar(
                t_tmpA[:, :], t_G[0][:, h * HD:(h + 1) * HD],
                t_wcol3[:, h:h + 1], None, ALU.mult),
                reads=[k_G[0], k_consts], writes=[k_tmpA])
            b.emit("dve", lambda v, h=h: v.scalar_tensor_tensor(
                t_tmpB[:, :], t_G[1][:, h * HD:(h + 1) * HD],
                t_wcol3[:, HEADS + h:HEADS + h + 1], t_tmpA[:, :],
                ALU.mult, ALU.add),
                reads=[k_G[1], k_consts, k_tmpA], writes=[k_tmpB])
            b.emit("dve", lambda v, h=h: v.scalar_tensor_tensor(
                t_tmpA[:, :], t_G[2][:, h * HD:(h + 1) * HD],
                t_wcol3[:, 2 * HEADS + h:2 * HEADS + h + 1], t_tmpB[:, :],
                ALU.mult, ALU.add),
                reads=[k_G[2], k_consts, k_tmpB], writes=[k_tmpA])
            b.emit("act", lambda s, h=h: s.copy(
                t_Sin[:, h * HD:(h + 1) * HD], t_tmpA[:, :]),
                reads=[k_tmpA], writes=[k_Sin])

        t_qth = [esC.enter_context(nc.sbuf_tensor(f"qth{i}", [128, NCH * C], BF16))
                 for i in range(2)]
        k_qth = [T(), T()]
        lj_np = lam_C[:, None] ** np.arange(NCH)[None, :]
        for h in range(HEADS):
            js = [j for j in range(NCH) if lj_np[h, j] >= 1e-30]
            if not js:
                continue
            par = h % 2
            b.emit("dve", None, fns=[
                lambda v, h=h, j=j, i=i, par=par: v.tensor_scalar(
                    t_qth[par][:, i * C:(i + 1) * C],
                    t_qbuf[:, h * TPC + j * C: h * TPC + (j + 1) * C],
                    t_lamj[:, h * NCH + j: h * NCH + j + 1], None, ALU.mult)
                for i, j in enumerate(js)],
                reads=[k_qbuf[h], k_consts], writes=[k_qth[par]])
            b.emit("pe", None, fns=[
                lambda t, h=h, i=i, par=par: t.matmul(
                    ps[2 * par + i // 4][:, (i % 4) * C:(i % 4) * C + C],
                    t_Sin[:, h * HD:(h + 1) * HD],
                    t_qth[par][:, i * C:(i + 1) * C],
                    start=True, stop=True, skip_group_check=True)
                for i, j in enumerate(js)],
                reads=[k_Sin, k_qth[par]],
                writes=[k_ps[2 * par], k_ps[2 * par + 1]])
            b.emit("dve", None, fns=[
                lambda v, h=h, j=j, i=i, par=par: v.tensor_tensor(
                    t_obuf[:, h * TPC + j * C: h * TPC + (j + 1) * C],
                    t_obuf[:, h * TPC + j * C: h * TPC + (j + 1) * C],
                    ps[2 * par + i // 4][:, (i % 4) * C:(i % 4) * C + C], ALU.add)
                for i, j in enumerate(js)],
                reads=[k_ps[2 * par], k_ps[2 * par + 1]],
                writes=[k_obuf[h]])
        tap("o_corr", lambda: t_obuf[:, :], k_obuf, [128, HEADS * TPC], BF16)
        tap("sin_t", lambda: t_Sin[:, :], [k_Sin], [128, HEADS * HD], BF16)
        for h in range(HEADS):
            b.emit("dve", lambda v, h=h: v.tensor_tensor(
                t_obuf[:, h * TPC:(h + 1) * TPC],
                t_obuf[:, h * TPC:(h + 1) * TPC],
                t_gsa[:, h * TPC:(h + 1) * TPC], ALU.mult),
                reads=[k_gsa], writes=[k_obuf[h]])
            b.emit("act", lambda s, h=h: s.activation(
                t_sq[:, :], t_obuf[:, h * TPC:(h + 1) * TPC], AF.Square),
                reads=[k_obuf[h]], writes=[k_sq])
            for half in range(2):
                b.emit("pe", lambda t, h=h, half=half: t.matmul(
                    ps[2 + half][0:16, :], t_ohb[:, h * 16:(h + 1) * 16],
                    t_sq[:, half * 512:half * 512 + 512],
                    start=(h == 0), stop=(h == HEADS - 1)),
                    reads=[k_sq, k_consts], writes=[k_ps[2 + half]])
        tap("xg", lambda: t_obuf[:, :], k_obuf, [128, HEADS * TPC], BF16)
        for half in range(2):
            b.emit("act", lambda s, half=half: s.activation(
                t_rstd[:, half * 512:half * 512 + 512], ps[2 + half][0:16, :],
                AF.Sqrt, bias=t_cvec[0:16, 1:2], scale=t_cvec[0:16, 0:1]),
                reads=[k_ps[2 + half], k_cvec], writes=[k_rstd])
        b.emit("dve", lambda v: v.reciprocal(t_rstd[:, :], t_rstd[:, :]),
               reads=[], writes=[k_rstd])
        b.emit("act", lambda s: s.copy(t_rstdb[:, :], t_rstd[:, :]),
               reads=[k_rstd], writes=[k_rstdb])
        tap("rstd", lambda: t_rstd[:, :], [k_rstd], [16, TPC], F32)
        for h in range(HEADS):
            for half in range(2):
                bank = half
                b.emit("pe", lambda t, h=h, half=half, bank=bank: t.matmul(
                    ps[bank][:, :], t_rsel[:, h * 128:(h + 1) * 128],
                    t_rstdb[:, half * 512:half * 512 + 512],
                    start=True, stop=True),
                    reads=[k_rstdb, k_consts], writes=[k_ps[bank]])
                b.emit("dve", lambda v, h=h, half=half, bank=bank:
                       v.scalar_tensor_tensor(
                           t_obuf[:, h * TPC + half * 512: h * TPC + half * 512 + 512],
                           t_obuf[:, h * TPC + half * 512: h * TPC + half * 512 + 512],
                           t_nwcol[:, h:h + 1], ps[bank][:, :],
                           ALU.mult, ALU.mult),
                       reads=[k_ps[bank], k_consts], writes=[k_obuf[h]])
        tap("o_n", lambda: t_obuf[:, :], k_obuf, [128, HEADS * TPC], BF16)
        esC.close()
        b.barrier()

        b.barrier()
        t_f = sb("f_acc", [128, KC * TPC], F32, side="right")
        k_f = T()
        b.emit("sync", lambda g: g.dma_start(
            out=t_f.ap().rearrange("p (kc t) -> p kc t", kc=KC),
            in_=d_xt.ap().rearrange("(kc p) t -> p kc t", p=128)),
            writes=[k_f], dma=True, dkey="xf")
        for m in range(KC):
            sg = (m // 2) % 2
            moff = (m % 2) * EMBED
            if m % 2 == 0:
                b.emit("pool", lambda g, m=m, sg=sg: g.dma_start(
                    out=t_stg[sg][:, 0:2 * EMBED].rearrange("p (m f) -> p m f",
                                                            m=2),
                    in_=d_wo.ap()[m:m + 2, :, :].rearrange("m p f -> p m f")),
                    writes=[k_stg[sg]], dma=True, dkey=f"stg{sg}")
            for half in range(2):
                bank = half
                mm_sweep(bank,
                         lambda kci, sg=sg, moff=moff:
                         t_stg[sg][:, moff + kci * 128: moff + (kci + 1) * 128],
                         lambda kci, half=half: t_obuf[:, kci * TPC + half * 512:
                                                       kci * TPC + half * 512 + 512],
                         KC, tuple([k_stg[sg]] + k_obuf))
                b.emit("dve", lambda v, m=m, half=half, bank=bank: v.tensor_tensor(
                    t_f[:, m * TPC + half * 512: m * TPC + half * 512 + 512],
                    t_f[:, m * TPC + half * 512: m * TPC + half * 512 + 512],
                    ps[bank][:, :], ALU.add),
                    reads=[k_ps[bank]], writes=[k_f])
        tap("r1", lambda: t_f[:, :], [k_f], [128, KC * TPC], F32)
        esA.close()

        def layer_norm(t_x, k_x, sq_scope_name, kc_done=None):
            esL = ExitStack()
            t_sqf2 = [esL.enter_context(nc.sbuf_tensor(
                f"sqf{i}_{sq_scope_name}", [128, TPC], F32R)) for i in range(2)]
            t_xr2 = [esL.enter_context(nc.sbuf_tensor(
                f"xr{i}_{sq_scope_name}", [128, TPC], F32R)) for i in range(2)]
            t_oqr = esL.enter_context(nc.sbuf_tensor(
                f"oqr_{sq_scope_name}", [128, 128], F32R))
            t_st = esL.enter_context(nc.sbuf_tensor(
                f"st_{sq_scope_name}", [1, 4 * TPC], F32))
            k_sqf2 = [T(), T()]
            k_xr2 = [T(), T()]
            k_oqr = T()
            k_st = T()
            b.emit("pool", lambda g: g.tensor_copy(t_oqr[:, :], t_onesq[:, :]),
                   reads=[k_consts], writes=[k_oqr])
            for kci in range(KC):
                sqp = kci % 2
                b.emit("act", lambda s, kci=kci, sqp=sqp: s.activation(
                    t_sqf2[sqp][:, :], t_x[:, kci * TPC:(kci + 1) * TPC],
                    AF.Square), reads=[k_x], writes=[k_sqf2[sqp]])
                b.emit("pool", lambda g, kci=kci, sqp=sqp: g.tensor_copy(
                    t_xr2[sqp][:, :], t_x[:, kci * TPC:(kci + 1) * TPC]),
                    reads=[k_x], writes=[k_xr2[sqp]])
                fns = []
                for half in range(2):
                    fns.append(lambda t, kci=kci, half=half, sqp=sqp: t.matmul(
                        ps[4 + half][:, :], t_oqr[:, :],
                        t_xr2[sqp][:, half * 512:half * 512 + 512],
                        start=(kci == 0), stop=(kci == KC - 1),
                        skip_group_check=True))
                    fns.append(lambda t, kci=kci, half=half, sqp=sqp: t.matmul(
                        ps[2 + half][:, :], t_oqr[:, :],
                        t_sqf2[sqp][:, half * 512:half * 512 + 512],
                        start=(kci == 0), stop=(kci == KC - 1),
                        skip_group_check=True))
                b.emit("pe", None, fns=fns,
                       reads=[k_xr2[sqp], k_sqf2[sqp], k_oqr],
                       writes=[k_ps[2], k_ps[3], k_ps[4], k_ps[5]])
            for half in range(2):
                b.emit("dve", lambda v, half=half: v.tensor_scalar(
                    t_st[0:1, half * 512:half * 512 + 512],
                    ps[4 + half][0:1, :], t_cvec[0:1, 2:3], None, ALU.mult),
                    reads=[k_ps[4 + half], k_cvec], writes=[k_st])
                b.emit("dve", lambda v, half=half: v.tensor_scalar(
                    t_st[0:1, TPC + half * 512: TPC + half * 512 + 512],
                    ps[2 + half][0:1, :], t_cvec[0:1, 2:3], None, ALU.mult),
                    reads=[k_ps[2 + half], k_cvec], writes=[k_st])
            b.emit("dve", lambda v: v.tensor_tensor(
                t_st[0:1, 2 * TPC:3 * TPC], t_st[0:1, 0:TPC], t_st[0:1, 0:TPC],
                ALU.mult), reads=[], writes=[k_st])
            b.emit("dve", lambda v: v.tensor_tensor(
                t_st[0:1, 2 * TPC:3 * TPC], t_st[0:1, TPC:2 * TPC],
                t_st[0:1, 2 * TPC:3 * TPC], ALU.subtract),
                reads=[], writes=[k_st])
            b.emit("act", lambda s: s.activation(
                t_st[0:1, 2 * TPC:3 * TPC], t_st[0:1, 2 * TPC:3 * TPC],
                AF.Sqrt, bias=t_cvec[0:1, 3:4]),
                reads=[k_cvec], writes=[k_st])
            b.emit("dve", lambda v: v.reciprocal(
                t_st[0:1, 2 * TPC:3 * TPC], t_st[0:1, 2 * TPC:3 * TPC]),
                reads=[], writes=[k_st])
            b.emit("dve", lambda v: v.tensor_tensor(
                t_st[0:1, 3 * TPC:4 * TPC], t_st[0:1, 0:TPC],
                t_st[0:1, 2 * TPC:3 * TPC], ALU.mult),
                reads=[], writes=[k_st])
            for half in range(2):
                b.emit("pe", lambda t, half=half: t.matmul(
                    ps[4 + half][:, :], t_onesrow[:, :],
                    t_st[0:1, 2 * TPC + half * 512: 2 * TPC + half * 512 + 512],
                    start=True, stop=True),
                    reads=[k_st, k_consts], writes=[k_ps[4 + half]])
                b.emit("pe", lambda t, half=half: t.matmul(
                    ps[2 + half][:, :], t_onesrow[:, :],
                    t_st[0:1, 3 * TPC + half * 512: 3 * TPC + half * 512 + 512],
                    start=True, stop=True),
                    reads=[k_st, k_consts], writes=[k_ps[2 + half]])
            for kci in range(KC):
                for half in range(2):
                    sl = lambda kci=kci, half=half: t_x[
                        :, kci * TPC + half * 512: kci * TPC + half * 512 + 512]
                    b.emit("dve", lambda v, sl=sl, half=half: v.tensor_tensor(
                        sl(), sl(), ps[4 + half][:, :], ALU.mult),
                        reads=[k_ps[4 + half]], writes=[k_x])
                    b.emit("dve", lambda v, sl=sl, half=half: v.tensor_tensor(
                        sl(), sl(), ps[2 + half][:, :], ALU.subtract),
                        reads=[k_ps[2 + half]], writes=[k_x])
                if kc_done is not None:
                    kc_done(kci, k_x)
            esL.close()

        b.barrier()
        layer_norm(t_f, k_f, "ln1")
        tap("h1", lambda: t_f[:, :], [k_f], [128, KC * TPC], F32)

        esF = ExitStack()
        t_h1b = esF.enter_context(nc.sbuf_tensor("h1b", [128, KC * TPC], BF16))
        t_lsb = esF.enter_context(nc.sbuf_tensor("lsb", [128, NCH * NE], F32))
        t_ysb = esF.enter_context(nc.sbuf_tensor("ysb", [128, NCH * NE], F32))
        t_msk = esF.enter_context(nc.sbuf_tensor("msk", [128, NCH * NE], F32))
        t_red = esF.enter_context(nc.sbuf_tensor("red", [128, 4 * NCH], F32))
        t_ctm = esF.enter_context(nc.sbuf_tensor("ctm", [128, NCH * NE], BF16))
        t_cT = esF.enter_context(nc.sbuf_tensor("cT", [16, TPC], BF16))
        k_h1b, k_lsb, k_ysb, k_msk, k_red, k_ctm, k_cT = (
            T(), T(), T(), T(), T(), T(), T())
        b.emit("pool", lambda g: g.dma_start(out=t_h1b[:, :], in_=t_f[:, :]),
               reads=[k_f], writes=[k_h1b], dma=True, dkey="misc")

        tap("h1b", lambda: t_h1b[:, :], [k_h1b], [128, KC * TPC], BF16)
        NT = NCH * NE
        for tc in range(NCH):
            fns = []
            for kci in range(KC):
                fns.append(lambda t, kci=kci, tc=tc: t.matmul(
                    ps[0][:, tc * NE:(tc + 1) * NE],
                    t_f[:, kci * TPC + tc * 128: kci * TPC + tc * 128 + 128],
                    t_rwt[:, kci * NE:(kci + 1) * NE],
                    start=(kci == 0), stop=(kci == KC - 1),
                    skip_group_check=True))
            b.emit("pe", None, fns=fns, reads=[k_f, k_rwt], writes=[k_ps[0]])
        b.emit("act", lambda s: s.copy(t_lsb[:, :], ps[0][:, 0:NT]),
               reads=[k_ps[0]], writes=[k_lsb])
        b.emit("dve", lambda v: v.tensor_tensor(
            t_ysb[:, :], t_lsb[:, :], t_biasrep[:, :], ALU.add),
            reads=[k_lsb, k_consts], writes=[k_ysb])
        b.emit("dve", None, fns=[
            lambda v, tc=tc: v.reduce_max(
                t_red[:, tc * 4:tc * 4 + 1], t_ysb[:, tc * NE:(tc + 1) * NE],
                axis=AXX) for tc in range(NCH)],
            reads=[k_ysb], writes=[k_red])
        b.emit("dve", None, fns=[
            lambda v, tc=tc: v.tensor_single_scalar(
                t_msk[:, tc * NE:(tc + 1) * NE], t_ysb[:, tc * NE:(tc + 1) * NE],
                t_red[:, tc * 4:tc * 4 + 1], ALU.is_ge) for tc in range(NCH)],
            reads=[k_ysb, k_red], writes=[k_msk])
        b.emit("dve", lambda v: v.scalar_tensor_tensor(
            t_ysb[:, :], t_msk[:, :], t_cvec[:, 4:5], t_ysb[:, :],
            ALU.mult, ALU.add),
            reads=[k_msk, k_cvec], writes=[k_ysb])
        b.emit("dve", None, fns=[
            lambda v, tc=tc: v.reduce_max(
                t_red[:, tc * 4 + 1:tc * 4 + 2], t_ysb[:, tc * NE:(tc + 1) * NE],
                axis=AXX) for tc in range(NCH)],
            reads=[k_ysb], writes=[k_red])
        b.emit("dve", None, fns=[
            lambda v, tc=tc: v.tensor_single_scalar(
                t_ysb[:, tc * NE:(tc + 1) * NE], t_ysb[:, tc * NE:(tc + 1) * NE],
                t_red[:, tc * 4 + 1:tc * 4 + 2], ALU.is_ge) for tc in range(NCH)],
            reads=[k_red], writes=[k_ysb])
        b.emit("dve", lambda v: v.tensor_tensor(
            t_msk[:, :], t_msk[:, :], t_ysb[:, :], ALU.add),
            reads=[k_ysb], writes=[k_msk])
        b.emit("dve", None, fns=[
            lambda v, tc=tc: v.reduce_max(
                t_red[:, tc * 4 + 2:tc * 4 + 3], t_lsb[:, tc * NE:(tc + 1) * NE],
                axis=AXX) for tc in range(NCH)],
            reads=[k_lsb], writes=[k_red])
        b.emit("dve", None, fns=[
            lambda v, tc=tc: v.tensor_single_scalar(
                t_lsb[:, tc * NE:(tc + 1) * NE], t_lsb[:, tc * NE:(tc + 1) * NE],
                t_red[:, tc * 4 + 2:tc * 4 + 3], ALU.subtract)
            for tc in range(NCH)],
            reads=[k_red], writes=[k_lsb])
        b.emit("act", lambda s: s.activation(t_lsb[:, :], t_lsb[:, :], AF.Exp),
               reads=[], writes=[k_lsb])
        b.emit("dve", lambda v: v.tensor_tensor(
            t_lsb[:, :], t_lsb[:, :], t_msk[:, :], ALU.mult),
            reads=[k_msk], writes=[k_lsb])
        b.emit("dve", None, fns=[
            lambda v, tc=tc: v.reduce_sum(
                t_red[:, tc * 4 + 3:tc * 4 + 4], t_lsb[:, tc * NE:(tc + 1) * NE],
                axis=AXX) for tc in range(NCH)],
            reads=[k_lsb], writes=[k_red])
        b.emit("dve", None, fns=[
            lambda v, tc=tc: v.reciprocal(
                t_red[:, tc * 4 + 3:tc * 4 + 4], t_red[:, tc * 4 + 3:tc * 4 + 4])
            for tc in range(NCH)],
            reads=[], writes=[k_red])
        b.emit("dve", None, fns=[
            lambda v, tc=tc: v.tensor_scalar(
                t_ctm[:, tc * NE:(tc + 1) * NE], t_lsb[:, tc * NE:(tc + 1) * NE],
                t_red[:, tc * 4 + 3:tc * 4 + 4], None, ALU.mult)
            for tc in range(NCH)],
            reads=[k_lsb, k_red], writes=[k_ctm])
        b.emit("pe", None, fns=[
            lambda t, tc=tc: t.transpose(
                psb2[0][0:NE, tc * 128:(tc + 1) * 128],
                t_ctm[:, tc * NE:(tc + 1) * NE], t_identb[:, :])
            for tc in range(NCH)],
            reads=[k_ctm, k_consts], writes=[k_psb2[0]])
        b.emit("act", lambda s: s.copy(t_cT[:, :], psb2[0][0:NE, :]),
               reads=[k_psb2[0]], writes=[k_cT])
        tap("cT", lambda: t_cT[:, :], [k_cT], [16, TPC], BF16)

        t_usil = esF.enter_context(nc.sbuf_tensor("usil", [128, EKC * TPC], BF16))
        t_hb = [esF.enter_context(nc.sbuf_tensor(f"hb{i}", [128, EKC * TPC], BF16))
                for i in range(2)]
        t_cb = esF.enter_context(nc.sbuf_tensor("cb", [128, TPC], BF16))
        k_usil, k_cb = T(), T()
        k_hb = [T(), T()]

        for pr in range(NPAIR):
            b.barrier(2)
            for ei in range(2):
                e_idx = pr * 2 + ei
                for wgt, dram_w, is_u in ((0, d_wue, True), (1, d_wge, False)):
                    for mp in range(0, EKC, 2):
                        sg = (mp // 2) % 2
                        b.emit("pool", lambda g, e_idx=e_idx, mp=mp, sg=sg,
                               dram_w=dram_w: g.dma_start(
                                   out=t_stg[sg][:, 0:2 * EMBED].rearrange(
                                       "p (m f) -> p m f", m=2),
                                   in_=dram_w.ap()[e_idx, mp:mp + 2, :, :]
                                   .rearrange("m p f -> p m f")),
                               writes=[k_stg[sg]], dma=True, dkey=f"stg{sg}")
                        for mi in range(2):
                            m = mp + mi
                            for half in range(2):
                                bank = half
                                mm_sweep(
                                    bank,
                                    lambda kci, sg=sg, mi=mi:
                                    t_stg[sg][:, mi * EMBED + kci * 128:
                                              mi * EMBED + (kci + 1) * 128],
                                    lambda kci, half=half:
                                    t_h1b[:, kci * TPC + half * 512:
                                          kci * TPC + half * 512 + 512],
                                    KC, (k_stg[sg], k_h1b))
                                dst = lambda m=m, half=half: (
                                    m * TPC + half * 512, m * TPC + half * 512 + 512)
                                if is_u:
                                    b.emit("act", lambda s, d=dst(), bank=bank:
                                           s.activation(t_usil[:, d[0]:d[1]],
                                                        ps[bank][:, :], AF.Silu),
                                           reads=[k_ps[bank]], writes=[k_usil])
                                else:
                                    b.emit("dve", lambda v, d=dst(), bank=bank,
                                           ei=ei: v.tensor_tensor(
                                               t_hb[ei][:, d[0]:d[1]],
                                               ps[bank][:, :],
                                               t_usil[:, d[0]:d[1]], ALU.mult),
                                           reads=[k_ps[bank], k_usil],
                                           writes=[k_hb[ei]])
                if pr == 0 and ei == 0:
                    tap("usil0", lambda: t_usil[:, :], [k_usil],
                        [128, EKC * TPC], BF16)
                    tap("hb0raw", lambda: t_hb[0][:, :], [k_hb[0]],
                        [128, EKC * TPC], BF16)
                if e_idx < NE:
                    for half in range(2):
                        bank = 2 + half
                        b.emit("pe", lambda t, e_idx=e_idx, half=half, bank=bank:
                               t.matmul(ps[bank][:, :],
                                        t_rsel[:, e_idx * 128:(e_idx + 1) * 128],
                                        t_cT[:, half * 512:half * 512 + 512],
                                        start=True, stop=True),
                               reads=[k_cT, k_consts], writes=[k_ps[bank]])
                        b.emit("act", lambda s, half=half, bank=bank: s.copy(
                            t_cb[:, half * 512:half * 512 + 512],
                            ps[bank][:, :]),
                            reads=[k_ps[bank]], writes=[k_cb])
                    for m in range(EKC):
                        b.emit("dve", lambda v, m=m, ei=ei: v.tensor_tensor(
                            t_hb[ei][:, m * TPC:(m + 1) * TPC],
                            t_hb[ei][:, m * TPC:(m + 1) * TPC],
                            t_cb[:, :], ALU.mult),
                            reads=[k_cb], writes=[k_hb[ei]])
            if pr == 0:
                tap("hb0", lambda: t_hb[0][:, :], [k_hb[0]],
                    [128, EKC * TPC], BF16)
            for mp in range(0, 16, 2):
                sg = (mp // 2) % 2
                b.emit("pool", lambda g, pr=pr, mp=mp, sg=sg: g.dma_start(
                    out=t_stg[sg][:, 0:4096].rearrange("p (m i f) -> p m i f",
                                                       m=2, i=2),
                    in_=d_wdp.ap()[pr, mp:mp + 2, :, :, :]
                    .rearrange("m i p f -> p m i f")),
                    writes=[k_stg[sg]], dma=True, dkey=f"stg{sg}")
                for mi in range(2):
                    m = mp + mi
                    for half in range(2):
                        bank = half
                        fns = []
                        for ei in range(2):
                            for ke in range(EKC):
                                fns.append(lambda t, sg=sg, mi=mi, ei=ei, ke=ke,
                                           half=half, bank=bank: t.matmul(
                                               ps[bank][:, :],
                                               t_stg[sg][:, (mi * 2 + ei) * 1024 +
                                                         ke * 128:
                                                         (mi * 2 + ei) * 1024 +
                                                         (ke + 1) * 128],
                                               t_hb[ei][:, ke * TPC + half * 512:
                                                        ke * TPC + half * 512 + 512],
                                               start=(ei == 0 and ke == 0),
                                               stop=(ei == 1 and ke == EKC - 1)))
                        b.emit("pe", None, fns=fns,
                               reads=[k_stg[sg], k_hb[0], k_hb[1]],
                               writes=[k_ps[bank]])
                        b.emit("dve", lambda v, m=m, half=half, bank=bank:
                               v.tensor_tensor(
                                   t_f[:, m * TPC + half * 512:
                                       m * TPC + half * 512 + 512],
                                   t_f[:, m * TPC + half * 512:
                                       m * TPC + half * 512 + 512],
                                   ps[bank][:, :], ALU.add),
                               reads=[k_ps[bank]], writes=[k_f])
        tap("f", lambda: t_f[:, :], [k_f], [128, KC * TPC], F32)
        esF.close()

        b.barrier()
        out_toks = []

        def emit_out_kc(kci, k_x):
            tok = b.emit("sync", lambda g, kci=kci: g.dma_start(
                out=d_out.ap()[kci * 128:(kci + 1) * 128, :],
                in_=t_f[:, kci * TPC:(kci + 1) * TPC]),
                reads=[k_x], dma=True, dkey="out")
            out_toks.append(tok)

        layer_norm(t_f, k_f, "ln2", kc_done=emit_out_kc)
        out_tok = out_toks[-1]
        b.raw("sync", lambda g, s=out_tok[0], v=out_tok[1]: g.wait_ge(s, v))

        with nc.Block() as block:
            b.run_blocks(block)
    return nc


_NC_CACHE = None
_TAP_RESULTS = {}


def kernel(**inputs):
    global _NC_CACHE
    x = np.asarray(inputs["L0_attn_x"], np.float32)
    shared = host_weights(inputs)
    if _NC_CACHE is None:
        _NC_CACHE = build_program()
    nc = _NC_CACHE
    in_maps = []
    for c in range(NCORES):
        bi, cb = c // 4, c % 4
        xt = np.ascontiguousarray(x[bi, cb * TPC:(cb + 1) * TPC, :].T)
        m = dict(shared)
        m["xt"] = xt
        m.update(host_constants(c))
        in_maps.append(m)
    res = run_bass_kernel_spmd(nc, in_maps, core_ids=list(range(NCORES)))
    outs = []
    for c in range(NCORES):
        outs.append(res.results[c]["out_t"].T)
        for kk, vv in res.results[c].items():
            if kk.startswith("tap_"):
                _TAP_RESULTS.setdefault(kk, []).append(np.asarray(vv))
    full = np.concatenate(outs, axis=0).reshape(2, 4096, EMBED)
    return full.astype(np.float32)
